# version 24
# baseline (speedup 1.0000x reference)
import contextlib
import math

import numpy as np
import concourse.bass as bass
import concourse.mybir as mybir
from concourse.bass_utils import run_bass_kernel_spmd

F32 = mybir.dt.float32
BF16 = mybir.dt.bfloat16
FP8 = mybir.dt.float8e4
I32 = mybir.dt.int32
AF = mybir.ActivationFunctionType
OP = mybir.AluOpType
AX = mybir.AxisListType

NCORE = 8
B = 2
N = 2048
D = 1024
H = 4096
KC = D // 128
HC = H // 128
C = 256
TT = C // 128
NCH = N // C
NST = B * NCH
SQD = float(math.sqrt(D))
RMAGIC = 12582912.0
FORCE_ZERO_STATE = False
PI = float(np.pi)

ENGINES = ("sync", "vector", "scalar", "tensor", "gpsimd")


COMPUTE_ENGINES = ("vector", "scalar", "tensor", "gpsimd")


class Sched:

    def __init__(self, nc, es):
        self.nc = nc
        self.q = {e: [] for e in ENGINES}
        self.cnt = {}
        self._names = {}
        self.es = es
        self.esem = {}
        self.ecnt = {e: 0 for e in ENGINES}

    def sem(self, name):
        h = self.es.enter_context(self.nc.semaphore(name))
        self._names[id(h)] = name
        self.cnt[name] = 0
        return h

    def init_engine_sems(self):
        for e in COMPUTE_ENGINES:
            self.esem[e] = self.sem("se_" + e)

    def emit(self, eng, fn, wait=(), sig=None, amt=1):
        wait = tuple(w for w in wait if w is not None and (callable(w) or w[0] is not None))
        prior = self.ecnt[eng] if eng in COMPUTE_ENGINES else 0
        ev = None
        if sig is not None:
            nm = self._names[id(sig)]
            self.cnt[nm] += amt
            ev = (sig, self.cnt[nm])
        elif eng in COMPUTE_ENGINES:
            self.ecnt[eng] += 1
            ev = (self.esem[eng], self.ecnt[eng])
        self.q[eng].append((wait, fn, ev, amt, prior))
        return ev

    def install(self, block, self_waits=False):
        for eng in ENGINES:
            items = self.q[eng]
            dec = getattr(block, eng)
            es_ = self.esem.get(eng)
            sw = self_waits or eng in ("vector", "scalar", "gpsimd")

            def make(items, es_, sw=sw):
                def body(e):
                    for waits, fn, ev, amt, prior in items:
                        if sw and es_ is not None and prior > 0:
                            e.wait_ge(es_, prior)
                        for w in waits:
                            s_, v = w() if callable(w) else w
                            e.wait_ge(s_, v)
                        instr = fn(e)
                        if ev is not None:
                            if instr is None:
                                e.sem_inc(ev[0], amt)
                            else:
                                instr.then_inc(ev[0], amt)
                return body

            dec(make(items, es_))


def build_program(taps=None, build_stage=2, debug_sync=False):
    taps = taps if taps is not None else {}
    nc = bass.Bass("TRN2", target_bir_lowering=False,
                   detect_race_conditions=debug_sync)

    dx = nc.dram_tensor("x", [B, N, D], F32, kind="ExternalInput")
    dwq = nc.dram_tensor("wq", [D, D], F32, kind="ExternalInput")
    dwk = nc.dram_tensor("wk", [D, D], F32, kind="ExternalInput")
    dwv = nc.dram_tensor("wv", [D, D], F32, kind="ExternalInput")
    dwa = nc.dram_tensor("wa", [D, 2 * D], F32, kind="ExternalInput")
    dwg = nc.dram_tensor("wg", [D, D], F32, kind="ExternalInput")
    dwo = nc.dram_tensor("wo", [D, D], F32, kind="ExternalInput")
    dgamma = nc.dram_tensor("gamma", [D], F32, kind="ExternalInput")
    dn1w = nc.dram_tensor("n1w", [D], F32, kind="ExternalInput")
    dn2w = nc.dram_tensor("n2w", [H], F32, kind="ExternalInput")
    dw1 = nc.dram_tensor("w1", [H, D], F32, kind="ExternalInput")
    dw2 = nc.dram_tensor("w2", [D, H], F32, kind="ExternalInput")
    dascale = nc.dram_tensor("ascale", [1], F32, kind="ExternalInput")
    dabias = nc.dram_tensor("abias", [1], F32, kind="ExternalInput")
    dc = nc.dram_tensor("c", [B, D], F32, kind="ExternalInput")
    dwmod = nc.dram_tensor("wmod", [D, 6 * D], F32, kind="ExternalInput")
    dout = nc.dram_tensor("out", [B, N, D], F32, kind="ExternalOutput")

    NBLK_ATTN = 4 * 64 + 16 * 8
    dws_attn = nc.dram_tensor("ws_attn", [128, NBLK_ATTN * 128], BF16, kind="Internal")
    dws_wo = nc.dram_tensor("ws_wo", [D, D], BF16, kind="Internal")
    dws_w1t = nc.dram_tensor("ws_w1t", [128, HC * KC * 128], FP8, kind="Internal")
    dws_w2t = nc.dram_tensor("ws_w2t", [H, D], FP8, kind="Internal")
    dws_mod = nc.dram_tensor("ws_mod", [B, 6 * D], F32, kind="Internal")
    dws_grep = nc.dram_tensor("ws_grep", [128, 2 * B * D], BF16, kind="Internal")
    dxmid = nc.dram_tensor("xmid", [B, N, D], F32, kind="Internal")

    tap_d = {}
    for name, shape in taps.items():
        tap_d[name] = nc.dram_tensor("tap_" + name, list(shape), F32, kind="ExternalOutput")

    es = contextlib.ExitStack()
    E = es.enter_context
    s = Sched(nc, es)

    s.init_engine_sems()
    s_ld = s.sem("s_ld")
    s_wrs = [s.sem(f"s_wr{i}") for i in range(6)]
    s_wfr = [s.sem(f"s_wfr{i}") for i in range(6)]
    s_xls = [s.sem(f"s_xl{i}") for i in range(4)]
    s_sos = [s.sem(f"s_so{i}") for i in range(4)]
    s_wm = [s.sem(f"s_wm{i}") for i in range(4)]
    s_wg = [s.sem(f"s_wg{i}") for i in range(3)]
    s_ws = [s.sem(f"s_ws{i}") for i in range(3)]
    s_wf = [s.sem(f"s_wf{i}") for i in range(4)]
    s_tt = [s.sem(f"s_tt{i}") for i in range(4)]
    s_ms = s.sem("s_ms")
    s_msb = [s.sem(f"s_msb{i}") for i in range(2)]
    s_mv = s.sem("s_mv")
    s_gr = s.sem("s_gr")
    s_st = s.sem("s_st")
    s_bar = s.sem("s_bar")

    identf = E(nc.sbuf_tensor("identf", [128, 128], F32))
    identb = E(nc.sbuf_tensor("identb", [128, 128], BF16))
    onesf = E(nc.sbuf_tensor("onesf", [128, 128], F32))
    onesb = E(nc.sbuf_tensor("onesb", [128, 1], BF16))
    zeroC = E(nc.sbuf_tensor("zeroC", [128, C], F32))
    cvec = E(nc.sbuf_tensor("cvec", [128, 10], F32))
    CEPS5, CEPS8, CPI2, CLSD, C19, C18, CL127, CZERO, CONE, CNEG1 = range(10)
    gam = E(nc.sbuf_tensor("gam", [128, KC], F32))
    n1v = E(nc.sbuf_tensor("n1v", [128, KC], F32))
    n2v = E(nc.sbuf_tensor("n2v", [128, HC], F32))
    asc = E(nc.sbuf_tensor("asc", [128, 1], F32))
    abi = E(nc.sbuf_tensor("abi", [128, 1], F32))
    ascr = E(nc.sbuf_tensor("ascr", [1, 1], F32))
    abir = E(nc.sbuf_tensor("abir", [1, 1], F32))
    m_sa = E(nc.sbuf_tensor("m_sa", [128, B * KC], F32))
    m_ca1 = E(nc.sbuf_tensor("m_ca1", [128, B * KC], F32))
    m_ga = E(nc.sbuf_tensor("m_ga", [128, B * KC], F32))
    m_sb = E(nc.sbuf_tensor("m_sb", [128, B * KC], F32))
    m_cb1 = E(nc.sbuf_tensor("m_cb1", [128, B * KC], F32))
    m_gb = E(nc.sbuf_tensor("m_gb", [128, B * KC], F32))
    w1sv = E(nc.sbuf_tensor("w1sv", [128, 1], F32))
    w2sv = E(nc.sbuf_tensor("w2sv", [128, 1], F32))
    w1sr = E(nc.sbuf_tensor("w1sr", [1, 1], F32))
    w2sr = E(nc.sbuf_tensor("w2sr", [1, 1], F32))
    stage = [E(nc.sbuf_tensor(f"stage{i}", [128, TT * 1024], F32)) for i in range(4)]
    NSLOT = 4
    NSLOTF = 6
    ring = E(nc.sbuf_tensor("ring", [128, NSLOT * 1024], BF16))
    ringf = E(nc.sbuf_tensor("ringf", [128, NSLOTF * 1024], FP8))

    pb = [E(nc.psum_tensor(f"pb{i}", [128, 512], F32)) for i in range(8)]
    ev_bank = [None] * 8

    W = lambda *evs: [e for e in evs if e is not None]

    ev_gp0 = None
    s.emit("gpsimd", lambda g: g.iota(identi[:], pattern=[[1, 128]], base=0,
                                      channel_multiplier=-1))
    s.emit("gpsimd", lambda g: g.memset(onesf[:], 1.0))
    s.emit("gpsimd", lambda g: g.memset(zeroC[:], 0.0))
    for j, valc in enumerate([1e-5, 1e-8, PI / 2, math.log(32.0), 1e-19, 1e-18,
                              math.log(127.0), 0.0, 1.0, -1.0]):
        ev_gp0 = s.emit("gpsimd", lambda g, j=j, v=valc: g.memset(cvec[:, j:j + 1], v))
    ev_idf = s.emit("vector", lambda v: v.tensor_scalar(identf[:], identi[:], 0.0, None, OP.is_equal),
                    wait=W(ev_gp0))
    ev_idb = s.emit("vector", lambda v: v.tensor_scalar(identb[:], identi[:], 0.0, None, OP.is_equal))
    ev_onb = s.emit("vector", lambda v: v.tensor_copy(onesb[:], onesf[:, 0:1]),
                    wait=W(ev_gp0))

    p0 = contextlib.ExitStack()
    P0 = p0.enter_context
    identi = P0(nc.sbuf_tensor("identi", [128, 128], I32))
    csb = P0(nc.sbuf_tensor("csb", [B, D], F32))

    def featvec_load(dst, dram, nk):
        ap = dram.ap().rearrange("(kc p) -> p kc", p=128)
        return s.emit("sync", lambda sy: sy.dma_start(out=dst[:, 0:nk], in_=ap,
                                                      allow_slow_non_contiguous=True),
                      sig=s_ld, amt=16)

    featvec_load(gam, dgamma, KC)
    featvec_load(n1v, dn1w, HC if False else KC)
    featvec_load(n2v, dn2w, HC)
    tiny = P0(nc.sbuf_tensor("tiny", [1, 2], F32))
    s.emit("sync", lambda sy: sy.dma_start(out=tiny[0:1, 0:1], in_=dascale.ap().unsqueeze(0)),
           sig=s_ld, amt=16)
    ev_t2 = s.emit("sync", lambda sy: sy.dma_start(out=tiny[0:1, 1:2], in_=dabias.ap().unsqueeze(0)),
                   sig=s_ld, amt=16)
    ev_cl = s.emit("sync", lambda sy: sy.dma_start(out=csb[:], in_=dc.ap()), sig=s_ld, amt=16)
    ev_misc = ev_cl
    ev_gs = s.emit("vector", lambda v: v.tensor_scalar(gam[:, 0:KC], gam[:, 0:KC], SQD, None, OP.mult),
                   wait=W(ev_misc))
    ev_t1 = ev_misc
    ev_t2 = ev_misc
    ev_mm = s.emit("tensor", lambda t: t.matmul(pb[3][0:128, 0:2], onesf[0:1, :], tiny[:],
                                                start=True, stop=True),
                   wait=W(ev_t1, ev_t2, ev_gp0))
    ev_as = s.emit("scalar", lambda sc: sc.copy(asc[:], pb[3][0:128, 0:1]), wait=W(ev_mm))
    ev_ab = s.emit("scalar", lambda sc: sc.copy(abi[:], pb[3][0:128, 1:2]))
    ev_asr = s.emit("vector", lambda v: v.tensor_copy(ascr[:], tiny[0:1, 0:1]), wait=W(ev_t1))
    ev_abr = s.emit("vector", lambda v: v.tensor_copy(abir[:], tiny[0:1, 1:2]), wait=W(ev_t2))
    ev_bank[3] = ev_ab

    csil = P0(nc.sbuf_tensor("csil", [B, D], F32))
    csilT = P0(nc.sbuf_tensor("csilT", [128, KC * B], BF16))
    mbuf = [P0(nc.sbuf_tensor(f"mbuf{i}", [B, 512], F32)) for i in range(2)]
    mmsk = P0(nc.sbuf_tensor("mmsk", [B, 512], F32))
    ebs = [P0(nc.sbuf_tensor(f"ebs{b}", [B, 128], F32)) for b in range(B)]
    grstg = P0(nc.sbuf_tensor("grstg", [128, 2 * B * D], BF16))
    ev_sil = s.emit("scalar", lambda sc: sc.activation(csil[:], csb[:], AF.Silu),
                    wait=W(ev_misc))
    for kc in range(KC):
        ev = s.emit("tensor", lambda t, kc=kc: t.matmul(
            pb[0][0:128, kc * B:(kc + 1) * B], csil[:, kc * 128:(kc + 1) * 128],
            identf[0:B, 0:B], start=(kc == 0), stop=(kc == KC - 1), is_transpose=True),
            wait=W(ev_sil, ev_idf, ev_bank[0]) if kc == 0 else ())
    ev_csT = s.emit("scalar", lambda sc: sc.copy(csilT[:], pb[0][0:128, 0:KC * B]),
                    wait=W(ev))
    ev_bank[0] = ev_csT
    NWM = 4
    wmb = [P0(nc.sbuf_tensor(f"wmb{i}", [128, 512], F32)) for i in range(NWM)]
    wmbb = [P0(nc.sbuf_tensor(f"wmbb{i}", [128, 512], BF16)) for i in range(NWM)]
    ev_cast = [None] * NWM
    ev_mmod = [None] * NWM
    ev = s.emit("gpsimd", lambda g: g.memset(ebs[0][0:B, :], 0.0))
    ev = s.emit("gpsimd", lambda g: g.memset(ebs[0][0:1, :], 1.0), wait=W(ev))
    ev = s.emit("gpsimd", lambda g: g.memset(ebs[1][0:B, :], 1.0), wait=W(ev))
    ev_eb = s.emit("gpsimd", lambda g: g.memset(ebs[1][0:1, :], 0.0), wait=W(ev))
    ev_mb = [None, None]
    GKIND = {2: 0, 5: 2 * D}

    def mod_tile(oc, kc):
        i = (oc * KC + kc) % NWM
        evl = s.emit("sync", lambda sy, oc=oc, kc=kc, i=i: sy.dma_start(
            out=wmb[i][:], in_=dwmod.ap()[kc * 128:(kc + 1) * 128, oc * 512:(oc + 1) * 512]),
            wait=W(ev_cast[i]), sig=s_wm[i], amt=16)
        ev_cast[i] = s.emit("scalar", lambda sc, i=i: sc.activation(wmbb[i][:], wmb[i][:], AF.Copy),
                            wait=W(evl, ev_mmod[i]))
        ev_mmod[i] = s.emit("tensor", lambda t, kc=kc, i=i: t.matmul(
            pb[1][0:B, 0:512], csilT[:, kc * B:(kc + 1) * B], wmbb[i][:],
            start=(kc == 0), stop=(kc == KC - 1)),
            wait=W(ev_cast[i], ev_csT if oc == 0 and kc == 0 else None,
                   ev_bank[1] if kc == 0 else None))
        if kc != KC - 1:
            return
        mi = oc % 2
        kind = oc // 2
        ev = s.emit("scalar", lambda sc, mi=mi: sc.copy(mbuf[mi][:], pb[1][0:B, 0:512]),
                    wait=W(ev_mmod[(oc * KC + KC - 1) % NWM], ev_mb[mi]))
        ev_bank[1] = ev
        if kind in (1, 4):
            ev = s.emit("vector", lambda v, mi=mi: v.tensor_scalar(
                mbuf[mi][:], mbuf[mi][:], 1.0, None, OP.add), wait=W(ev))
        elif kind in (2, 5):
            ev2 = s.emit("vector", lambda v, mi=mi: v.tensor_scalar(
                mmsk[:], mbuf[mi][:], 0.0, None, OP.is_ge), wait=W(ev))
            ev2 = s.emit("vector", lambda v: v.tensor_scalar(
                mmsk[:], mmsk[:], 2.0, 1.0, OP.mult, OP.subtract))
            ev3 = s.emit("scalar", lambda sc, mi=mi: sc.activation(mbuf[mi][:], mbuf[mi][:], AF.Abs),
                         wait=W(ev2))
            ev3 = s.emit("vector", lambda v, mi=mi: v.tensor_scalar(
                mbuf[mi][:], mbuf[mi][:], 0.25, None, OP.max), wait=W(ev3))
            ev = s.emit("vector", lambda v, mi=mi: v.tensor_tensor(
                mbuf[mi][:], mbuf[mi][:], mmsk[:], OP.mult))
            for b in range(B):
                evr = s.emit("tensor", lambda t, b=b, mi=mi: t.matmul(
                    pb[2][0:128, 0:512], ebs[b][:], mbuf[mi][:],
                    start=True, stop=True),
                    wait=W(ev, ev_eb, ev_bank[2]))
                ev_bank[2] = s.emit("scalar", lambda sc, b=b, oc=oc, kind=kind: sc.copy(
                    grstg[:, GKIND[kind] + b * D + (oc % 2) * 512:
                          GKIND[kind] + b * D + (oc % 2) * 512 + 512],
                    pb[2][0:128, 0:512]), wait=W(evr))
        ev_mb[mi] = s.emit("sync", lambda sy, oc=oc, mi=mi: sy.dma_start(
            out=dws_mod.ap()[:, oc * 512:(oc + 1) * 512], in_=mbuf[mi][:]),
            wait=W(ev, ev_bank[2] if kind in (2, 5) else None), sig=s_msb[mi], amt=16)

    NWS = 3
    wstg = [P0(nc.sbuf_tensor(f"wstg{i}", [128, 2048], F32)) for i in range(NWS)]
    wstgb = [P0(nc.sbuf_tensor(f"wstgb{i}", [128, 2048], BF16)) for i in range(NWS)]
    ev_wst = [None] * NWS

    def attn_tile(dram, nmc, blk0, kc, wo=False):
        i = attn_tile.n % NWS
        attn_tile.n += 1
        ncol = nmc * 128
        evl = s.emit("sync", lambda sy, dram=dram, kc=kc, i=i, ncol=ncol: sy.dma_start(
            out=wstg[i][:, 0:ncol], in_=dram.ap()[kc * 128:(kc + 1) * 128, :]),
            wait=W(ev_wst[i]), sig=s_wg[i], amt=16)
        if attn_tile.n % 2:
            evc = s.emit("scalar", lambda sc, i=i, ncol=ncol: sc.activation(
                wstgb[i][:, 0:ncol], wstg[i][:, 0:ncol], AF.Copy), wait=W(evl))
        else:
            evc = s.emit("vector", lambda v, i=i, ncol=ncol: v.tensor_copy(
                wstgb[i][:, 0:ncol], wstg[i][:, 0:ncol]), wait=W(evl))
        if wo:
            ev_wst[i] = s.emit("scalar", lambda sc, kc=kc, i=i: sc.dma_start(
                out=dws_wo.ap()[kc * 128:(kc + 1) * 128, :], in_=wstgb[i][:, 0:1024]),
                wait=W(evc), sig=s_ws[i], amt=16)
            return

        def store_blk(sc, i=i, nmc=nmc, blk0=blk0, kc=kc):
            src = wstgb[i][:, 0:nmc * 128].rearrange("p (mc j) -> p mc j", j=128)
            dst = dws_attn.ap().rearrange("p (blk j) -> p blk j", j=128)
            return sc.dma_start(out=dst[:, blk0 + kc:blk0 + nmc * KC:KC, :], in_=src)
        ev_wst[i] = s.emit("scalar", store_blk, wait=W(evc), sig=s_ws[i], amt=16)
    attn_tile.n = 0

    accb = P0(nc.sbuf_tensor("accb", [128, HC + KC], F32))
    acc1 = P0(nc.sbuf_tensor("acc1", [128, 2], F32))
    scal = P0(nc.sbuf_tensor("scal", [1, 4], F32))
    NWF = 4
    wstgF = [P0(nc.sbuf_tensor(f"wstgF{i}", [128, 4096], F32)) for i in range(NWF)]
    NTTL = 4
    ttlb = [P0(nc.sbuf_tensor(f"ttlb{i}", [128, 4096], FP8)) for i in range(NTTL)]
    wsb_ = [P0(nc.sbuf_tensor(f"wsb{i}", [128, 1], F32)) for i in range(2)]
    ev_wstF = [None] * NWF
    ev_ttlb = [None] * NTTL
    FFN_W = [(dw1, HC, w1sv, w1sr, 0), (dw2, KC, w2sv, w2sr, HC)]

    def ffn_scale_tile(wi, ti):
        dram, ntile, sv, sr, acc0 = FFN_W[wi]
        cols = int(dram.shape[1])
        i = ffn_scale_tile.n % NWF
        ffn_scale_tile.n += 1
        evl = s.emit("gpsimd", lambda sy, dram=dram, ti=ti, i=i, cols=cols: sy.dma_start(
            out=wstgF[i][:, 0:cols], in_=dram.ap()[ti * 128:(ti + 1) * 128, :]),
            wait=W(ev_wstF[i]), sig=s_wf[i], amt=16)
        ev_wstF[i] = s.emit("vector", lambda v, ti=ti, i=i, cols=cols, acc0=acc0:
                            v.scalar_tensor_tensor(
                                wstgF[i][:, 0:cols], wstgF[i][:, 0:cols], -1.0,
                                wstgF[i][:, 0:cols], OP.mult, OP.max,
                                accum_out=accb[:, acc0 + ti:acc0 + ti + 1]),
                            wait=W(evl))
        return ev_wstF[i]

    ffn_scale_tile.n = 0

    def ffn_finalize(wi, ev_acc):
        dram, ntile, sv, sr, acc0 = FFN_W[wi]
        cols = int(dram.shape[1])
        ev = s.emit("vector", lambda v, ntile=ntile, acc0=acc0: v.tensor_reduce(
            acc1[:, wi:wi + 1], accb[:, acc0:acc0 + ntile], AX.X, OP.add), wait=W(ev_acc))
        ev = s.emit("tensor", lambda t: t.matmul(pb[3][0:1, 0:1], acc1[:, wi:wi + 1],
                                                 onesf[:, 0:1], start=True, stop=True),
                    wait=W(ev, ev_bank[3]))
        nelem = float(128 * ntile * cols)
        ev = s.emit("scalar", lambda sc, nelem=nelem: sc.activation(
            scal[0:1, 0:1], pb[3][0:1, 0:1], AF.Copy, scale=1.0 / nelem), wait=W(ev))
        ev_bank[3] = ev
        ev = s.emit("vector", lambda v: v.tensor_scalar(scal[0:1, 0:1], scal[0:1, 0:1],
                                                        1e-5, None, OP.max), wait=W(ev))
        ev = s.emit("vector", lambda v: v.reciprocal(scal[0:1, 1:2], scal[0:1, 0:1]))
        s.emit("vector", lambda v, sr=sr: v.tensor_copy(sr[:], scal[0:1, 0:1]))
        ev = s.emit("tensor", lambda t: t.matmul(pb[3][0:128, 0:2], onesf[0:1, :], scal[0:1, 0:2],
                                                 start=True, stop=True),
                    wait=W(ev, ev_bank[3]))
        s.emit("scalar", lambda sc, sv=sv: sc.copy(sv[:], pb[3][0:128, 0:1]), wait=W(ev))
        ev_ws = s.emit("scalar", lambda sc, wi=wi: sc.copy(wsb_[wi][:], pb[3][0:128, 1:2]))
        ev_bank[3] = ev_ws
        return ev_ws

    def ffn_quant_tile(wi, ti, ev_ws):
        dram, ntile, sv, sr, acc0 = FFN_W[wi]
        cols = int(dram.shape[1])
        i = ffn_quant_tile.n % NWF
        ffn_quant_tile.n += 1
        evl = s.emit("gpsimd", lambda sy, dram=dram, ti=ti, i=i, cols=cols: sy.dma_start(
            out=wstgF[i][:, 0:cols], in_=dram.ap()[ti * 128:(ti + 1) * 128, :]),
            wait=W(ev_wstF[i], ffn_quant_tile.tp[i], ev_ws), sig=s_wf[i], amt=16)
        ev_wstF[i] = None
        ev = s.emit("vector", lambda v, i=i, cols=cols, wi=wi: v.tensor_scalar(
            wstgF[i][:, 0:cols], wstgF[i][:, 0:cols], wsb_[wi][:], None, OP.mult),
            wait=W(evl))
        ev = s.emit("vector", lambda v, i=i, cols=cols: v.tensor_scalar(
            wstgF[i][:, 0:cols], wstgF[i][:, 0:cols], RMAGIC, RMAGIC,
            OP.add, OP.subtract))
        ev_q = s.emit("vector", lambda v, i=i, cols=cols: v.tensor_scalar(
            wstgF[i][:, 0:cols], wstgF[i][:, 0:cols], 1.0, -1.0, OP.min, OP.max))
        i2 = ti % NTTL
        evc = None
        for jb in range(cols // 512):
            bk = 4 + ((ti + jb) % 4)
            ev = None
            for j4 in range(4):
                j = jb * 4 + j4
                ev = s.emit("tensor", lambda t, i=i, j=j, j4=j4, bk=bk: t.matmul(
                    pb[bk][0:128, j4 * 128:(j4 + 1) * 128],
                    wstgF[i][:, j * 128:(j + 1) * 128], identf[:],
                    start=(j4 == 0), stop=(j4 == 3), is_transpose=True),
                    wait=W(ev_q, ev_idf, ev_bank[bk],
                           ev_ttlb[i2] if j == 0 else None) if j4 == 0 else ())
            ffn_quant_tile.tp[i] = ev
            evc = s.emit("scalar", lambda sc, i2=i2, jb=jb, bk=bk: sc.copy(
                ttlb[i2][:, jb * 512:(jb + 1) * 512], pb[bk][0:128, 0:512]),
                wait=W(ev))
            ev_bank[bk] = evc
        if wi == 0:
            def stw(sy, ti=ti, i2=i2, cols=cols):
                return sy.dma_start(
                    out=dws_w1t.ap()[:, ti * KC * 128:(ti + 1) * KC * 128],
                    in_=ttlb[i2][:, 0:cols])
        else:
            def stw(sy, ti=ti, i2=i2, cols=cols):
                dst = dws_w2t.ap().rearrange("(j p) o -> p j o", p=128)[
                    :, :, ti * 128:(ti + 1) * 128]
                srcv = ttlb[i2][:, 0:cols].rearrange("p (j c) -> p j c", c=128)
                return sy.dma_start(out=dst, in_=srcv,
                                    allow_slow_non_contiguous=True)
        ev_ttlb[i2] = s.emit("sync", stw, wait=W(evc), sig=s_tt[i2], amt=16)
    ffn_quant_tile.n = 0
    ffn_quant_tile.tp = [None] * NWF

    attn_seq = []
    for dram, nmc, blk0 in [(dwq, 8, 0), (dwk, 8, 64), (dwv, 8, 128),
                            (dwg, 8, 192), (dwa, 16, 256)]:
        for kc in range(KC):
            attn_seq.append((dram, nmc, blk0, kc, False))
    for kc in range(KC):
        attn_seq.append((dwo, 8, 0, kc, True))
    mod_seq = [(oc, kc) for oc in range(12) for kc in range(KC)]
    scale_seq = [(0, ti) for ti in range(HC)] + [(1, ti) for ti in range(KC)]
    ev_acc = {0: None, 1: None}
    ws1 = None
    qpend = []
    nrounds = max(len(attn_seq), len(mod_seq), len(scale_seq))
    for r in range(nrounds):
        if r < len(mod_seq):
            mod_tile(*mod_seq[r])
        if r < len(attn_seq):
            attn_tile(*attn_seq[r])
        if r < len(scale_seq):
            wi, ti = scale_seq[r]
            ev_acc[wi] = ffn_scale_tile(wi, ti)
            if wi == 0 and ti == HC - 1:
                ws1 = ffn_finalize(0, ev_acc[0])
        elif ws1 is not None and qpend is not None and len(qpend) < HC:
            ffn_quant_tile(0, len(qpend), ws1)
            qpend.append(None)
    ev_garep = ev_bank[2]
    ev_grs = s.emit("sync", lambda sy: sy.dma_start(out=dws_grep.ap(), in_=grstg[:]),
                    wait=W(ev_garep), sig=s_gr, amt=16)
    ev_mv = {}
    for kind, dstv in [(0, m_sa), (1, m_ca1), (2, m_ga), (3, m_sb), (4, m_cb1), (5, m_gb)]:
        for b in range(B):
            srcap = dws_mod.ap()[b].rearrange("(kind kc p) -> p kc kind", p=128, kind=6)[:, :, kind]
            ev_mv[kind] = s.emit("sync", lambda sy, dstv=dstv, srcap=srcap, b=b: sy.dma_start(
                out=dstv[:, b * KC:(b + 1) * KC], in_=srcap, allow_slow_non_contiguous=True),
                wait=W(ev_mb[0], ev_mb[1]), sig=s_mv, amt=16)
    if ws1 is None:
        ws1 = ffn_finalize(0, ev_acc[0])
    ws2 = ffn_finalize(1, ev_acc[1])
    for ti in range(len(qpend), HC):
        ffn_quant_tile(0, ti, ws1)
    for ti in range(KC):
        ffn_quant_tile(1, ti, ws2)

    p0.close()
    for eng in ENGINES:
        s.emit(eng, lambda e: None, sig=s_bar)
    bar1 = (s_bar, s.cnt["s_bar"])
    ws_done = [(s_ws[i], s.cnt[f"s_ws{i}"]) for i in range(3)]
    tt_done = [(s_tt[i], s.cnt[f"s_tt{i}"]) for i in range(4)]
    sc_done_l = ws_done + tt_done
    sc_done_l += [(s_msb[0], s.cnt["s_msb0"]), (s_msb[1], s.cnt["s_msb1"])]
    p0_done = (sc_done_l + [(s_wg[i], s.cnt[f"s_wg{i}"]) for i in range(3)]
               + [(s_wf[i], s.cnt[f"s_wf{i}"]) for i in range(4)]
               + [(s_wm[i], s.cnt[f"s_wm{i}"]) for i in range(4)]
               + [(s_gr, s.cnt["s_gr"])])
    modv_done = (s_mv, s.cnt["s_mv"])

    ring_state = {"g": 0, "loaded": [], "release": {}, "slot": {},
                  "meta": {id(ring): {"n": 0, "nslot": NSLOT, "g_of": [],
                                      "sems": s_wrs},
                           id(ringf): {"n": 0, "nslot": NSLOTF, "g_of": [],
                                       "sems": s_wfr}}}

    def ring_release(g, ev):
        ring_state["release"][g] = ev

    def ring_load(buf, reqs, wait_extra=()):
        g = ring_state["g"]
        st = ring_state["meta"][id(buf)]
        li = st["n"]
        st["n"] += 1
        st["g_of"].append(g)
        slot = li % st["nslot"]
        ring_state["slot"][g] = slot
        waits = list(wait_extra)
        if li >= st["nslot"]:
            gprev = st["g_of"][li - st["nslot"]]
            waits.append(lambda gprev=gprev: ring_state["release"][gprev])
        ev = None
        for (src, c0, ncols) in reqs:
            ev = s.emit("gpsimd", lambda gp, buf=buf, slot=slot, c0=c0, ncols=ncols, src=src:
                        gp.dma_start(out=buf[:, slot * 1024 + c0:slot * 1024 + c0 + ncols],
                                     in_=src),
                        wait=waits, sig=st["sems"][slot], amt=16)
            waits = []
        ring_state["loaded"].append(ev)
        ring_state["g"] += 1
        return g

    def ring_wait(g):
        return ring_state["loaded"][g]

    def ring_ap(buf, g, c0, ncols):
        slot = ring_state["slot"][g]
        return buf[:, slot * 1024 + c0:slot * 1024 + c0 + ncols]

    attn_pg0 = {}
    attn_wg0 = {}

    PROJ_BLK = {"k": 64, "v": 128, "g": 192, "ar": 256, "ai": 256, "q": 0}

    def proj_desc():
        desc = []
        for pas in (0, 1):
            for kind in ("k", "v", "g", "ar", "ai", "q"):
                for b in (0, 1):
                    hb = pas if b == 0 else 1 - pas
                    for mcx in range(4):
                        mc = (8 if kind == "ai" else 0) + hb * 4 + mcx
                        desc.append((pas, b, kind, mc, hb))
        return desc

    def queue_proj_groups(ck, extra=()):
        attn_pg0[ck] = ring_state["g"]
        for (pas, b, kind, mc, hb) in proj_desc():
            blk0 = PROJ_BLK[kind]
            srcw = dws_attn.ap()[:, (blk0 + mc * KC) * 128:(blk0 + mc * KC + 8) * 128]
            ring_load(ring, [(srcw, 0, 1024)], wait_extra=extra)
            extra = ()

    def queue_wo_groups(ck):
        attn_wg0[ck] = ring_state["g"]
        for oc in range(2):
            for kcp in range(4):
                reqs = []
                for jj in range(2):
                    kc = kcp * 2 + jj
                    reqs.append((dws_wo.ap()[kc * 128:(kc + 1) * 128, oc * 512:(oc + 1) * 512],
                                 jj * 512, 512))
                ring_load(ring, reqs)

    if build_stage >= 1:
        queue_proj_groups(0, extra=[bar1] + sc_done_l)
        for ck in range(1, NCH):
            queue_proj_groups(ck)
            queue_wo_groups(ck - 1)
        queue_wo_groups(NCH - 1)

    p1 = contextlib.ExitStack()
    P1 = p1.enter_context
    x_fm = P1(nc.sbuf_tensor("x_fm", [128, KC * C], F32))
    xh = P1(nc.sbuf_tensor("xh", [128, KC * C], F32))
    hsq = P1(nc.sbuf_tensor("hsq", [128, KC * C], BF16))
    xn = [P1(nc.sbuf_tensor(f"xn{i}", [128, KC * C], BF16)) for i in range(2)]
    yb = [P1(nc.sbuf_tensor(f"yb{i}", [128, KC * C], BF16)) for i in range(2)]
    w7 = P1(nc.sbuf_tensor("w7", [128, KC * C], F32))
    w8 = P1(nc.sbuf_tensor("w8", [128, KC * C], F32))
    w9 = P1(nc.sbuf_tensor("w9", [128, KC * C], F32))
    w10 = P1(nc.sbuf_tensor("w10", [128, KC * C], F32))
    w11 = P1(nc.sbuf_tensor("w11", [128, KC * C], F32))
    psi = P1(nc.sbuf_tensor("psi", [128, KC * C], F32))
    qb = [P1(nc.sbuf_tensor(f"qb{i}", [128, KC * C], BF16)) for i in range(2)]
    kbn = [P1(nc.sbuf_tensor(f"kbn{i}", [128, KC * C], BF16)) for i in range(2)]
    vb = [P1(nc.sbuf_tensor(f"vb{i}", [128, KC * C], BF16)) for i in range(2)]
    sgb = [P1(nc.sbuf_tensor(f"sgb{i}", [128, KC * C], BF16)) for i in range(2)]
    arb = [P1(nc.sbuf_tensor(f"arb{i}", [128, KC * C], F32)) for i in range(2)]
    aib = [P1(nc.sbuf_tensor(f"aib{i}", [128, KC * C], F32)) for i in range(2)]
    mrow = P1(nc.sbuf_tensor("mrow", [1, 2 * C], F32))
    srow = P1(nc.sbuf_tensor("srow", [1, C], F32))
    mrep = P1(nc.sbuf_tensor("mrep", [128, 2 * C], F32))
    grep = P1(nc.sbuf_tensor("grep", [128, C], F32))
    mvst = P1(nc.sbuf_tensor("mvst", [128, 4], F32))
    varr = P1(nc.sbuf_tensor("varr", [128, 2], F32))
    msq2 = P1(nc.sbuf_tensor("msq2", [128, 2], F32))
    sagam = P1(nc.sbuf_tensor("sagam", [128, B * KC], F32))
    ga_rep = P1(nc.sbuf_tensor("ga_rep", [128, B * D], BF16))
    st_psi = [P1(nc.sbuf_tensor(f"stpsi{b}", [128, KC], F32)) for b in range(B)]
    st_zr = [P1(nc.sbuf_tensor(f"stzr{b}", [128, KC], F32)) for b in range(B)]
    st_zi = [P1(nc.sbuf_tensor(f"stzi{b}", [128, KC], F32)) for b in range(B)]

    ev1 = {}

    def xf_slice(buf, kc, t0=0, nt=C):
        return buf[:, kc * C + t0:kc * C + t0 + nt]

    def lastcol(buf):
        return buf[:].rearrange("p (kc c) -> p kc c", c=C)[:, :, C - 1]

    def emit_A1_tp(ss):
        b, ck = ss % 2, ss // 2
        slot = ss % 4
        wv = [] if ss < 4 else [ev1.get(("store", ss - 4))]
        src = dx.ap()[b].rearrange("(ck tt p) d -> ck p tt d", p=128, tt=TT)[ck]
        ev1[("xl", ss)] = s.emit("sync", lambda sy, slot=slot, src=src: sy.dma_start(
            out=stage[slot][:], in_=src), wait=W(*wv), sig=s_xls[slot], amt=16)
        last = None
        for tt in range(TT):
            for kcg in range(2):
                bk = kcg
                ev = None
                for i2 in range(4):
                    kc = kcg * 4 + i2
                    ev = s.emit("tensor", lambda t, slot=slot, tt=tt, kc=kc, i2=i2, bk=bk: t.matmul(
                        pb[bk][0:128, i2 * 128:(i2 + 1) * 128],
                        stage[slot][:, tt * 1024 + kc * 128:tt * 1024 + (kc + 1) * 128],
                        identf[:], start=(i2 == 0), stop=(i2 == 3), is_transpose=True),
                        wait=W(ev1[("xl", ss)], ev_bank[bk],
                               ev1.get(("xh", ss - 1))) if i2 == 0 else ())

                def evacx(sc, tt=tt, kcg=kcg, bk=bk):
                    dst = x_fm[:].rearrange("p (kc c) -> p kc c", c=C)[
                        :, kcg * 4:(kcg + 1) * 4, tt * 128:(tt + 1) * 128]
                    return sc.copy(dst, pb[bk][0:128, 0:512])
                last = ev_bank[bk] = s.emit(
                    "scalar", evacx,
                    wait=W(ev) + (p0_done if ss < 2 else []))
        ev1[("xfm", ss)] = last

    def emit_A1_stats(ss):
        b, ck = ss % 2, ss // 2
        slot = ss % 4
        ev = None
        for tt in range(TT):
            s.emit("vector", lambda v, slot=slot, tt=tt: v.tensor_scalar(
                hsq[:].bitcast(F32)[:, 0:1024], stage[slot][:, tt * 1024:(tt + 1) * 1024],
                1.0 / D, None, OP.mult, OP.add, accum_out=mvst[:, tt:tt + 1]),
                wait=(W(ev1[("xl", ss)], ev1.get(("mrow", ss - 1)), ev1.get(("hsq", ss - 1)),
                        modv_done if ss == 0 else None)
                      + (p0_done if ss < 2 else [])) if tt == 0 else ())
            ev = s.emit("vector", lambda v, slot=slot, tt=tt: v.scalar_tensor_tensor(
                hsq[:].bitcast(F32)[:, 0:1024], stage[slot][:, tt * 1024:(tt + 1) * 1024], 1.0,
                stage[slot][:, tt * 1024:(tt + 1) * 1024], OP.mult, OP.mult,
                accum_out=mvst[:, 2 + tt:3 + tt]))
        ev = s.emit("vector", lambda v: v.tensor_scalar(
            varr[:], mvst[:, 2:4], 1.0 / D, None, OP.mult), wait=W(ev))
        ev = s.emit("vector", lambda v: v.tensor_tensor(
            msq2[:], mvst[:, 0:2], mvst[:, 0:2], OP.mult))
        ev = s.emit("vector", lambda v: v.tensor_tensor(varr[:], varr[:], msq2[:], OP.subtract))
        ev = s.emit("scalar", lambda sc: sc.activation(varr[:], varr[:], AF.Ln,
                                                       bias=cvec[:, CEPS5:CEPS5 + 1]),
                    wait=W(ev))
        ev_rstd = s.emit("scalar", lambda sc: sc.activation(varr[:], varr[:], AF.Exp, scale=-0.5))
        ev = None
        for i2, (buf_, col) in enumerate([(mvst, 0), (mvst, 1), (varr, 0), (varr, 1)]):
            ev = s.emit("tensor", lambda t, buf_=buf_, col=col, i2=i2: t.matmul(
                pb[3][0:1, i2 * 128:(i2 + 1) * 128], buf_[:, col:col + 1], identf[:],
                start=(i2 == 0), stop=(i2 == 3), is_transpose=True),
                wait=W(ev_rstd, ev_bank[3]) if i2 == 0 else ())
        ev_mrow = s.emit("scalar", lambda sc: sc.copy(mrow[:], pb[3][0:1, 0:512]),
                         wait=W(ev))
        ev_bank[3] = ev_mrow
        ev1[("mrow", ss)] = ev_mrow
        ev = s.emit("tensor", lambda t: t.matmul(pb[2][0:128, 0:C], onesf[0:1, :],
                                                 mrow[0:1, 0:C], start=True, stop=False),
                    wait=W(ev_mrow, ev_bank[2], ev1.get(("xh", ss - 1)),
                           ev1.get(("hsq", ss - 1))))
        ev = s.emit("tensor", lambda t: t.matmul(pb[2][0:128, C:2 * C], onesf[0:1, :],
                                                 mrow[0:1, C:2 * C], start=False, stop=True))
        ev_mrep = s.emit("scalar", lambda sc: sc.copy(mrep[:], pb[2][0:128, 0:2 * C]),
                         wait=W(ev))
        ev_bank[2] = ev_mrep
        ev = None
        for kc in range(KC):
            s.emit("vector", lambda v, kc=kc: v.tensor_tensor(
                xf_slice(xh, kc), xf_slice(x_fm, kc), mrep[:, 0:C], OP.subtract),
                wait=W(ev1[("xfm", ss)], ev_mrep) if kc == 0 else ())
            ev = s.emit("vector", lambda v, kc=kc, b=b: v.scalar_tensor_tensor(
                xf_slice(xh, kc), xf_slice(xh, kc), m_ca1[:, b * KC + kc:b * KC + kc + 1],
                mrep[:, C:2 * C], OP.mult, OP.mult))
            ev1[("xhk", ss, kc)] = ev
        ev1[("xh", ss)] = ev

    def emit_S21(ss):
        b = ss % 2
        ev_h = None
        for kc in range(KC):
            ev_h = s.emit("scalar", lambda sc, kc=kc, b=b: sc.activation(
                xf_slice(hsq, kc), xf_slice(xh, kc), AF.Square,
                bias=m_sa[:, b * KC + kc:b * KC + kc + 1]),
                wait=W(ev1[("xhk", ss, kc)]))
        ev1[("hsq", ss)] = ev_h
        ev = None
        for kc in range(KC):
            ev = s.emit("tensor", lambda t, kc=kc: t.matmul(
                pb[3][0:1, 0:C], onesb[:], xf_slice(hsq, kc),
                start=(kc == 0), stop=(kc == KC - 1)),
                wait=W(ev_h, ev_bank[3]) if kc == 0 else ())
        ev = s.emit("scalar", lambda sc: sc.copy(srow[:], pb[3][0:1, 0:C]), wait=W(ev))
        ev = s.emit("scalar", lambda sc: sc.activation(srow[:], srow[:], AF.Ln,
                                                       bias=cvec[0:1, CEPS5:CEPS5 + 1]))
        ev = s.emit("scalar", lambda sc: sc.activation(srow[:], srow[:], AF.Exp, scale=-0.5))
        evm = s.emit("tensor", lambda t: t.matmul(pb[3][0:128, 0:C], onesf[0:1, :],
                                                  srow[0:1, :], start=True, stop=True),
                     wait=W(ev, ev1.get(("xn", ss - 1))))
        ev = s.emit("scalar", lambda sc: sc.copy(grep[:], pb[3][0:128, 0:C]),
                    wait=W(evm))
        ev_bank[3] = ev
        ev1[("grep", ss)] = ev

    def emit_XN(ss):
        b = ss % 2
        ck2 = ss // 2
        ids = []
        for kc in range(KC):
            ids.append(s.emit("scalar", lambda sc, kc=kc, b=b: sc.activation(
                xf_slice(xh, kc), xf_slice(xh, kc), AF.Identity,
                scale=gam[:, kc:kc + 1], bias=sagam[:, b * KC + kc:b * KC + kc + 1])))
        ev = None
        for kc in range(KC):
            ev = s.emit("vector", lambda v, kc=kc, b=b: v.tensor_tensor(
                xf_slice(xn[b], kc), xf_slice(xh, kc), grep[:], OP.mult),
                wait=W(ev1[("grep", ss)], ids[kc],
                       ev1.get(("PpairB", ck2 - 1, b)) if kc == 0 else None))
        ev1[("xn", ss)] = ev

    def chain_stream(ts, hs, wprev_key):
        b, ck = ts % 2, ts // 2
        p = ts % 2
        HW = 4 * C

        def sl(buf):
            return buf[:, hs * HW:(hs + 1) * HW]
        wprev = ev1.get(wprev_key) if wprev_key else None
        evP_ = ev1[("evP", ts)]
        h = {}
        steps = []

        def op(eng, fn, wk=(), name=None, free_tag=None):
            def th(eng=eng, fn=fn, wk=wk, name=name, free_tag=free_tag):
                ev = s.emit(eng, fn, wait=W(*[h[k] if isinstance(k, str) else k for k in wk]))
                if name:
                    h[name] = ev
                if free_tag:
                    ev1[(free_tag, ts, hs)] = ev
            steps.append(th)

        op("vector", lambda v: v.tensor_tensor(
            sl(qb[p]), sl(qb[p]), sl(sgb[p]), OP.mult),
            wk=(evP_, wprev), name="qg", free_tag="fsgb")
        op("vector", lambda v: v.tensor_tensor(
            sl(w11), sl(kbn[p]), sl(vb[p]), OP.mult),
            wk=(evP_, wprev), name="kv", free_tag="fkv")
        op("scalar", lambda sc: sc.activation(sl(w8), sl(arb[p]), AF.Square),
           wk=(evP_, wprev), name="e1")
        op("scalar", lambda sc: sc.activation(sl(w9), sl(aib[p]), AF.Square), name="e2")
        op("vector", lambda v: v.tensor_tensor(sl(w8), sl(w8), sl(w9), OP.add),
           wk=("e2",), name="e3")
        op("scalar", lambda sc: sc.activation(sl(w9), sl(w8), AF.Ln,
                                              bias=cvec[:, C18:C18 + 1]),
           wk=("e3",), name="e4")
        op("scalar", lambda sc: sc.activation(sl(w9), sl(w9), AF.Exp, scale=0.5),
           name="e5")
        op("vector", lambda v: v.tensor_tensor(sl(w10), sl(w9), sl(arb[p]), OP.add),
           wk=("e5",), name="e6a", free_tag="farb")
        op("scalar", lambda sc: sc.activation(sl(w10), sl(w10), AF.Relu),
           wk=("e6a",), name="e6b")
        op("vector", lambda v: v.tensor_tensor(sl(w8), sl(w9), sl(w10), OP.mult),
           wk=("e6b",), name="e7")
        op("scalar", lambda sc: sc.activation(sl(w8), sl(w8), AF.Ln, scale=2.0,
                                              bias=cvec[:, C18:C18 + 1]),
           wk=("e7",), name="e8")
        op("scalar", lambda sc: sc.activation(sl(w8), sl(w8), AF.Exp, scale=0.5),
           name="e9")
        op("vector", lambda v: v.tensor_tensor(sl(w8), sl(w8), sl(w10), OP.add),
           wk=("e9",), name="e10")
        op("scalar", lambda sc: sc.activation(sl(w8), sl(w8), AF.Ln,
                                              bias=cvec[:, C19:C19 + 1]),
           wk=("e10",), name="e11")
        op("scalar", lambda sc: sc.activation(sl(w8), sl(w8), AF.Exp, scale=-1.0),
           name="e12")
        op("vector", lambda v: v.tensor_tensor(sl(w10), sl(aib[p]), sl(w8), OP.mult),
           wk=("e12",), name="e13a", free_tag="faib")
        op("vector", lambda v: v.tensor_scalar(sl(w10), sl(w10), 1.0, -1.0,
                                               OP.min, OP.max), name="e13b")
        op("scalar", lambda sc: sc.activation(sl(w7), sl(w9), AF.Sigmoid),
           wk=(wprev,), name="e14")
        op("scalar", lambda sc: sc.activation(sl(w10), sl(w10), AF.Arctan),
           wk=("e13b",), name="e15")

        def scanstep(out, in0, in1, state, wname):
            def th(out=out, in0=in0, in1=in1, state=state, wname=wname):
                for kx in range(4):
                    kc = hs * 4 + kx
                    init = 0.0 if ck == 0 or FORCE_ZERO_STATE else state[b][:, kc:kc + 1]
                    s.emit("vector", lambda v, kc=kc, init=init, out=out, in0=in0,
                           in1=in1: v.tensor_tensor_scan(
                               xf_slice(out, kc), xf_slice(in0, kc),
                               xf_slice(in1, kc) if in1 is not zeroC else zeroC[:],
                               init, OP.mult if in1 is not zeroC else OP.add, OP.add),
                           wait=W(h[wname]) if kx == 0 else ())
            steps.append(th)

        def statecopy(state, srcbuf):
            def th(state=state, srcbuf=srcbuf):
                lc = srcbuf[:].rearrange("p (kc c) -> p kc c", c=C)[
                    :, hs * 4:(hs + 1) * 4, C - 1]
                s.emit("vector", lambda v, lc=lc: v.tensor_copy(
                    state[b][:, hs * 4:(hs + 1) * 4], lc))
            steps.append(th)

        scanstep(psi, w10, zeroC, st_psi, "e15")
        statecopy(st_psi, psi)
        op("vector", lambda v: v.tensor_scalar(
            w8[:].bitcast(I32)[:, hs * HW:(hs + 1) * HW], sl(psi), 2.0 / PI, None, OP.mult),
            name="w8i")
        op("vector", lambda v: v.scalar_tensor_tensor(
            sl(w8), w8[:].bitcast(I32)[:, hs * HW:(hs + 1) * HW], -PI / 2.0, sl(psi),
            OP.mult, OP.add), name="e20")
        op("scalar", lambda sc: sc.activation(sl(w9), sl(w8), AF.Sin),
           wk=("e20",), name="e21")
        op("scalar", lambda sc: sc.activation(sl(w10), sl(w8), AF.Sin, scale=-1.0,
                                              bias=cvec[:, CPI2:CPI2 + 1]), name="e22")
        op("scalar", lambda sc: sc.activation(sl(w8), sl(w9), AF.Square), name="e23")
        op("scalar", lambda sc: sc.activation(sl(w8), sl(w8), AF.Identity,
                                              scale=-2.0, bias=cvec[:, CONE:CONE + 1]),
           name="e24")
        op("vector", lambda v: v.scalar_tensor_tensor(
            sl(w9), sl(w9), 2.0, sl(w10), OP.mult, OP.mult),
            wk=("e22", "e23"), name="e25")
        op("scalar", lambda sc: sc.activation(sl(w10), sl(w8), AF.Square),
           wk=("e25",), name="e26")
        op("scalar", lambda sc: sc.activation(sl(w10), sl(w10), AF.Identity,
                                              scale=2.0, bias=cvec[:, CNEG1:CNEG1 + 1]),
           name="e27")
        op("vector", lambda v: v.scalar_tensor_tensor(
            sl(w9), sl(w9), 2.0, sl(w8), OP.mult, OP.mult),
            wk=("e24",), name="e28")
        op("vector", lambda v: v.tensor_tensor(sl(w8), sl(w10), sl(w11), OP.mult),
           wk=("e27", "e26"), name="e29")
        op("vector", lambda v: v.tensor_tensor(sl(w11), sl(w9), sl(w11), OP.mult),
           name="e30")
        scanstep(psi, w7, w8, st_zr, "e14")
        statecopy(st_zr, psi)
        scanstep(w8, w7, w11, st_zi, "e14")
        statecopy(st_zi, w8)
        op("vector", lambda v: v.tensor_tensor(sl(w10), sl(w10), sl(psi), OP.mult), name="m1")
        op("vector", lambda v: v.tensor_tensor(sl(w9), sl(w9), sl(w8), OP.mult), name="m2")
        op("vector", lambda v: v.tensor_tensor(sl(w10), sl(w10), sl(w9), OP.add), name="m3")

        def m4():
            ev = s.emit("vector", lambda v: v.tensor_tensor(
                sl(yb[p]), sl(w10), sl(qb[p]), OP.mult))
            ev1[("y", ts, hs)] = ev
        steps.append(m4)
        return steps

    def p1b_thunks(ck):
        g0 = attn_pg0[ck]
        bankrot = [4, 5, 6, 7]
        state = {"bi": 0, "last_evac": {}, "last_mm": {}}
        thunks = []
        desc = proj_desc()
        for pi in range(0, len(desc), 2):
            pair = [(gi, desc[gi]) for gi in (pi, pi + 1)]
            def th(pair=pair):
                bk = bankrot[state["bi"] % 4]
                state["bi"] += 1
                ev = None
                for half_i, (gi, (pas, b, kind, mc, hb)) in enumerate(pair):
                    g = g0 + gi
                    for kc in range(KC):
                        first = half_i == 0 and kc == 0
                        ev = s.emit("tensor", lambda t, g=g, kc=kc, bk=bk, b=b,
                                    half_i=half_i, first=first: t.matmul(
                                        pb[bk][0:128, half_i * C:(half_i + 1) * C],
                                        ring_ap(ring, g, kc * 128, 128),
                                        xn[b][:, kc * C:(kc + 1) * C],
                                        start=first,
                                        stop=(half_i == 1 and kc == KC - 1)),
                                    wait=W(ring_wait(g), ev1[("xn", 2 * ck + b)],
                                           ev_bank[bk] if first else None) if kc == 0 else ())
                    ring_release(g, ev)
                    state["last_mm"][b] = ev
                gi, (pas, b, kind, mc, hb) = pair[0]
                ts = 2 * ck + b
                tsp = ts - 2
                silu = False
                if kind == "k":
                    dst, mcc, fw = kbn[b], mc, ("fkv", tsp, hb)
                elif kind == "v":
                    dst, mcc, fw = vb[b], mc, ("fkv", tsp, hb)
                elif kind == "g":
                    dst, mcc, fw = sgb[b], mc, ("fsgb", tsp, hb)
                    silu = True
                elif kind == "ar":
                    dst, mcc, fw = arb[b], mc, ("farb", tsp, hb)
                elif kind == "ai":
                    dst, mcc, fw = aib[b], mc - 8, ("faib", tsp, hb)
                else:
                    dst, mcc, fw = qb[b], mc, ("y", tsp, hb)
                wfree = ev1.get(fw)
                assert wfree is not None or tsp < 0, fw
                if silu:
                    evv = s.emit("scalar", lambda sc, dst=dst, mcc=mcc, bk=bk: sc.activation(
                        xf_slice(dst, mcc, nt=2 * C), pb[bk][0:128, 0:2 * C], AF.Silu),
                        wait=W(ev, wfree))
                else:
                    evv = s.emit("scalar", lambda sc, dst=dst, mcc=mcc, bk=bk: sc.copy(
                        xf_slice(dst, mcc, nt=2 * C), pb[bk][0:128, 0:2 * C]),
                        wait=W(ev, wfree))
                ev_bank[bk] = evv
                state["last_evac"][b] = evv
            thunks.append(th)

        def fin():
            for b in range(2):
                ev1[("PpairB", ck, b)] = state["last_mm"][b]
                ev1[("evP", 2 * ck + b)] = state["last_evac"][b]
        return thunks, fin

    def stream_zip(sa, sb):
        out = []
        for i in range(max(len(sa), len(sb))):
            if i < len(sa):
                out.append(sa[i])
            if i < len(sb):
                out.append(sb[i])
        return out

    def run_interleaved(steps, thunks):
        n = len(steps)
        bounds = ([4] * 12 + [16] * 4 + [32] * 4 + [n] * 4) if n else [0] * 24
        feeds = {}
        pos = 0
        for j, th in enumerate(thunks):
            want = 4 + (j * max(n - 6, 1)) // max(len(thunks) - 1, 1)
            pos = min(max(pos, bounds[j] if j < len(bounds) else n, want), n)
            feeds.setdefault(pos, []).append(th)
        for i, stp in enumerate(steps):
            stp()
            for th in feeds.get(i, []):
                th()
        for th in feeds.get(n, []):
            th()

    def emit_W1pair(ck):
        g0 = attn_wg0[ck]
        evr = {}
        for oc in range(2):
            evmm = {}
            for kcp in range(4):
                g = g0 + oc * 4 + kcp
                for jj in range(2):
                    kc = kcp * 2 + jj
                    for b in range(2):
                        for t_ in range(TT):
                            bk = 4 + b * 2 + t_
                            wv = []
                            if jj == 0 and b == 0 and t_ == 0:
                                wv.append(ring_wait(g))
                            if kc == 0:
                                wv += [ev1[("y", 2 * ck + b, 0)], ev_bank[bk]]
                            if kc == 4:
                                wv.append(ev1[("y", 2 * ck + b, 1)])
                            evmm[(b, t_)] = s.emit(
                                "tensor", lambda t, g=g, kc=kc, jj=jj, t_=t_, bk=bk, b=b: t.matmul(
                                    pb[bk][0:128, 0:512],
                                    yb[b][:, kc * C + t_ * 128:kc * C + (t_ + 1) * 128],
                                    ring_ap(ring, g, jj * 512, 512),
                                    start=(kc == 0), stop=(kc == KC - 1)),
                                wait=W(*wv))
                            if jj == 1 and b == 1 and t_ == TT - 1:
                                ring_release(g, evmm[(b, t_)])
            for b in range(2):
                slot = (2 * ck + b) % 4
                for t_ in range(TT):
                    bk = 4 + b * 2 + t_
                    s.emit("vector", lambda v, bk=bk, b=b, oc=oc: v.tensor_tensor(
                        psi[:, 0:512], pb[bk][0:128, 0:512],
                        ga_rep[:, b * D + oc * 512:b * D + (oc + 1) * 512], OP.mult),
                        wait=W(evmm[(b, t_)], ga_ready))
                    evr[(b, t_)] = s.emit("vector", lambda v, slot=slot, t_=t_, oc=oc: v.tensor_tensor(
                        stage[slot][:, t_ * 1024 + oc * 512:t_ * 1024 + (oc + 1) * 512],
                        stage[slot][:, t_ * 1024 + oc * 512:t_ * 1024 + (oc + 1) * 512],
                        psi[:, 0:512], OP.add))
                    ev_bank[bk] = evr[(b, t_)]
        for b in range(2):
            ts = 2 * ck + b
            slot = ts % 4
            dstm = dxmid.ap()[b].rearrange("(ck tt p) d -> ck p tt d", p=128, tt=TT)[ck]
            ev1[("store", ts)] = s.emit("sync", lambda sy, slot=slot, dstm=dstm: sy.dma_start(
                out=dstm, in_=stage[slot][:]), wait=W(evr[(b, TT - 1)]), sig=s_sos[slot], amt=16)

    if build_stage >= 1:
        s.emit("sync", lambda sy: sy.dma_start(out=ga_rep[:], in_=dws_grep.ap()[:, 0:B * D]),
               wait=W(bar1, ev_grs), sig=s_gr, amt=16)
        ga_ready = (s_gr, s.cnt["s_gr"])
        for b_ in range(B):
            s.emit("vector", lambda v, b_=b_: v.tensor_tensor(
                sagam[:, b_ * KC:(b_ + 1) * KC], m_sa[:, b_ * KC:(b_ + 1) * KC],
                gam[:, 0:KC], OP.mult), wait=W(bar1, modv_done) if b_ == 0 else ())
        for b0 in range(2):
            emit_A1_tp(b0)
            emit_A1_stats(b0)
            emit_S21(b0)
            emit_XN(b0)
        for it in range(NCH + 1):
            th96, finb = p1b_thunks(it) if it < NCH else ([], None)
            for pas in (0, 1):
                if it >= 1:
                    if pas == 0:
                        sa = chain_stream(2 * it - 2, 0, ("y", 2 * it - 3, 0))
                        sb = chain_stream(2 * it - 1, 1, ("y", 2 * it - 4, 1))
                    else:
                        sa = chain_stream(2 * it - 2, 1, ("y", 2 * it - 1, 1))
                        sb = chain_stream(2 * it - 1, 0, ("y", 2 * it - 2, 0))
                    steps = stream_zip(sa, sb)
                else:
                    steps = []
                run_interleaved(steps, th96[pas * 24:(pas + 1) * 24])
            if finb is not None:
                finb()
            if it >= 1:
                emit_W1pair(it - 1)
            if it + 1 < NCH:
                for ss2 in (2 * it + 2, 2 * it + 3):
                    emit_A1_tp(ss2)
                    emit_A1_stats(ss2)
                    emit_S21(ss2)
                    emit_XN(ss2)

    s.emit("gpsimd", lambda e: None, sig=s_bar)

    ffn_m1g0 = {}
    ffn_m2g0 = {}

    def queue_mm1_groups(ss, extra=()):
        ffn_m1g0[ss] = ring_state["g"]
        for hc in range(HC):
            srcw = dws_w1t.ap()[:, (hc * KC) * 128:(hc * KC + 8) * 128]
            ring_load(ringf, [(srcw, 0, 1024)], wait_extra=extra)
            extra = ()

    def queue_mm2_groups(ts):
        ffn_m2g0[ts] = ring_state["g"]
        for oc in range(2):
            for kcp in range(16):
                reqs = []
                for jj in range(2):
                    kc = kcp * 2 + jj
                    reqs.append((dws_w2t.ap()[kc * 128:(kc + 1) * 128,
                                              oc * 512:(oc + 1) * 512], jj * 512, 512))
                ring_load(ringf, reqs)

    if build_stage >= 2:
        queue_mm1_groups(0, extra=tt_done)
        for ss in range(1, NST):
            queue_mm1_groups(ss)
            queue_mm2_groups(ss - 1)
        queue_mm2_groups(NST - 1)

    p1.close()
    for eng in ENGINES:
        if eng != "gpsimd":
            s.emit(eng, lambda e: None, sig=s_bar)
    bar2 = (s_bar, s.cnt["s_bar"])

    p2 = contextlib.ExitStack()
    P2 = p2.enter_context
    HHALF = HC // 2
    x2fm = P2(nc.sbuf_tensor("x2fm", [128, KC * C], F32))
    xh2 = P2(nc.sbuf_tensor("xh2", [128, KC * C], F32))
    w2d = P2(nc.sbuf_tensor("w2d", [128, 4 * C], F32))
    m2scr = P2(nc.sbuf_tensor("m2scr", [128, 512], F32))
    hsq1 = P2(nc.sbuf_tensor("hsq1", [128, KC * C], BF16))
    xq1 = [P2(nc.sbuf_tensor(f"xq1_{i}", [128, KC * C], BF16)) for i in range(2)]
    Rb = P2(nc.sbuf_tensor("Rb", [128, HC * C], F32))
    B2 = P2(nc.sbuf_tensor("B2", [128, HC * C], F32))
    h2b = P2(nc.sbuf_tensor("h2b", [128, HC * C], BF16))
    xq2 = P2(nc.sbuf_tensor("xq2", [128, HC * C], BF16))
    gb_rep = P2(nc.sbuf_tensor("gb_rep", [128, B * D], BF16))
    mrow2 = P2(nc.sbuf_tensor("mrow2", [1, 2 * C], F32))
    srow2 = P2(nc.sbuf_tensor("srow2", [1, C], F32))
    arow = P2(nc.sbuf_tensor("arow", [1, C], F32))
    growA = P2(nc.sbuf_tensor("growA", [1, C], F32))
    growB = P2(nc.sbuf_tensor("growB", [1, C], F32))
    hrow = P2(nc.sbuf_tensor("hrow", [1, C], F32))
    hrowB = P2(nc.sbuf_tensor("hrowB", [1, C], F32))
    mrep2 = P2(nc.sbuf_tensor("mrep2", [128, 2 * C], F32))
    g1rep = P2(nc.sbuf_tensor("g1rep", [128, 4 * C], F32))
    g2rep = P2(nc.sbuf_tensor("g2rep", [128, 4 * C], F32))
    a2rep = [P2(nc.sbuf_tensor(f"a2rep{i}", [128, 2 * C], F32)) for i in range(2)]
    mvst2 = P2(nc.sbuf_tensor("mvst2", [128, 4], F32))
    varr2 = P2(nc.sbuf_tensor("varr2", [128, 2], F32))
    msq22 = P2(nc.sbuf_tensor("msq22", [128, 2], F32))
    mx2 = P2(nc.sbuf_tensor("mx2", [128, 2], F32))
    sqmx = P2(nc.sbuf_tensor("sqmx", [128, 2], F32))
    f2tp = P2(nc.sbuf_tensor("f2tp", [128, 2], F32))
    f2tok = P2(nc.sbuf_tensor("f2tok", [128, 2], F32))
    abn2 = P2(nc.sbuf_tensor("abn2", [128, HC], F32))
    sbn1 = P2(nc.sbuf_tensor("sbn1", [128, B * KC], F32))

    def emit_xl2(ss):
        slot = ss % 4
        lastp1 = {0: NST - 4, 1: NST - 3, 2: NST - 2, 3: NST - 1}
        wv = ([ev1[("store", ss)], ev1.get(("store", lastp1[ss]))] if ss < 4
              else [ev1.get(("ostore", ss - 4)), ev1[("store", ss)]])
        b, ck = ss % 2, ss // 2
        src = dxmid.ap()[b].rearrange("(ck tt p) d -> ck p tt d", p=128, tt=TT)[ck]
        ev1[("xl2", ss)] = s.emit("sync", lambda sy, slot=slot, src=src: sy.dma_start(
            out=stage[slot][:], in_=src), wait=W(*wv), sig=s_xls[slot], amt=16)

    def emit_A2_tp(ss):
        slot = ss % 4
        last = None
        for tt in range(TT):
            for kcg in range(2):
                bk = kcg
                ev = None
                for i2 in range(4):
                    kc = kcg * 4 + i2
                    ev = s.emit("tensor", lambda t, slot=slot, tt=tt, kc=kc, i2=i2, bk=bk: t.matmul(
                        pb[bk][0:128, i2 * 128:(i2 + 1) * 128],
                        stage[slot][:, tt * 1024 + kc * 128:tt * 1024 + (kc + 1) * 128],
                        identf[:], start=(i2 == 0), stop=(i2 == 3), is_transpose=True),
                        wait=W(ev1[("xl2", ss)], ev_bank[bk],
                               ev1.get(("xh2", ss - 1)), bar2 if ss == 0 else None) if i2 == 0 else ())

                def evacx(sc, tt=tt, kcg=kcg, bk=bk):
                    dst = x2fm[:].rearrange("p (kc c) -> p kc c", c=C)[
                        :, kcg * 4:(kcg + 1) * 4, tt * 128:(tt + 1) * 128]
                    return sc.copy(dst, pb[bk][0:128, 0:512])
                last = ev_bank[bk] = s.emit("scalar", evacx, wait=W(ev))
        ev1[("xfm2", ss)] = last

    def emit_stats_a_v(ss):
        slot = ss % 4
        ev = None
        for tt in range(TT):
            s.emit("vector", lambda v, slot=slot, tt=tt: v.tensor_scalar(
                hsq1[:].bitcast(F32)[:, 0:1024], stage[slot][:, tt * 1024:(tt + 1) * 1024],
                1.0 / D, None, OP.mult, OP.add, accum_out=mvst2[:, tt:tt + 1]),
                wait=W(ev1[("xl2", ss)], ev1.get(("mrow2", ss - 1)),
                       ev1.get(("sqabs", ss - 1))) if tt == 0 else ())
            ev = s.emit("vector", lambda v, slot=slot, tt=tt: v.scalar_tensor_tensor(
                hsq1[:].bitcast(F32)[:, 0:1024], stage[slot][:, tt * 1024:(tt + 1) * 1024], 1.0,
                stage[slot][:, tt * 1024:(tt + 1) * 1024], OP.mult, OP.mult,
                accum_out=mvst2[:, 2 + tt:3 + tt]))
        ev = s.emit("vector", lambda v: v.tensor_scalar(
            varr2[:], mvst2[:, 2:4], 1.0 / D, None, OP.mult), wait=W(ev))
        ev = s.emit("vector", lambda v: v.tensor_tensor(
            msq22[:], mvst2[:, 0:2], mvst2[:, 0:2], OP.mult))
        ev = s.emit("vector", lambda v: v.tensor_tensor(varr2[:], varr2[:], msq22[:], OP.subtract))
        ev1[("acc", ss)] = ev
        ev = s.emit("scalar", lambda sc: sc.activation(varr2[:], varr2[:], AF.Ln,
                                                       bias=cvec[:, CEPS5:CEPS5 + 1]), wait=W(ev))
        ev1[("rstd", ss)] = s.emit("scalar", lambda sc: sc.activation(
            varr2[:], varr2[:], AF.Exp, scale=-0.5))

    def emit_stats_a_pe(ss):
        ev = None
        for i2, (buf_, col) in enumerate([(mvst2, 0), (mvst2, 1), (varr2, 0), (varr2, 1)]):
            ev = s.emit("tensor", lambda t, buf_=buf_, col=col, i2=i2: t.matmul(
                pb[3][0:1, i2 * 128:(i2 + 1) * 128], buf_[:, col:col + 1], identf[:],
                start=(i2 == 0), stop=(i2 == 3), is_transpose=True),
                wait=W(ev1[("rstd", ss)], ev_bank[3]) if i2 == 0 else ())
        ev_mrow = s.emit("scalar", lambda sc: sc.copy(mrow2[:], pb[3][0:1, 0:512]), wait=W(ev))
        ev_bank[3] = ev_mrow
        ev1[("mrow2", ss)] = ev_mrow
        ev = s.emit("tensor", lambda t: t.matmul(pb[2][0:128, 0:C], onesf[0:1, :],
                                                 mrow2[0:1, 0:C], start=True, stop=False),
                    wait=W(ev_mrow, ev_bank[2], ev1.get(("xh2", ss - 1))))
        ev = s.emit("tensor", lambda t: t.matmul(pb[2][0:128, C:2 * C], onesf[0:1, :],
                                                 mrow2[0:1, C:2 * C], start=False, stop=True))
        ev_mrep = s.emit("scalar", lambda sc: sc.copy(mrep2[:], pb[2][0:128, 0:2 * C]), wait=W(ev))
        ev_bank[2] = ev_mrep
        ev1[("mrep", ss)] = ev_mrep

    def emit_stats_b(ss):
        b = ss % 2
        ev = None
        for kc in range(KC):
            s.emit("vector", lambda v, kc=kc: v.tensor_tensor(
                xf_slice(xh2, kc), xf_slice(x2fm, kc), mrep2[:, 0:C], OP.subtract),
                wait=W(ev1[("xfm2", ss)], ev1[("mrep", ss)],
                       ev1.get(("xq1", ss - 1))) if kc == 0 else ())
            ev = s.emit("vector", lambda v, kc=kc, b=b: v.scalar_tensor_tensor(
                xf_slice(xh2, kc), xf_slice(xh2, kc), m_cb1[:, b * KC + kc:b * KC + kc + 1],
                mrep2[:, C:2 * C], OP.mult, OP.mult))
            ev1[("xh2k", ss, kc)] = ev
        ev1[("xh2", ss)] = ev

    def emit_SQ_a_s(ss):
        b = ss % 2
        for kc in range(KC):
            ev1[("hsqk", ss, kc)] = s.emit("scalar", lambda sc, kc=kc, b=b: sc.activation(
                xf_slice(hsq1, kc), xf_slice(xh2, kc), AF.Square,
                bias=m_sb[:, b * KC + kc:b * KC + kc + 1]),
                wait=W(ev1[("xh2k", ss, kc)], ev1[("acc", ss)] if kc == 0 else None))
        ev = None
        for kc in range(KC):
            ev = s.emit("scalar", lambda sc, kc=kc, b=b: sc.activation(
                xf_slice(xh2, kc), xf_slice(xh2, kc), AF.Identity,
                scale=n1v[:, kc:kc + 1], bias=sbn1[:, b * KC + kc:b * KC + kc + 1]))
        ev1[("xn1", ss)] = ev

    def emit_SQ_row(ss):
        ev = s.emit("scalar", lambda sc: sc.copy(srow2[:], pb[3][0:1, 0:C]),
                    wait=[lambda ss=ss: ev1[("ssum1", ss)]])
        ev1[("srowev", ss)] = ev
        ev = s.emit("scalar", lambda sc: sc.activation(srow2[:], srow2[:], AF.Ln, scale=1.0 / D,
                                                       bias=cvec[0:1, CEPS8:CEPS8 + 1]))
        ev1[("r1", ss)] = s.emit("scalar", lambda sc: sc.activation(
            srow2[:], srow2[:], AF.Exp, scale=-0.5))
        ev1[("sqabs", ss)] = s.emit("scalar", lambda sc: sc.activation(hsq1[:], xh2[:], AF.Abs),
                                    wait=[lambda ss=ss: ev1[("ssum1", ss)]] + W(ev1[("xn1", ss)]))

    def emit_SQ_sums(ss):
        ev = None
        for kc in range(KC):
            ev = s.emit("tensor", lambda t, kc=kc: t.matmul(
                pb[3][0:1, 0:C], onesb[:], xf_slice(hsq1, kc),
                start=(kc == 0), stop=(kc == KC - 1)),
                wait=W(ev1[("hsqk", ss, kc)], ev_bank[3] if kc == 0 else None))
        ev1[("ssum1", ss)] = ev
        ev_bank[3] = ev1[("srowev", ss)]

    def emit_SQ_b(ss):
        ev = None
        for sz in [4, 2, 1]:
            ev = s.emit("vector", lambda v, sz=sz: v.tensor_tensor(
                hsq1[:, 0:sz * C], hsq1[:, 0:sz * C], hsq1[:, sz * C:2 * sz * C], OP.max),
                wait=W(ev1[("sqabs", ss)]) if sz == 4 else ())
        ev_tr = s.emit("vector", lambda v: v.tensor_copy(w2d[:, 0:C], hsq1[:, 0:C]),
                       wait=W(ev, ev1.get(("p2tp", ss - 2))))
        evm = None
        for tt in range(TT):
            ev = s.emit("tensor", lambda t, tt=tt: t.matmul(
                pb[0][0:128, 0:128], w2d[:, tt * 128:(tt + 1) * 128], identf[:],
                start=True, stop=True, is_transpose=True),
                wait=W(ev_tr, ev_bank[0]))
            ev = s.emit("scalar", lambda sc, tt=tt: sc.copy(
                w2d[:, C + tt * 128:C + (tt + 1) * 128], pb[0][0:128, 0:128]), wait=W(ev))
            ev_bank[0] = ev
            evm = s.emit("vector", lambda v, tt=tt: v.tensor_reduce(
                sqmx[:, tt:tt + 1], w2d[:, C + tt * 128:C + (tt + 1) * 128], AX.X, OP.max),
                wait=W(ev))
        ev1[("sqtp", ss)] = ev_bank[0]
        evc = s.emit("vector", lambda v: v.tensor_scalar(
            sqmx[:], sqmx[:], 1e-5, None, OP.max), wait=W(evm))
        ev = None
        for tt in range(TT):
            ev = s.emit("tensor", lambda t, tt=tt: t.matmul(
                pb[1][0:1, tt * 128:(tt + 1) * 128], sqmx[:, tt:tt + 1], identf[:],
                start=(tt == 0), stop=(tt == TT - 1), is_transpose=True),
                wait=W(evc, ev_bank[1]) if tt == 0 else ())
        ev = s.emit("scalar", lambda sc: sc.copy(growA[:], pb[1][0:1, 0:C]), wait=W(ev))
        ev_bank[1] = ev
        ev = s.emit("vector", lambda v: v.tensor_tensor(growA[:], growA[:], srow2[:], OP.mult),
                    wait=W(ev, ev1[("r1", ss)]))
        ev = s.emit("vector", lambda v: v.tensor_scalar(growA[:], growA[:], 1e-5, None, OP.max))
        ev = s.emit("scalar", lambda sc: sc.activation(growB[:], growA[:], AF.Ln), wait=W(ev))
        ev_s1 = s.emit("scalar", lambda sc: sc.activation(growB[:], growB[:], AF.Exp, scale=-1.0,
                                                          bias=cvec[0:1, CL127:CL127 + 1]))
        ev_g1 = s.emit("vector", lambda v: v.tensor_tensor(growB[:], growB[:], srow2[:], OP.mult),
                       wait=W(ev_s1))
        ev_a = s.emit("vector", lambda v: v.tensor_scalar(
            arow[:], growA[:], w1sr[:], 1.0 / 127.0, OP.mult, OP.mult), wait=W(ev))
        ev_a = s.emit("scalar", lambda sc: sc.activation(arow[:], arow[:], AF.Square),
                      wait=W(ev_a))
        ev_a = s.emit("vector", lambda v: v.tensor_scalar(
            arow[:], arow[:], ascr[:], None, OP.mult), wait=W(ev_a))
        ev = s.emit("tensor", lambda t: t.matmul(pb[3][0:128, 0:C], onesf[0:1, :], growB[0:1, :],
                                                 start=True, stop=True),
                    wait=W(ev_g1, ev_bank[3], ev1.get(("q1", ss - 1))))
        evg = s.emit("scalar", lambda sc: sc.copy(g1rep[:, 0:C], pb[3][0:128, 0:C]), wait=W(ev))
        ev_bank[3] = evg
        evg = s.emit("vector", lambda v: v.tensor_copy(g1rep[:, C:2 * C], g1rep[:, 0:C]),
                     wait=W(evg))
        evg = s.emit("vector", lambda v: v.tensor_copy(g1rep[:, 2 * C:4 * C], g1rep[:, 0:2 * C]))
        ev1[("g1rep", ss)] = evg
        ev = s.emit("tensor", lambda t: t.matmul(pb[3][0:128, 0:C], onesf[0:1, :], arow[0:1, :],
                                                 start=True, stop=True),
                    wait=W(ev_a, ev_bank[3], ev1.get(("p1a", ss - 2))))
        eva = s.emit("scalar", lambda sc, ss=ss: sc.copy(a2rep[ss % 2][:, 0:C], pb[3][0:128, 0:C]),
                     wait=W(ev))
        ev_bank[3] = eva
        eva = s.emit("vector", lambda v, ss=ss: v.tensor_copy(
            a2rep[ss % 2][:, C:2 * C], a2rep[ss % 2][:, 0:C]), wait=W(eva))
        ev1[("a2rep", ss)] = eva

    def emit_Q1(ss):
        p = ss % 2
        ev = None
        for j in range(2):
            ev = s.emit("vector", lambda v, j=j: v.tensor_tensor(
                xh2[:, j * 4 * C:(j + 1) * 4 * C], xh2[:, j * 4 * C:(j + 1) * 4 * C],
                g1rep[:], OP.mult),
                wait=W(ev1[("g1rep", ss)], ev1[("xn1", ss)],
                       ev1.get(("M1done", ss - 2))) if j == 0 else ())
        ev = s.emit("vector", lambda v, p=p: v.tensor_scalar(
            xq1[p][:], xh2[:], RMAGIC, RMAGIC, OP.add, OP.subtract), wait=W(ev))
        ev1[("xq1", ss)] = ev
        ev1[("q1", ss)] = ev

    def emit_M1(ss, half, q=None):
        p = ss % 2
        g0 = ffn_m1g0[ss]
        ev_evac = None
        evlast = None
        hcxr = range(0, HHALF, 2) if q is None else range(q * 8, q * 8 + 8, 2)
        for hcx in hcxr:
            hc0 = half * HHALF + hcx
            bk = 4 + ((hcx // 2 + 2) % 4)
            ev = None
            for hi in range(2):
                hc = hc0 + hi
                g = g0 + hc
                for kc in range(KC):
                    first = hi == 0 and kc == 0
                    ev = s.emit("tensor", lambda t, g=g, kc=kc, bk=bk, p=p, hi=hi,
                                first=first: t.matmul(
                                    pb[bk][0:128, hi * C:(hi + 1) * C],
                                    ring_ap(ringf, g, kc * 128, 128),
                                    xq1[p][:, kc * C:(kc + 1) * C],
                                    start=first, stop=(hi == 1 and kc == KC - 1)),
                        wait=W(ring_wait(g), ev1[("xq1", ss)],
                               ev_bank[bk] if first else None) if kc == 0 else ())
                ring_release(g, ev)
            evlast = ev
            ev_evac = ev_bank[bk] = s.emit("scalar", lambda sc, hc0=hc0, bk=bk: sc.activation(
                Rb[:, hc0 * C:(hc0 + 2) * C], pb[bk][0:128, 0:2 * C], AF.Relu),
                wait=W(ev, ev1.get(("Rfree", ss - 1, half))))
        if q is None or q == 1:
            ev1[("Rh", ss, half)] = ev_evac
        if half == 1 and q == 1:
            ev1[("M1done", ss)] = evlast

    def hf_slice(buf, hc):
        return buf[:, hc * C:(hc + 1) * C]

    def emit_p1(ts):
        for half in range(2):
            h0 = half * HHALF
            ev = s.emit("vector", lambda v, h0=h0: v.tensor_tensor(
                Rb[:, h0 * C:(h0 + HHALF) * C], Rb[:, h0 * C:(h0 + HHALF) * C],
                Rb[:, h0 * C:(h0 + HHALF) * C], OP.mult),
                wait=W(ev1[("Rh", ts, half)]))
            for q in range(8):
                ev = s.emit("vector", lambda v, h0=h0, q=q, ts=ts: v.tensor_tensor(
                    Rb[:, (h0 + 2 * q) * C:(h0 + 2 * q + 2) * C],
                    Rb[:, (h0 + 2 * q) * C:(h0 + 2 * q + 2) * C],
                    a2rep[ts % 2][:], OP.mult),
                    wait=W(ev1[("a2rep", ts)]) if q == 0 else ())
            ev1[("p1a", ts, half)] = ev
            evw = None
            for hcx in range(HHALF):
                hc = h0 + hcx
                evw = s.emit("scalar", lambda sc, hc=hc: sc.activation(
                    hf_slice(B2, hc), hf_slice(Rb, hc), AF.Identity,
                    scale=n2v[:, hc:hc + 1], bias=abn2[:, hc:hc + 1]),
                    wait=W(ev, ev1.get(("xq2", ts - 1))) if hcx == 0 else ())
            evsq = s.emit("scalar", lambda sc, h0=h0: sc.activation(
                h2b[:, h0 * C:(h0 + HHALF) * C], Rb[:, h0 * C:(h0 + HHALF) * C],
                AF.Square, bias=abi[:]),
                wait=W(evw, ev1.get(("treedone", ts - 1))))
            ev1[("Rfree", ts, half)] = evsq
            ev1[("h2b", ts, half)] = evsq
        ev1[("p1a", ts)] = ev1[("p1a", ts, 1)]

    def emit_p2row_sums(ts):
        ev = None
        for hc in range(HC):
            ev = s.emit("tensor", lambda t, hc=hc: t.matmul(
                pb[3][0:1, 0:C], onesb[:], hf_slice(h2b, hc),
                start=(hc == 0), stop=(hc == HC - 1)),
                wait=W(ev1[("h2b", ts, 0)], ev1[("h2b", ts, 1)],
                       ev_bank[3]) if hc == 0 else ())
        ev1[("ssum2", ts)] = ev
        ev = s.emit("scalar", lambda sc: sc.copy(hrow[:], pb[3][0:1, 0:C]), wait=W(ev))
        ev_bank[3] = ev
        ev = s.emit("scalar", lambda sc: sc.activation(hrow[:], hrow[:], AF.Ln, scale=1.0 / H,
                                                       bias=cvec[0:1, CEPS8:CEPS8 + 1]))
        ev1[("r2", ts)] = s.emit("scalar", lambda sc: sc.activation(
            hrow[:], hrow[:], AF.Exp, scale=-0.5))

    def emit_p2row_abs(ts):
        ev = s.emit("scalar", lambda sc: sc.activation(h2b[:], B2[:], AF.Abs),
                    wait=W(ev1[("ssum2", ts)]))
        ev1[("absdone", ts)] = ev
        for sz in [16, 8, 4, 2, 1]:
            ev = s.emit("vector", lambda v, sz=sz: v.tensor_tensor(
                h2b[:, 0:sz * C], h2b[:, 0:sz * C], h2b[:, sz * C:2 * sz * C], OP.max),
                wait=W(ev) if sz == 16 else ())
        ev = s.emit("vector", lambda v: v.tensor_copy(w2d[:, 0:C], h2b[:, 0:C]),
                    wait=W(ev, ev1.get(("sqtp", ts + 1))))
        ev1[("treedone", ts)] = ev

    def emit_p2row_rows(ts):
        evm = None
        for tt in range(TT):
            ev2 = s.emit("tensor", lambda t, tt=tt: t.matmul(
                pb[0][0:128, 0:128], w2d[:, tt * 128:(tt + 1) * 128], identf[:],
                start=True, stop=True, is_transpose=True),
                wait=W(ev1[("treedone", ts)], ev_bank[0]))
            ev2 = s.emit("scalar", lambda sc, tt=tt: sc.copy(
                w2d[:, C + tt * 128:C + (tt + 1) * 128], pb[0][0:128, 0:128]), wait=W(ev2))
            ev_bank[0] = ev2
            evm = s.emit("vector", lambda v, tt=tt: v.tensor_reduce(
                mx2[:, tt:tt + 1], w2d[:, C + tt * 128:C + (tt + 1) * 128], AX.X, OP.max),
                wait=W(ev2))
        ev1[("p2tp", ts)] = ev_bank[0]
        evc = s.emit("vector", lambda v: v.tensor_scalar(mx2[:], mx2[:], 0.0, None, OP.max),
                     wait=W(evm))
        ev1[("mx2", ts)] = evc
        ev = None
        for tt in range(TT):
            ev = s.emit("tensor", lambda t, tt=tt: t.matmul(
                pb[1][0:1, tt * 128:(tt + 1) * 128], mx2[:, tt:tt + 1], identf[:],
                start=(tt == 0), stop=(tt == TT - 1), is_transpose=True),
                wait=W(evc, ev_bank[1]) if tt == 0 else ())
        ev = s.emit("scalar", lambda sc: sc.copy(hrowB[:], pb[1][0:1, 0:C]), wait=W(ev))
        ev_bank[1] = ev
        ev = s.emit("vector", lambda v: v.tensor_tensor(hrowB[:], hrowB[:], hrow[:], OP.mult),
                    wait=W(ev, ev1[("r2", ts)]))
        ev = s.emit("vector", lambda v: v.tensor_scalar(hrowB[:], hrowB[:], 1e-5, None, OP.max))
        ev = s.emit("scalar", lambda sc: sc.activation(hrowB[:], hrowB[:], AF.Ln), wait=W(ev))
        ev = s.emit("scalar", lambda sc: sc.activation(hrowB[:], hrowB[:], AF.Exp, scale=-1.0,
                                                       bias=cvec[0:1, CL127:CL127 + 1]))
        ev_g2 = s.emit("vector", lambda v: v.tensor_tensor(hrowB[:], hrowB[:], hrow[:], OP.mult),
                       wait=W(ev))
        ev = s.emit("tensor", lambda t: t.matmul(pb[3][0:128, 0:C], onesf[0:1, :], hrowB[0:1, :],
                                                 start=True, stop=True),
                    wait=W(ev_g2, ev_bank[3], ev1.get(("xq2", ts - 1))))
        evg = s.emit("scalar", lambda sc: sc.copy(g2rep[:, 0:C], pb[3][0:128, 0:C]), wait=W(ev))
        ev_bank[3] = evg
        evg = s.emit("vector", lambda v: v.tensor_copy(g2rep[:, C:2 * C], g2rep[:, 0:C]),
                     wait=W(evg))
        evg = s.emit("vector", lambda v: v.tensor_copy(g2rep[:, 2 * C:4 * C], g2rep[:, 0:2 * C]))
        ev1[("g2rep", ts)] = evg
        evt = None
        for tt in range(TT):
            evt = s.emit("tensor", lambda t, tt=tt: t.matmul(
                pb[1][0:128, tt:tt + 1], hrow[0:1, tt * 128:(tt + 1) * 128],
                onesf[0:1, 0:1], start=(tt == 0), stop=(tt == TT - 1), is_transpose=True),
                wait=W(ev1[("r2", ts)], ev_bank[1]) if tt == 0 else ())
        evt = s.emit("scalar", lambda sc: sc.copy(f2tp[:, 0:TT], pb[1][0:128, 0:TT]),
                     wait=W(evt))
        ev_bank[1] = evt
        evf = s.emit("vector", lambda v: v.tensor_tensor(
            f2tok[:], mx2[:], f2tp[:, 0:TT], OP.mult), wait=W(evt, ev1[("mx2", ts)]))
        evf = s.emit("vector", lambda v: v.tensor_scalar(
            f2tok[:], f2tok[:], 1e-5, None, OP.max))
        evf = s.emit("vector", lambda v: v.tensor_scalar(
            f2tok[:], f2tok[:], 1.0 / 127.0, w2sv[:], OP.mult, OP.mult))
        ev1[("f2", ts)] = evf

    def emit_quant2(ts):
        ev = None
        for j in range(8):
            ev = s.emit("vector", lambda v, j=j: v.tensor_tensor(
                B2[:, j * 4 * C:(j + 1) * 4 * C], B2[:, j * 4 * C:(j + 1) * 4 * C],
                g2rep[:], OP.mult),
                wait=W(ev1[("g2rep", ts)], ev1[("absdone", ts)],
                       ev1.get(("M2done", ts - 1))) if j == 0 else ())
        ev = s.emit("vector", lambda v: v.tensor_scalar(
            xq2[:], B2[:], RMAGIC, RMAGIC, OP.add, OP.subtract), wait=W(ev))
        ev1[("xq2", ts)] = ev

    def emit_M2(ts, oc, part=0):
        b, ck = ts % 2, ts // 2
        g0 = ffn_m2g0[ts]
        slot = ts % 4
        if oc == 0:
            kr = range(0, 4) if part == 0 else range(4, 16)
        else:
            kr = range(0, 8) if part == 0 else range(8, 16)
        bks = (6, 7) if oc == 0 else (4, 5)
        for kcp in kr:
            g = g0 + oc * 16 + kcp
            for jj in range(2):
                kc = kcp * 2 + jj
                for t_ in range(TT):
                    bk = bks[t_]
                    wv = []
                    if jj == 0 and t_ == 0:
                        wv.append(ring_wait(g))
                    if kc == 0:
                        wv += [ev1[("xq2", ts)], ev_bank[bk]]
                    ev1[("m2mm", ts, oc, t_)] = s.emit(
                        "tensor", lambda t, g=g, kc=kc, jj=jj, t_=t_, bk=bk: t.matmul(
                            pb[bk][0:128, 0:512],
                            xq2[:, kc * C + t_ * 128:kc * C + (t_ + 1) * 128],
                            ring_ap(ringf, g, jj * 512, 512),
                            start=(kc == 0), stop=(kc == HC - 1)),
                        wait=W(*wv))
                    if jj == 1 and t_ == TT - 1:
                        ring_release(g, ev1[("m2mm", ts, oc, t_)])
        if part == 1:
            evr = ev1.get(("m2evr", ts))
            for t_ in range(TT):
                bk = bks[t_]
                ev = s.emit("vector", lambda v, bk=bk, b=b, oc=oc, t_=t_: v.scalar_tensor_tensor(
                    m2scr[:], pb[bk][0:128, 0:512], f2tok[:, t_:t_ + 1],
                    gb_rep[:, b * D + oc * 512:b * D + (oc + 1) * 512], OP.mult, OP.mult),
                    wait=W(ev1[("m2mm", ts, oc, t_)], ev1[("f2", ts)], gb_ready))
                evr = s.emit("vector", lambda v, slot=slot, t_=t_, oc=oc: v.tensor_tensor(
                    stage[slot][:, t_ * 1024 + oc * 512:t_ * 1024 + (oc + 1) * 512],
                    stage[slot][:, t_ * 1024 + oc * 512:t_ * 1024 + (oc + 1) * 512],
                    m2scr[:], OP.add))
                ev_bank[bk] = evr
            ev1[("m2evr", ts)] = evr
        if oc == 1 and part == 1:
            ev1[("M2done", ts)] = ev1[("m2mm", ts, 1, TT - 1)]
            dsto = dout.ap()[b].rearrange("(ck tt p) d -> ck p tt d", p=128, tt=TT)[ck]
            ev1[("ostore", ts)] = s.emit("sync", lambda sy, slot=slot, dsto=dsto: sy.dma_start(
                out=dsto, in_=stage[slot][:]), wait=W(ev1[("m2evr", ts)]),
                sig=s_sos[slot], amt=16)

    if build_stage >= 2:
        s.emit("sync", lambda sy: sy.dma_start(out=gb_rep[:], in_=dws_grep.ap()[:, B * D:]),
               wait=W(bar2, ev_grs), sig=s_gr, amt=16)
        gb_ready = (s_gr, s.cnt["s_gr"])
        s.emit("vector", lambda v: v.tensor_scalar(
            abn2[:], n2v[:], abi[:], None, OP.mult), wait=W(bar2))
        for b_ in range(B):
            s.emit("vector", lambda v, b_=b_: v.tensor_tensor(
                sbn1[:, b_ * KC:(b_ + 1) * KC], m_sb[:, b_ * KC:(b_ + 1) * KC],
                n1v[:, 0:KC], OP.mult))
        emit_xl2(0)
        emit_xl2(1)
        emit_A2_tp(0)
        emit_stats_a_v(0)
        emit_stats_a_pe(0)
        emit_stats_b(0)
        emit_SQ_a_s(0)
        emit_SQ_row(0)
        emit_SQ_sums(0)
        emit_SQ_b(0)
        emit_Q1(0)
        for it in range(NST + 1):
            nss = it + 1
            if nss + 1 < NST:
                emit_xl2(nss + 1)
            if it >= 1:
                emit_p2row_sums(it - 1)
                emit_p2row_abs(it - 1)
            if it < NST:
                emit_M1(it, 0)
            if it >= 1:
                emit_p2row_rows(it - 1)
            if it >= 1:
                emit_quant2(it - 1)
            if nss < NST:
                emit_stats_a_v(nss)
                emit_A2_tp(nss)
            if it < NST:
                emit_M1(it, 1, 0)
            if nss < NST:
                emit_stats_a_pe(nss)
            if it < NST:
                emit_M1(it, 1, 1)
            if nss < NST:
                emit_stats_b(nss)
                emit_SQ_a_s(nss)
                emit_SQ_row(nss)
            if it < NST:
                emit_p1(it)
            if it >= 1:
                emit_M2(it - 1, 0, 0)
            if nss < NST:
                emit_SQ_sums(nss)
            if it >= 1:
                emit_M2(it - 1, 0, 1)
                emit_M2(it - 1, 1, 0)
            if nss < NST:
                emit_SQ_b(nss)
                emit_Q1(nss)
            if it >= 1:
                emit_M2(it - 1, 1, 1)
    p2.close()

    tap_src = {"xmid": dxmid, "mod": dws_mod, "w1t": dws_w1t, "w2t": dws_w2t,
               "attn": dws_attn, "wo": dws_wo}
    all_done = [(s_sos[i], s.cnt[f"s_so{i}"]) for i in range(4)] + sc_done_l
    prev_tap = None
    for name in taps:
        srcd = tap_src[name]
        prev_tap = s.emit("gpsimd", lambda sy, a=srcd, d=tap_d[name]: sy.dma_start(
            out=d.ap(), in_=a.ap()), wait=all_done + W(prev_tap), sig=s_st, amt=16)
        all_done = []
    fin = [(s_sos[i], s.cnt[f"s_so{i}"]) for i in range(4)] + [(s_st, s.cnt["s_st"])]
    s.emit("sync", lambda sy: None, wait=fin)

    block = es.enter_context(nc.Block())
    s.install(block, self_waits=debug_sync)
    es.close()
    return nc


_CACHE = {}


def kernel(L0_attn_x, L0_attn_gamma, L0_attn_wq, L0_attn_wk, L0_attn_wv,
           L0_attn_wa, L0_attn_wg, L0_attn_wo, L0_ffn_norm1_w, L0_ffn_w1,
           L0_ffn_act_scale, L0_ffn_act_bias, L0_ffn_norm2_w, L0_ffn_w2,
           L0_c, L0_wmod):
    if "nc" not in _CACHE:
        _CACHE["nc"] = build_program()
    nc = _CACHE["nc"]
    cc = np.ascontiguousarray
    shared = {
        "wq": cc(L0_attn_wq), "wk": cc(L0_attn_wk), "wv": cc(L0_attn_wv),
        "wa": cc(L0_attn_wa), "wg": cc(L0_attn_wg), "wo": cc(L0_attn_wo),
        "gamma": cc(L0_attn_gamma), "n1w": cc(L0_ffn_norm1_w),
        "n2w": cc(L0_ffn_norm2_w), "w1": cc(L0_ffn_w1), "w2": cc(L0_ffn_w2),
        "ascale": cc(L0_ffn_act_scale), "abias": cc(L0_ffn_act_bias),
        "wmod": cc(L0_wmod),
    }
    in_maps = []
    for i in range(NCORE):
        m = dict(shared)
        m["x"] = cc(L0_attn_x[i * B:(i + 1) * B])
        m["c"] = cc(L0_c[i * B:(i + 1) * B])
        in_maps.append(m)
    res = run_bass_kernel_spmd(nc, in_maps, core_ids=list(range(NCORE)))
    return np.concatenate([r["out"] for r in res.results], axis=0)


# version 25
# speedup vs baseline: 1.0261x; 1.0261x over previous
import contextlib
import math

import numpy as np
import concourse.bass as bass
import concourse.mybir as mybir
from concourse.bass_utils import run_bass_kernel_spmd

F32 = mybir.dt.float32
BF16 = mybir.dt.bfloat16
FP8 = mybir.dt.float8e4
I32 = mybir.dt.int32
AF = mybir.ActivationFunctionType
OP = mybir.AluOpType
AX = mybir.AxisListType

NCORE = 8
B = 2
N = 2048
D = 1024
H = 4096
KC = D // 128
HC = H // 128
C = 256
TT = C // 128
NCH = N // C
NST = B * NCH
SQD = float(math.sqrt(D))
RMAGIC = 12582912.0
FORCE_ZERO_STATE = False
PI = float(np.pi)

ENGINES = ("sync", "vector", "scalar", "tensor", "gpsimd")


COMPUTE_ENGINES = ("vector", "scalar", "tensor", "gpsimd")


class Sched:

    def __init__(self, nc, es):
        self.nc = nc
        self.q = {e: [] for e in ENGINES}
        self.cnt = {}
        self._names = {}
        self.es = es
        self.esem = {}
        self.ecnt = {e: 0 for e in ENGINES}

    def sem(self, name):
        h = self.es.enter_context(self.nc.semaphore(name))
        self._names[id(h)] = name
        self.cnt[name] = 0
        return h

    def init_engine_sems(self):
        for e in COMPUTE_ENGINES:
            self.esem[e] = self.sem("se_" + e)

    def emit(self, eng, fn, wait=(), sig=None, amt=1):
        wait = tuple(w for w in wait if w is not None and (callable(w) or w[0] is not None))
        prior = self.ecnt[eng] if eng in COMPUTE_ENGINES else 0
        ev = None
        if sig is not None:
            nm = self._names[id(sig)]
            self.cnt[nm] += amt
            ev = (sig, self.cnt[nm])
        elif eng in COMPUTE_ENGINES:
            self.ecnt[eng] += 1
            ev = (self.esem[eng], self.ecnt[eng])
        self.q[eng].append((wait, fn, ev, amt, prior))
        return ev

    def install(self, block, self_waits=False):
        for eng in ENGINES:
            items = self.q[eng]
            dec = getattr(block, eng)
            es_ = self.esem.get(eng)
            sw = self_waits or eng in ("vector", "scalar", "gpsimd")

            def make(items, es_, sw=sw):
                def body(e):
                    for waits, fn, ev, amt, prior in items:
                        if sw and es_ is not None and prior > 0:
                            e.wait_ge(es_, prior)
                        for w in waits:
                            s_, v = w() if callable(w) else w
                            e.wait_ge(s_, v)
                        instr = fn(e)
                        if ev is not None:
                            if instr is None:
                                e.sem_inc(ev[0], amt)
                            else:
                                instr.then_inc(ev[0], amt)
                return body

            dec(make(items, es_))


def build_program(taps=None, build_stage=2, debug_sync=False):
    taps = taps if taps is not None else {}
    nc = bass.Bass("TRN2", target_bir_lowering=False,
                   detect_race_conditions=debug_sync)

    dx = nc.dram_tensor("x", [B, N, D], F32, kind="ExternalInput")
    dwq = nc.dram_tensor("wq", [D, D], F32, kind="ExternalInput")
    dwk = nc.dram_tensor("wk", [D, D], F32, kind="ExternalInput")
    dwv = nc.dram_tensor("wv", [D, D], F32, kind="ExternalInput")
    dwa = nc.dram_tensor("wa", [D, 2 * D], F32, kind="ExternalInput")
    dwg = nc.dram_tensor("wg", [D, D], F32, kind="ExternalInput")
    dwo = nc.dram_tensor("wo", [D, D], F32, kind="ExternalInput")
    dgamma = nc.dram_tensor("gamma", [D], F32, kind="ExternalInput")
    dn1w = nc.dram_tensor("n1w", [D], F32, kind="ExternalInput")
    dn2w = nc.dram_tensor("n2w", [H], F32, kind="ExternalInput")
    dw1 = nc.dram_tensor("w1", [H, D], F32, kind="ExternalInput")
    dw2 = nc.dram_tensor("w2", [D, H], F32, kind="ExternalInput")
    dascale = nc.dram_tensor("ascale", [1], F32, kind="ExternalInput")
    dabias = nc.dram_tensor("abias", [1], F32, kind="ExternalInput")
    dc = nc.dram_tensor("c", [B, D], F32, kind="ExternalInput")
    dwmod = nc.dram_tensor("wmod", [D, 6 * D], F32, kind="ExternalInput")
    dout = nc.dram_tensor("out", [B, N, D], F32, kind="ExternalOutput")

    NBLK_ATTN = 4 * 64 + 16 * 8
    dws_attn = nc.dram_tensor("ws_attn", [128, NBLK_ATTN * 128], BF16, kind="Internal")
    dws_wo = nc.dram_tensor("ws_wo", [D, D], BF16, kind="Internal")
    dws_w1t = nc.dram_tensor("ws_w1t", [128, HC * KC * 128], FP8, kind="Internal")
    dws_w2t = nc.dram_tensor("ws_w2t", [H, D], FP8, kind="Internal")
    dws_mod = nc.dram_tensor("ws_mod", [B, 6 * D], F32, kind="Internal")
    dws_grep = nc.dram_tensor("ws_grep", [128, 2 * B * D], BF16, kind="Internal")
    dxmid = nc.dram_tensor("xmid", [B, N, D], F32, kind="Internal")

    tap_d = {}
    for name, shape in taps.items():
        tap_d[name] = nc.dram_tensor("tap_" + name, list(shape), F32, kind="ExternalOutput")

    es = contextlib.ExitStack()
    E = es.enter_context
    s = Sched(nc, es)

    s.init_engine_sems()
    s_ld = s.sem("s_ld")
    s_wrs = [s.sem(f"s_wr{i}") for i in range(6)]
    s_wfr = [s.sem(f"s_wfr{i}") for i in range(6)]
    s_xls = [s.sem(f"s_xl{i}") for i in range(4)]
    s_sos = [s.sem(f"s_so{i}") for i in range(4)]
    s_wm = [s.sem(f"s_wm{i}") for i in range(4)]
    s_wg = [s.sem(f"s_wg{i}") for i in range(3)]
    s_ws = [s.sem(f"s_ws{i}") for i in range(3)]
    s_wf = [s.sem(f"s_wf{i}") for i in range(4)]
    s_tt = [s.sem(f"s_tt{i}") for i in range(4)]
    s_ms = s.sem("s_ms")
    s_msb = [s.sem(f"s_msb{i}") for i in range(2)]
    s_mv = s.sem("s_mv")
    s_gr = s.sem("s_gr")
    s_st = s.sem("s_st")
    s_bar = s.sem("s_bar")

    identf = E(nc.sbuf_tensor("identf", [128, 128], F32))
    identb = E(nc.sbuf_tensor("identb", [128, 128], BF16))
    onesf = E(nc.sbuf_tensor("onesf", [128, 128], F32))
    onesb = E(nc.sbuf_tensor("onesb", [128, 1], BF16))
    zeroC = E(nc.sbuf_tensor("zeroC", [128, C], F32))
    cvec = E(nc.sbuf_tensor("cvec", [128, 10], F32))
    CEPS5, CEPS8, CPI2, CLSD, C19, C18, CL127, CZERO, CONE, CNEG1 = range(10)
    gam = E(nc.sbuf_tensor("gam", [128, KC], F32))
    n1v = E(nc.sbuf_tensor("n1v", [128, KC], F32))
    n2v = E(nc.sbuf_tensor("n2v", [128, HC], F32))
    asc = E(nc.sbuf_tensor("asc", [128, 1], F32))
    abi = E(nc.sbuf_tensor("abi", [128, 1], F32))
    ascr = E(nc.sbuf_tensor("ascr", [1, 1], F32))
    abir = E(nc.sbuf_tensor("abir", [1, 1], F32))
    m_sa = E(nc.sbuf_tensor("m_sa", [128, B * KC], F32))
    m_ca1 = E(nc.sbuf_tensor("m_ca1", [128, B * KC], F32))
    m_ga = E(nc.sbuf_tensor("m_ga", [128, B * KC], F32))
    m_sb = E(nc.sbuf_tensor("m_sb", [128, B * KC], F32))
    m_cb1 = E(nc.sbuf_tensor("m_cb1", [128, B * KC], F32))
    m_gb = E(nc.sbuf_tensor("m_gb", [128, B * KC], F32))
    w1sv = E(nc.sbuf_tensor("w1sv", [128, 1], F32))
    w2sv = E(nc.sbuf_tensor("w2sv", [128, 1], F32))
    w1sr = E(nc.sbuf_tensor("w1sr", [1, 1], F32))
    w2sr = E(nc.sbuf_tensor("w2sr", [1, 1], F32))
    stage = [E(nc.sbuf_tensor(f"stage{i}", [128, TT * 1024], F32)) for i in range(4)]
    NSLOT = 4
    NSLOTF = 6
    ring = E(nc.sbuf_tensor("ring", [128, NSLOT * 1024], BF16))
    ringf = E(nc.sbuf_tensor("ringf", [128, NSLOTF * 1024], FP8))

    pb = [E(nc.psum_tensor(f"pb{i}", [128, 512], F32)) for i in range(8)]
    ev_bank = [None] * 8

    W = lambda *evs: [e for e in evs if e is not None]

    ev_gp0 = None
    s.emit("gpsimd", lambda g: g.iota(identi[:], pattern=[[1, 128]], base=0,
                                      channel_multiplier=-1))
    s.emit("gpsimd", lambda g: g.memset(onesf[:], 1.0))
    s.emit("gpsimd", lambda g: g.memset(zeroC[:], 0.0))
    for j, valc in enumerate([1e-5, 1e-8, PI / 2, math.log(32.0), 1e-19, 1e-18,
                              math.log(127.0), 0.0, 1.0, -1.0]):
        ev_gp0 = s.emit("gpsimd", lambda g, j=j, v=valc: g.memset(cvec[:, j:j + 1], v))
    ev_idf = s.emit("vector", lambda v: v.tensor_scalar(identf[:], identi[:], 0.0, None, OP.is_equal),
                    wait=W(ev_gp0))
    ev_idb = s.emit("vector", lambda v: v.tensor_scalar(identb[:], identi[:], 0.0, None, OP.is_equal))
    ev_onb = s.emit("vector", lambda v: v.tensor_copy(onesb[:], onesf[:, 0:1]),
                    wait=W(ev_gp0))

    p0 = contextlib.ExitStack()
    P0 = p0.enter_context
    identi = P0(nc.sbuf_tensor("identi", [128, 128], I32))
    csb = P0(nc.sbuf_tensor("csb", [B, D], F32))

    def featvec_load(dst, dram, nk):
        ap = dram.ap().rearrange("(kc p) -> p kc", p=128)
        return s.emit("sync", lambda sy: sy.dma_start(out=dst[:, 0:nk], in_=ap,
                                                      allow_slow_non_contiguous=True),
                      sig=s_ld, amt=16)

    featvec_load(gam, dgamma, KC)
    featvec_load(n1v, dn1w, HC if False else KC)
    featvec_load(n2v, dn2w, HC)
    tiny = P0(nc.sbuf_tensor("tiny", [1, 2], F32))
    s.emit("sync", lambda sy: sy.dma_start(out=tiny[0:1, 0:1], in_=dascale.ap().unsqueeze(0)),
           sig=s_ld, amt=16)
    ev_t2 = s.emit("sync", lambda sy: sy.dma_start(out=tiny[0:1, 1:2], in_=dabias.ap().unsqueeze(0)),
                   sig=s_ld, amt=16)
    ev_cl = s.emit("sync", lambda sy: sy.dma_start(out=csb[:], in_=dc.ap()), sig=s_ld, amt=16)
    ev_misc = ev_cl
    ev_gs = s.emit("vector", lambda v: v.tensor_scalar(gam[:, 0:KC], gam[:, 0:KC], SQD, None, OP.mult),
                   wait=W(ev_misc))
    ev_t1 = ev_misc
    ev_t2 = ev_misc
    ev_mm = s.emit("tensor", lambda t: t.matmul(pb[3][0:128, 0:2], onesf[0:1, :], tiny[:],
                                                start=True, stop=True),
                   wait=W(ev_t1, ev_t2, ev_gp0))
    ev_as = s.emit("scalar", lambda sc: sc.copy(asc[:], pb[3][0:128, 0:1]), wait=W(ev_mm))
    ev_ab = s.emit("scalar", lambda sc: sc.copy(abi[:], pb[3][0:128, 1:2]))
    ev_asr = s.emit("vector", lambda v: v.tensor_copy(ascr[:], tiny[0:1, 0:1]), wait=W(ev_t1))
    ev_abr = s.emit("vector", lambda v: v.tensor_copy(abir[:], tiny[0:1, 1:2]), wait=W(ev_t2))
    ev_bank[3] = ev_ab

    csil = P0(nc.sbuf_tensor("csil", [B, D], F32))
    csilT = P0(nc.sbuf_tensor("csilT", [128, KC * B], BF16))
    mbuf = [P0(nc.sbuf_tensor(f"mbuf{i}", [B, 512], F32)) for i in range(2)]
    mmsk = P0(nc.sbuf_tensor("mmsk", [B, 512], F32))
    ebs = [P0(nc.sbuf_tensor(f"ebs{b}", [B, 128], F32)) for b in range(B)]
    grstg = P0(nc.sbuf_tensor("grstg", [128, 2 * B * D], BF16))
    ev_sil = s.emit("scalar", lambda sc: sc.activation(csil[:], csb[:], AF.Silu),
                    wait=W(ev_misc))
    for kc in range(KC):
        ev = s.emit("tensor", lambda t, kc=kc: t.matmul(
            pb[0][0:128, kc * B:(kc + 1) * B], csil[:, kc * 128:(kc + 1) * 128],
            identf[0:B, 0:B], start=(kc == 0), stop=(kc == KC - 1), is_transpose=True),
            wait=W(ev_sil, ev_idf, ev_bank[0]) if kc == 0 else ())
    ev_csT = s.emit("scalar", lambda sc: sc.copy(csilT[:], pb[0][0:128, 0:KC * B]),
                    wait=W(ev))
    ev_bank[0] = ev_csT
    NWM = 4
    wmb = [P0(nc.sbuf_tensor(f"wmb{i}", [128, 512], F32)) for i in range(NWM)]
    wmbb = [P0(nc.sbuf_tensor(f"wmbb{i}", [128, 512], BF16)) for i in range(NWM)]
    ev_cast = [None] * NWM
    ev_mmod = [None] * NWM
    ev = s.emit("gpsimd", lambda g: g.memset(ebs[0][0:B, :], 0.0))
    ev = s.emit("gpsimd", lambda g: g.memset(ebs[0][0:1, :], 1.0), wait=W(ev))
    ev = s.emit("gpsimd", lambda g: g.memset(ebs[1][0:B, :], 1.0), wait=W(ev))
    ev_eb = s.emit("gpsimd", lambda g: g.memset(ebs[1][0:1, :], 0.0), wait=W(ev))
    ev_mb = [None, None]
    GKIND = {2: 0, 5: 2 * D}

    def mod_tile(oc, kc):
        i = (oc * KC + kc) % NWM
        evl = s.emit("sync", lambda sy, oc=oc, kc=kc, i=i: sy.dma_start(
            out=wmb[i][:], in_=dwmod.ap()[kc * 128:(kc + 1) * 128, oc * 512:(oc + 1) * 512]),
            wait=W(ev_cast[i]), sig=s_wm[i], amt=16)
        ev_cast[i] = s.emit("scalar", lambda sc, i=i: sc.activation(wmbb[i][:], wmb[i][:], AF.Copy),
                            wait=W(evl, ev_mmod[i]))
        ev_mmod[i] = s.emit("tensor", lambda t, kc=kc, i=i: t.matmul(
            pb[1][0:B, 0:512], csilT[:, kc * B:(kc + 1) * B], wmbb[i][:],
            start=(kc == 0), stop=(kc == KC - 1)),
            wait=W(ev_cast[i], ev_csT if oc == 0 and kc == 0 else None,
                   ev_bank[1] if kc == 0 else None))
        if kc != KC - 1:
            return
        mi = oc % 2
        kind = oc // 2
        ev = s.emit("scalar", lambda sc, mi=mi: sc.copy(mbuf[mi][:], pb[1][0:B, 0:512]),
                    wait=W(ev_mmod[(oc * KC + KC - 1) % NWM], ev_mb[mi]))
        ev_bank[1] = ev
        if kind in (1, 4):
            ev = s.emit("vector", lambda v, mi=mi: v.tensor_scalar(
                mbuf[mi][:], mbuf[mi][:], 1.0, None, OP.add), wait=W(ev))
        elif kind in (2, 5):
            ev2 = s.emit("vector", lambda v, mi=mi: v.tensor_scalar(
                mmsk[:], mbuf[mi][:], 0.0, None, OP.is_ge), wait=W(ev))
            ev2 = s.emit("vector", lambda v: v.tensor_scalar(
                mmsk[:], mmsk[:], 2.0, 1.0, OP.mult, OP.subtract))
            ev3 = s.emit("scalar", lambda sc, mi=mi: sc.activation(mbuf[mi][:], mbuf[mi][:], AF.Abs),
                         wait=W(ev2))
            ev3 = s.emit("vector", lambda v, mi=mi: v.tensor_scalar(
                mbuf[mi][:], mbuf[mi][:], 0.25, None, OP.max), wait=W(ev3))
            ev = s.emit("vector", lambda v, mi=mi: v.tensor_tensor(
                mbuf[mi][:], mbuf[mi][:], mmsk[:], OP.mult))
            for b in range(B):
                evr = s.emit("tensor", lambda t, b=b, mi=mi: t.matmul(
                    pb[2][0:128, 0:512], ebs[b][:], mbuf[mi][:],
                    start=True, stop=True),
                    wait=W(ev, ev_eb, ev_bank[2]))
                ev_bank[2] = s.emit("scalar", lambda sc, b=b, oc=oc, kind=kind: sc.copy(
                    grstg[:, GKIND[kind] + b * D + (oc % 2) * 512:
                          GKIND[kind] + b * D + (oc % 2) * 512 + 512],
                    pb[2][0:128, 0:512]), wait=W(evr))
        ev_mb[mi] = s.emit("sync", lambda sy, oc=oc, mi=mi: sy.dma_start(
            out=dws_mod.ap()[:, oc * 512:(oc + 1) * 512], in_=mbuf[mi][:]),
            wait=W(ev, ev_bank[2] if kind in (2, 5) else None), sig=s_msb[mi], amt=16)

    NWS = 3
    wstg = [P0(nc.sbuf_tensor(f"wstg{i}", [128, 2048], F32)) for i in range(NWS)]
    wstgb = [P0(nc.sbuf_tensor(f"wstgb{i}", [128, 2048], BF16)) for i in range(NWS)]
    ev_wst = [None] * NWS

    def attn_tile(dram, nmc, blk0, kc, wo=False):
        i = attn_tile.n % NWS
        attn_tile.n += 1
        ncol = nmc * 128
        evl = s.emit("sync", lambda sy, dram=dram, kc=kc, i=i, ncol=ncol: sy.dma_start(
            out=wstg[i][:, 0:ncol], in_=dram.ap()[kc * 128:(kc + 1) * 128, :]),
            wait=W(ev_wst[i]), sig=s_wg[i], amt=16)
        if attn_tile.n % 2:
            evc = s.emit("scalar", lambda sc, i=i, ncol=ncol: sc.activation(
                wstgb[i][:, 0:ncol], wstg[i][:, 0:ncol], AF.Copy), wait=W(evl))
        else:
            evc = s.emit("vector", lambda v, i=i, ncol=ncol: v.tensor_copy(
                wstgb[i][:, 0:ncol], wstg[i][:, 0:ncol]), wait=W(evl))
        if wo:
            ev_wst[i] = s.emit("scalar", lambda sc, kc=kc, i=i: sc.dma_start(
                out=dws_wo.ap()[kc * 128:(kc + 1) * 128, :], in_=wstgb[i][:, 0:1024]),
                wait=W(evc), sig=s_ws[i], amt=16)
            return

        def store_blk(sc, i=i, nmc=nmc, blk0=blk0, kc=kc):
            src = wstgb[i][:, 0:nmc * 128].rearrange("p (mc j) -> p mc j", j=128)
            dst = dws_attn.ap().rearrange("p (blk j) -> p blk j", j=128)
            return sc.dma_start(out=dst[:, blk0 + kc:blk0 + nmc * KC:KC, :], in_=src)
        ev_wst[i] = s.emit("scalar", store_blk, wait=W(evc), sig=s_ws[i], amt=16)
    attn_tile.n = 0

    accb = P0(nc.sbuf_tensor("accb", [128, HC + KC], F32))
    acc1 = P0(nc.sbuf_tensor("acc1", [128, 2], F32))
    scal = P0(nc.sbuf_tensor("scal", [1, 4], F32))
    NWF = 4
    wstgF = [P0(nc.sbuf_tensor(f"wstgF{i}", [128, 4096], F32)) for i in range(NWF)]
    NTTL = 4
    ttlb = [P0(nc.sbuf_tensor(f"ttlb{i}", [128, 4096], FP8)) for i in range(NTTL)]
    wsb_ = [P0(nc.sbuf_tensor(f"wsb{i}", [128, 1], F32)) for i in range(2)]
    ev_wstF = [None] * NWF
    ev_ttlb = [None] * NTTL
    FFN_W = [(dw1, HC, w1sv, w1sr, 0), (dw2, KC, w2sv, w2sr, HC)]

    def ffn_scale_tile(wi, ti):
        dram, ntile, sv, sr, acc0 = FFN_W[wi]
        cols = int(dram.shape[1])
        i = ffn_scale_tile.n % NWF
        ffn_scale_tile.n += 1
        evl = s.emit("gpsimd", lambda sy, dram=dram, ti=ti, i=i, cols=cols: sy.dma_start(
            out=wstgF[i][:, 0:cols], in_=dram.ap()[ti * 128:(ti + 1) * 128, :]),
            wait=W(ev_wstF[i]), sig=s_wf[i], amt=16)
        ev_wstF[i] = s.emit("vector", lambda v, ti=ti, i=i, cols=cols, acc0=acc0:
                            v.scalar_tensor_tensor(
                                wstgF[i][:, 0:cols], wstgF[i][:, 0:cols], -1.0,
                                wstgF[i][:, 0:cols], OP.mult, OP.max,
                                accum_out=accb[:, acc0 + ti:acc0 + ti + 1]),
                            wait=W(evl))
        return ev_wstF[i]

    ffn_scale_tile.n = 0

    def ffn_finalize(wi, ev_acc):
        dram, ntile, sv, sr, acc0 = FFN_W[wi]
        cols = int(dram.shape[1])
        ev = s.emit("vector", lambda v, ntile=ntile, acc0=acc0: v.tensor_reduce(
            acc1[:, wi:wi + 1], accb[:, acc0:acc0 + ntile], AX.X, OP.add), wait=W(ev_acc))
        ev = s.emit("tensor", lambda t: t.matmul(pb[3][0:1, 0:1], acc1[:, wi:wi + 1],
                                                 onesf[:, 0:1], start=True, stop=True),
                    wait=W(ev, ev_bank[3]))
        nelem = float(128 * ntile * cols)
        ev = s.emit("scalar", lambda sc, nelem=nelem: sc.activation(
            scal[0:1, 0:1], pb[3][0:1, 0:1], AF.Copy, scale=1.0 / nelem), wait=W(ev))
        ev_bank[3] = ev
        ev = s.emit("vector", lambda v: v.tensor_scalar(scal[0:1, 0:1], scal[0:1, 0:1],
                                                        1e-5, None, OP.max), wait=W(ev))
        ev = s.emit("vector", lambda v: v.reciprocal(scal[0:1, 1:2], scal[0:1, 0:1]))
        s.emit("vector", lambda v, sr=sr: v.tensor_copy(sr[:], scal[0:1, 0:1]))
        ev = s.emit("tensor", lambda t: t.matmul(pb[3][0:128, 0:2], onesf[0:1, :], scal[0:1, 0:2],
                                                 start=True, stop=True),
                    wait=W(ev, ev_bank[3]))
        s.emit("scalar", lambda sc, sv=sv: sc.copy(sv[:], pb[3][0:128, 0:1]), wait=W(ev))
        ev_ws = s.emit("scalar", lambda sc, wi=wi: sc.copy(wsb_[wi][:], pb[3][0:128, 1:2]))
        ev_bank[3] = ev_ws
        return ev_ws

    def ffn_quant_tile(wi, ti, ev_ws):
        dram, ntile, sv, sr, acc0 = FFN_W[wi]
        cols = int(dram.shape[1])
        i = ffn_quant_tile.n % NWF
        ffn_quant_tile.n += 1
        evl = s.emit("gpsimd", lambda sy, dram=dram, ti=ti, i=i, cols=cols: sy.dma_start(
            out=wstgF[i][:, 0:cols], in_=dram.ap()[ti * 128:(ti + 1) * 128, :]),
            wait=W(ev_wstF[i], ffn_quant_tile.tp[i], ev_ws), sig=s_wf[i], amt=16)
        ev_wstF[i] = None
        ev = s.emit("vector", lambda v, i=i, cols=cols, wi=wi: v.tensor_scalar(
            wstgF[i][:, 0:cols], wstgF[i][:, 0:cols], wsb_[wi][:], None, OP.mult),
            wait=W(evl))
        ev = s.emit("vector", lambda v, i=i, cols=cols: v.tensor_scalar(
            wstgF[i][:, 0:cols], wstgF[i][:, 0:cols], RMAGIC, RMAGIC,
            OP.add, OP.subtract))
        ev_q = s.emit("vector", lambda v, i=i, cols=cols: v.tensor_scalar(
            wstgF[i][:, 0:cols], wstgF[i][:, 0:cols], 1.0, -1.0, OP.min, OP.max))
        i2 = ti % NTTL
        evc = None
        for jb in range(cols // 512):
            bk = 4 + ((ti + jb) % 4)
            ev = None
            for j4 in range(4):
                j = jb * 4 + j4
                ev = s.emit("tensor", lambda t, i=i, j=j, j4=j4, bk=bk: t.matmul(
                    pb[bk][0:128, j4 * 128:(j4 + 1) * 128],
                    wstgF[i][:, j * 128:(j + 1) * 128], identf[:],
                    start=(j4 == 0), stop=(j4 == 3), is_transpose=True),
                    wait=W(ev_q, ev_idf, ev_bank[bk],
                           ev_ttlb[i2] if j == 0 else None) if j4 == 0 else ())
            ffn_quant_tile.tp[i] = ev
            evc = s.emit("scalar", lambda sc, i2=i2, jb=jb, bk=bk: sc.copy(
                ttlb[i2][:, jb * 512:(jb + 1) * 512], pb[bk][0:128, 0:512]),
                wait=W(ev))
            ev_bank[bk] = evc
        if wi == 0:
            def stw(sy, ti=ti, i2=i2, cols=cols):
                return sy.dma_start(
                    out=dws_w1t.ap()[:, ti * KC * 128:(ti + 1) * KC * 128],
                    in_=ttlb[i2][:, 0:cols])
        else:
            def stw(sy, ti=ti, i2=i2, cols=cols):
                dst = dws_w2t.ap().rearrange("(j p) o -> p j o", p=128)[
                    :, :, ti * 128:(ti + 1) * 128]
                srcv = ttlb[i2][:, 0:cols].rearrange("p (j c) -> p j c", c=128)
                return sy.dma_start(out=dst, in_=srcv,
                                    allow_slow_non_contiguous=True)
        ev_ttlb[i2] = s.emit("sync", stw, wait=W(evc), sig=s_tt[i2], amt=16)
    ffn_quant_tile.n = 0
    ffn_quant_tile.tp = [None] * NWF

    attn_seq = []
    for dram, nmc, blk0 in [(dwq, 8, 0), (dwk, 8, 64), (dwv, 8, 128),
                            (dwg, 8, 192), (dwa, 16, 256)]:
        for kc in range(KC):
            attn_seq.append((dram, nmc, blk0, kc, False))
    for kc in range(KC):
        attn_seq.append((dwo, 8, 0, kc, True))
    mod_seq = [(oc, kc) for oc in range(12) for kc in range(KC)]
    scale_seq = [(0, ti) for ti in range(HC)] + [(1, ti) for ti in range(KC)]
    ev_acc = {0: None, 1: None}
    ws1 = None
    qpend = []
    nrounds = max(len(attn_seq), len(mod_seq), len(scale_seq))
    for r in range(nrounds):
        if r < len(mod_seq):
            mod_tile(*mod_seq[r])
        if r < len(attn_seq):
            attn_tile(*attn_seq[r])
        if r < len(scale_seq):
            wi, ti = scale_seq[r]
            ev_acc[wi] = ffn_scale_tile(wi, ti)
            if wi == 0 and ti == HC - 1:
                ws1 = ffn_finalize(0, ev_acc[0])
        elif ws1 is not None and qpend is not None and len(qpend) < HC:
            ffn_quant_tile(0, len(qpend), ws1)
            qpend.append(None)
    ev_garep = ev_bank[2]
    ev_grs = s.emit("sync", lambda sy: sy.dma_start(out=dws_grep.ap(), in_=grstg[:]),
                    wait=W(ev_garep), sig=s_gr, amt=16)
    ev_mv = {}
    for kind, dstv in [(0, m_sa), (1, m_ca1), (2, m_ga), (3, m_sb), (4, m_cb1), (5, m_gb)]:
        for b in range(B):
            srcap = dws_mod.ap()[b].rearrange("(kind kc p) -> p kc kind", p=128, kind=6)[:, :, kind]
            ev_mv[kind] = s.emit("sync", lambda sy, dstv=dstv, srcap=srcap, b=b: sy.dma_start(
                out=dstv[:, b * KC:(b + 1) * KC], in_=srcap, allow_slow_non_contiguous=True),
                wait=W(ev_mb[0], ev_mb[1]), sig=s_mv, amt=16)
    if ws1 is None:
        ws1 = ffn_finalize(0, ev_acc[0])
    ws2 = ffn_finalize(1, ev_acc[1])
    for ti in range(len(qpend), HC):
        ffn_quant_tile(0, ti, ws1)
    for ti in range(KC):
        ffn_quant_tile(1, ti, ws2)

    p0.close()
    for eng in ENGINES:
        s.emit(eng, lambda e: None, sig=s_bar)
    bar1 = (s_bar, s.cnt["s_bar"])
    ws_done = [(s_ws[i], s.cnt[f"s_ws{i}"]) for i in range(3)]
    tt_done = [(s_tt[i], s.cnt[f"s_tt{i}"]) for i in range(4)]
    sc_done_l = ws_done + tt_done
    sc_done_l += [(s_msb[0], s.cnt["s_msb0"]), (s_msb[1], s.cnt["s_msb1"])]
    p0_done = (sc_done_l + [(s_wg[i], s.cnt[f"s_wg{i}"]) for i in range(3)]
               + [(s_wf[i], s.cnt[f"s_wf{i}"]) for i in range(4)]
               + [(s_wm[i], s.cnt[f"s_wm{i}"]) for i in range(4)]
               + [(s_gr, s.cnt["s_gr"])])
    modv_done = (s_mv, s.cnt["s_mv"])

    ring_state = {"g": 0, "loaded": [], "release": {}, "slot": {},
                  "meta": {id(ring): {"n": 0, "nslot": NSLOT, "g_of": [],
                                      "sems": s_wrs},
                           id(ringf): {"n": 0, "nslot": NSLOTF, "g_of": [],
                                       "sems": s_wfr}}}

    def ring_release(g, ev):
        ring_state["release"][g] = ev

    def ring_load(buf, reqs, wait_extra=()):
        g = ring_state["g"]
        st = ring_state["meta"][id(buf)]
        li = st["n"]
        st["n"] += 1
        st["g_of"].append(g)
        slot = li % st["nslot"]
        ring_state["slot"][g] = slot
        waits = list(wait_extra)
        if li >= st["nslot"]:
            gprev = st["g_of"][li - st["nslot"]]
            waits.append(lambda gprev=gprev: ring_state["release"][gprev])
        ev = None
        for (src, c0, ncols) in reqs:
            ev = s.emit("gpsimd", lambda gp, buf=buf, slot=slot, c0=c0, ncols=ncols, src=src:
                        gp.dma_start(out=buf[:, slot * 1024 + c0:slot * 1024 + c0 + ncols],
                                     in_=src),
                        wait=waits, sig=st["sems"][slot], amt=16)
            waits = []
        ring_state["loaded"].append(ev)
        ring_state["g"] += 1
        return g

    def ring_wait(g):
        return ring_state["loaded"][g]

    def ring_ap(buf, g, c0, ncols):
        slot = ring_state["slot"][g]
        return buf[:, slot * 1024 + c0:slot * 1024 + c0 + ncols]

    attn_pg0 = {}
    attn_wg0 = {}

    PROJ_BLK = {"k": 64, "v": 128, "g": 192, "ar": 256, "ai": 256, "q": 0}

    def proj_desc():
        desc = []
        for pas in (0, 1):
            for kind in ("k", "v", "g", "ar", "ai", "q"):
                for b in (0, 1):
                    hb = pas if b == 0 else 1 - pas
                    for mcx in range(4):
                        mc = (8 if kind == "ai" else 0) + hb * 4 + mcx
                        desc.append((pas, b, kind, mc, hb))
        return desc

    def queue_proj_groups(ck, extra=()):
        attn_pg0[ck] = ring_state["g"]
        for (pas, b, kind, mc, hb) in proj_desc():
            blk0 = PROJ_BLK[kind]
            srcw = dws_attn.ap()[:, (blk0 + mc * KC) * 128:(blk0 + mc * KC + 8) * 128]
            ring_load(ring, [(srcw, 0, 1024)], wait_extra=extra)
            extra = ()

    def queue_wo_groups(ck):
        attn_wg0[ck] = ring_state["g"]
        for oc in range(2):
            for kcp in range(4):
                reqs = []
                for jj in range(2):
                    kc = kcp * 2 + jj
                    reqs.append((dws_wo.ap()[kc * 128:(kc + 1) * 128, oc * 512:(oc + 1) * 512],
                                 jj * 512, 512))
                ring_load(ring, reqs)

    if build_stage >= 1:
        queue_proj_groups(0, extra=[bar1] + sc_done_l)
        for ck in range(1, NCH):
            queue_proj_groups(ck)
            queue_wo_groups(ck - 1)
        queue_wo_groups(NCH - 1)

    p1 = contextlib.ExitStack()
    P1 = p1.enter_context
    x_fm = P1(nc.sbuf_tensor("x_fm", [128, KC * C], F32))
    xh = P1(nc.sbuf_tensor("xh", [128, KC * C], F32))
    hsq = P1(nc.sbuf_tensor("hsq", [128, KC * C], BF16))
    xn = [P1(nc.sbuf_tensor(f"xn{i}", [128, KC * C], BF16)) for i in range(2)]
    yb = [P1(nc.sbuf_tensor(f"yb{i}", [128, KC * C], BF16)) for i in range(2)]
    w7 = P1(nc.sbuf_tensor("w7", [128, KC * C], F32))
    w8 = P1(nc.sbuf_tensor("w8", [128, KC * C], F32))
    w9 = P1(nc.sbuf_tensor("w9", [128, KC * C], F32))
    w10 = P1(nc.sbuf_tensor("w10", [128, KC * C], F32))
    w11 = P1(nc.sbuf_tensor("w11", [128, KC * C], F32))
    psi = P1(nc.sbuf_tensor("psi", [128, KC * C], F32))
    qb = [P1(nc.sbuf_tensor(f"qb{i}", [128, KC * C], BF16)) for i in range(2)]
    kbn = [P1(nc.sbuf_tensor(f"kbn{i}", [128, KC * C], BF16)) for i in range(2)]
    vb = [P1(nc.sbuf_tensor(f"vb{i}", [128, KC * C], BF16)) for i in range(2)]
    sgb = [P1(nc.sbuf_tensor(f"sgb{i}", [128, KC * C], BF16)) for i in range(2)]
    arb = [P1(nc.sbuf_tensor(f"arb{i}", [128, KC * C], F32)) for i in range(2)]
    aib = [P1(nc.sbuf_tensor(f"aib{i}", [128, KC * C], F32)) for i in range(2)]
    mrow = P1(nc.sbuf_tensor("mrow", [1, 2 * C], F32))
    srow = P1(nc.sbuf_tensor("srow", [1, C], F32))
    mrep = P1(nc.sbuf_tensor("mrep", [128, 2 * C], F32))
    grep = P1(nc.sbuf_tensor("grep", [128, C], F32))
    mvst = P1(nc.sbuf_tensor("mvst", [128, 4], F32))
    varr = P1(nc.sbuf_tensor("varr", [128, 2], F32))
    msq2 = P1(nc.sbuf_tensor("msq2", [128, 2], F32))
    sagam = P1(nc.sbuf_tensor("sagam", [128, B * KC], F32))
    ga_rep = P1(nc.sbuf_tensor("ga_rep", [128, B * D], BF16))
    st_psi = [P1(nc.sbuf_tensor(f"stpsi{b}", [128, KC], F32)) for b in range(B)]
    st_zr = [P1(nc.sbuf_tensor(f"stzr{b}", [128, KC], F32)) for b in range(B)]
    st_zi = [P1(nc.sbuf_tensor(f"stzi{b}", [128, KC], F32)) for b in range(B)]

    ev1 = {}

    def xf_slice(buf, kc, t0=0, nt=C):
        return buf[:, kc * C + t0:kc * C + t0 + nt]

    def lastcol(buf):
        return buf[:].rearrange("p (kc c) -> p kc c", c=C)[:, :, C - 1]

    def emit_A1_tp(ss):
        b, ck = ss % 2, ss // 2
        slot = ss % 4
        wv = [] if ss < 4 else [ev1.get(("store", ss - 4))]
        src = dx.ap()[b].rearrange("(ck tt p) d -> ck p tt d", p=128, tt=TT)[ck]
        ev1[("xl", ss)] = s.emit("sync", lambda sy, slot=slot, src=src: sy.dma_start(
            out=stage[slot][:], in_=src), wait=W(*wv), sig=s_xls[slot], amt=16)
        last = None
        for tt in range(TT):
            for kcg in range(2):
                bk = kcg
                ev = None
                for i2 in range(4):
                    kc = kcg * 4 + i2
                    ev = s.emit("tensor", lambda t, slot=slot, tt=tt, kc=kc, i2=i2, bk=bk: t.matmul(
                        pb[bk][0:128, i2 * 128:(i2 + 1) * 128],
                        stage[slot][:, tt * 1024 + kc * 128:tt * 1024 + (kc + 1) * 128],
                        identf[:], start=(i2 == 0), stop=(i2 == 3), is_transpose=True),
                        wait=W(ev1[("xl", ss)], ev_bank[bk],
                               ev1.get(("xh", ss - 1))) if i2 == 0 else ())

                def evacx(sc, tt=tt, kcg=kcg, bk=bk):
                    dst = x_fm[:].rearrange("p (kc c) -> p kc c", c=C)[
                        :, kcg * 4:(kcg + 1) * 4, tt * 128:(tt + 1) * 128]
                    return sc.copy(dst, pb[bk][0:128, 0:512])
                last = ev_bank[bk] = s.emit(
                    "scalar", evacx,
                    wait=W(ev) + (p0_done if ss < 2 else []))
        ev1[("xfm", ss)] = last

    def emit_A1_stats(ss):
        b, ck = ss % 2, ss // 2
        slot = ss % 4
        ev = None
        for tt in range(TT):
            s.emit("vector", lambda v, slot=slot, tt=tt: v.tensor_scalar(
                hsq[:].bitcast(F32)[:, 0:1024], stage[slot][:, tt * 1024:(tt + 1) * 1024],
                1.0 / D, None, OP.mult, OP.add, accum_out=mvst[:, tt:tt + 1]),
                wait=(W(ev1[("xl", ss)], ev1.get(("mrow", ss - 1)), ev1.get(("hsq", ss - 1)),
                        modv_done if ss == 0 else None)
                      + (p0_done if ss < 2 else [])) if tt == 0 else ())
            ev = s.emit("vector", lambda v, slot=slot, tt=tt: v.scalar_tensor_tensor(
                hsq[:].bitcast(F32)[:, 0:1024], stage[slot][:, tt * 1024:(tt + 1) * 1024], 1.0,
                stage[slot][:, tt * 1024:(tt + 1) * 1024], OP.mult, OP.mult,
                accum_out=mvst[:, 2 + tt:3 + tt]))
        ev = s.emit("vector", lambda v: v.tensor_scalar(
            varr[:], mvst[:, 2:4], 1.0 / D, None, OP.mult), wait=W(ev))
        ev = s.emit("vector", lambda v: v.tensor_tensor(
            msq2[:], mvst[:, 0:2], mvst[:, 0:2], OP.mult))
        ev = s.emit("vector", lambda v: v.tensor_tensor(varr[:], varr[:], msq2[:], OP.subtract))
        ev = s.emit("scalar", lambda sc: sc.activation(varr[:], varr[:], AF.Ln,
                                                       bias=cvec[:, CEPS5:CEPS5 + 1]),
                    wait=W(ev))
        ev_rstd = s.emit("scalar", lambda sc: sc.activation(varr[:], varr[:], AF.Exp, scale=-0.5))
        ev = None
        for i2, (buf_, col) in enumerate([(mvst, 0), (mvst, 1), (varr, 0), (varr, 1)]):
            ev = s.emit("tensor", lambda t, buf_=buf_, col=col, i2=i2: t.matmul(
                pb[3][0:1, i2 * 128:(i2 + 1) * 128], buf_[:, col:col + 1], identf[:],
                start=(i2 == 0), stop=(i2 == 3), is_transpose=True),
                wait=W(ev_rstd, ev_bank[3]) if i2 == 0 else ())
        ev_mrow = s.emit("scalar", lambda sc: sc.copy(mrow[:], pb[3][0:1, 0:512]),
                         wait=W(ev))
        ev_bank[3] = ev_mrow
        ev1[("mrow", ss)] = ev_mrow
        ev = s.emit("tensor", lambda t: t.matmul(pb[2][0:128, 0:C], onesf[0:1, :],
                                                 mrow[0:1, 0:C], start=True, stop=False),
                    wait=W(ev_mrow, ev_bank[2], ev1.get(("xh", ss - 1)),
                           ev1.get(("hsq", ss - 1))))
        ev = s.emit("tensor", lambda t: t.matmul(pb[2][0:128, C:2 * C], onesf[0:1, :],
                                                 mrow[0:1, C:2 * C], start=False, stop=True))
        ev_mrep = s.emit("scalar", lambda sc: sc.copy(mrep[:], pb[2][0:128, 0:2 * C]),
                         wait=W(ev))
        ev_bank[2] = ev_mrep
        ev = None
        for kc in range(KC):
            s.emit("vector", lambda v, kc=kc: v.tensor_tensor(
                xf_slice(xh, kc), xf_slice(x_fm, kc), mrep[:, 0:C], OP.subtract),
                wait=W(ev1[("xfm", ss)], ev_mrep) if kc == 0 else ())
            ev = s.emit("vector", lambda v, kc=kc, b=b: v.scalar_tensor_tensor(
                xf_slice(xh, kc), xf_slice(xh, kc), m_ca1[:, b * KC + kc:b * KC + kc + 1],
                mrep[:, C:2 * C], OP.mult, OP.mult))
            ev1[("xhk", ss, kc)] = ev
        ev1[("xh", ss)] = ev

    def emit_S21(ss):
        b = ss % 2
        ev_h = None
        for kc in range(KC):
            ev_h = s.emit("scalar", lambda sc, kc=kc, b=b: sc.activation(
                xf_slice(hsq, kc), xf_slice(xh, kc), AF.Square,
                bias=m_sa[:, b * KC + kc:b * KC + kc + 1]),
                wait=W(ev1[("xhk", ss, kc)]))
        ev1[("hsq", ss)] = ev_h
        ev = None
        for kc in range(KC):
            ev = s.emit("tensor", lambda t, kc=kc: t.matmul(
                pb[3][0:1, 0:C], onesb[:], xf_slice(hsq, kc),
                start=(kc == 0), stop=(kc == KC - 1)),
                wait=W(ev_h, ev_bank[3]) if kc == 0 else ())
        ev = s.emit("scalar", lambda sc: sc.copy(srow[:], pb[3][0:1, 0:C]), wait=W(ev))
        ev = s.emit("scalar", lambda sc: sc.activation(srow[:], srow[:], AF.Ln,
                                                       bias=cvec[0:1, CEPS5:CEPS5 + 1]))
        ev = s.emit("scalar", lambda sc: sc.activation(srow[:], srow[:], AF.Exp, scale=-0.5))
        evm = s.emit("tensor", lambda t: t.matmul(pb[3][0:128, 0:C], onesf[0:1, :],
                                                  srow[0:1, :], start=True, stop=True),
                     wait=W(ev, ev1.get(("xn", ss - 1))))
        ev = s.emit("scalar", lambda sc: sc.copy(grep[:], pb[3][0:128, 0:C]),
                    wait=W(evm))
        ev_bank[3] = ev
        ev1[("grep", ss)] = ev

    def emit_XN(ss):
        b = ss % 2
        ck2 = ss // 2
        ids = []
        for kc in range(KC):
            ids.append(s.emit("scalar", lambda sc, kc=kc, b=b: sc.activation(
                xf_slice(xh, kc), xf_slice(xh, kc), AF.Identity,
                scale=gam[:, kc:kc + 1], bias=sagam[:, b * KC + kc:b * KC + kc + 1])))
        ev = None
        for kc in range(KC):
            ev = s.emit("vector", lambda v, kc=kc, b=b: v.tensor_tensor(
                xf_slice(xn[b], kc), xf_slice(xh, kc), grep[:], OP.mult),
                wait=W(ev1[("grep", ss)], ids[kc],
                       ev1.get(("PpairB", ck2 - 1, b)) if kc == 0 else None))
        ev1[("xn", ss)] = ev

    def chain_stream(ts, hs, wprev_key):
        b, ck = ts % 2, ts // 2
        p = ts % 2
        HW = 4 * C

        def sl(buf):
            return buf[:, hs * HW:(hs + 1) * HW]
        wprev = ev1.get(wprev_key) if wprev_key else None
        evP_ = ev1[("evP", ts)]
        h = {}
        steps = []

        def op(eng, fn, wk=(), name=None, free_tag=None):
            def th(eng=eng, fn=fn, wk=wk, name=name, free_tag=free_tag):
                ev = s.emit(eng, fn, wait=W(*[h[k] if isinstance(k, str) else k for k in wk]))
                if name:
                    h[name] = ev
                if free_tag:
                    ev1[(free_tag, ts, hs)] = ev
            steps.append(th)

        op("vector", lambda v: v.tensor_tensor(
            sl(qb[p]), sl(qb[p]), sl(sgb[p]), OP.mult),
            wk=(evP_, wprev), name="qg", free_tag="fsgb")
        op("vector", lambda v: v.tensor_tensor(
            sl(w11), sl(kbn[p]), sl(vb[p]), OP.mult),
            wk=(evP_, wprev), name="kv", free_tag="fkv")
        op("scalar", lambda sc: sc.activation(sl(w8), sl(arb[p]), AF.Square),
           wk=(evP_, wprev), name="e1")
        op("scalar", lambda sc: sc.activation(sl(w9), sl(aib[p]), AF.Square), name="e2")
        op("vector", lambda v: v.tensor_tensor(sl(w8), sl(w8), sl(w9), OP.add),
           wk=("e2",), name="e3")
        op("scalar", lambda sc: sc.activation(sl(w9), sl(w8), AF.Ln,
                                              bias=cvec[:, C18:C18 + 1]),
           wk=("e3",), name="e4")
        op("scalar", lambda sc: sc.activation(sl(w9), sl(w9), AF.Exp, scale=0.5),
           name="e5")
        op("vector", lambda v: v.tensor_tensor(sl(w10), sl(w9), sl(arb[p]), OP.add),
           wk=("e5",), name="e6a", free_tag="farb")
        op("scalar", lambda sc: sc.activation(sl(w10), sl(w10), AF.Relu),
           wk=("e6a",), name="e6b")
        op("vector", lambda v: v.tensor_tensor(sl(w8), sl(w9), sl(w10), OP.mult),
           wk=("e6b",), name="e7")
        op("scalar", lambda sc: sc.activation(sl(w8), sl(w8), AF.Ln, scale=2.0,
                                              bias=cvec[:, C18:C18 + 1]),
           wk=("e7",), name="e8")
        op("scalar", lambda sc: sc.activation(sl(w8), sl(w8), AF.Exp, scale=0.5),
           name="e9")
        op("vector", lambda v: v.tensor_tensor(sl(w8), sl(w8), sl(w10), OP.add),
           wk=("e9",), name="e10")
        op("scalar", lambda sc: sc.activation(sl(w8), sl(w8), AF.Ln,
                                              bias=cvec[:, C19:C19 + 1]),
           wk=("e10",), name="e11")
        op("scalar", lambda sc: sc.activation(sl(w8), sl(w8), AF.Exp, scale=-1.0),
           name="e12")
        op("vector", lambda v: v.tensor_tensor(sl(w10), sl(aib[p]), sl(w8), OP.mult),
           wk=("e12",), name="e13a", free_tag="faib")
        op("vector", lambda v: v.tensor_scalar(sl(w10), sl(w10), 1.0, -1.0,
                                               OP.min, OP.max), name="e13b")
        op("scalar", lambda sc: sc.activation(sl(w7), sl(w9), AF.Sigmoid),
           wk=(wprev,), name="e14")
        op("scalar", lambda sc: sc.activation(sl(w10), sl(w10), AF.Arctan),
           wk=("e13b",), name="e15")

        def scanstep(out, in0, in1, state, wname):
            def th(out=out, in0=in0, in1=in1, state=state, wname=wname):
                for kx in range(4):
                    kc = hs * 4 + kx
                    init = 0.0 if ck == 0 or FORCE_ZERO_STATE else state[b][:, kc:kc + 1]
                    s.emit("vector", lambda v, kc=kc, init=init, out=out, in0=in0,
                           in1=in1: v.tensor_tensor_scan(
                               xf_slice(out, kc), xf_slice(in0, kc),
                               xf_slice(in1, kc) if in1 is not zeroC else zeroC[:],
                               init, OP.mult if in1 is not zeroC else OP.add, OP.add),
                           wait=W(h[wname]) if kx == 0 else ())
            steps.append(th)

        def statecopy(state, srcbuf):
            def th(state=state, srcbuf=srcbuf):
                lc = srcbuf[:].rearrange("p (kc c) -> p kc c", c=C)[
                    :, hs * 4:(hs + 1) * 4, C - 1]
                s.emit("vector", lambda v, lc=lc: v.tensor_copy(
                    state[b][:, hs * 4:(hs + 1) * 4], lc))
            steps.append(th)

        scanstep(psi, w10, zeroC, st_psi, "e15")
        statecopy(st_psi, psi)
        op("vector", lambda v: v.tensor_scalar(
            sl(w9), sl(psi), 2.0 / PI, RMAGIC, OP.mult, OP.add), name="w8i")
        op("vector", lambda v: v.tensor_scalar(
            sl(w9), sl(w9), RMAGIC, -PI / 2.0, OP.subtract, OP.mult))
        op("vector", lambda v: v.tensor_tensor(
            sl(w8), sl(psi), sl(w9), OP.add), name="e20")
        op("scalar", lambda sc: sc.activation(sl(w9), sl(w8), AF.Sin, scale=2.0),
           wk=("e20",), name="e21")
        op("scalar", lambda sc: sc.activation(sl(w9), sl(w9), AF.Square), name="e23")
        op("scalar", lambda sc: sc.activation(sl(w10), sl(w9), AF.Identity,
                                              scale=-2.0, bias=cvec[:, CONE:CONE + 1]),
           name="e27")
        op("scalar", lambda sc: sc.activation(sl(w9), sl(w8), AF.Sin, scale=4.0),
           name="e28")
        op("vector", lambda v: v.tensor_tensor(sl(w8), sl(w10), sl(w11), OP.mult),
           wk=("e27", "e28"), name="e29")
        op("vector", lambda v: v.tensor_tensor(sl(w11), sl(w9), sl(w11), OP.mult),
           wk=("e28",), name="e30")
        scanstep(psi, w7, w8, st_zr, "e14")
        statecopy(st_zr, psi)
        scanstep(w8, w7, w11, st_zi, "e14")
        statecopy(st_zi, w8)
        op("vector", lambda v: v.tensor_tensor(sl(w10), sl(w10), sl(psi), OP.mult), name="m1")
        op("vector", lambda v: v.tensor_tensor(sl(w9), sl(w9), sl(w8), OP.mult), name="m2")
        op("vector", lambda v: v.tensor_tensor(sl(w10), sl(w10), sl(w9), OP.add), name="m3")

        def m4():
            ev = s.emit("vector", lambda v: v.tensor_tensor(
                sl(yb[p]), sl(w10), sl(qb[p]), OP.mult))
            ev1[("y", ts, hs)] = ev
        steps.append(m4)
        return steps

    def p1b_thunks(ck):
        g0 = attn_pg0[ck]
        bankrot = [4, 5, 6, 7]
        state = {"bi": 0, "last_evac": {}, "last_mm": {}}
        thunks = []
        desc = proj_desc()
        for pi in range(0, len(desc), 2):
            pair = [(gi, desc[gi]) for gi in (pi, pi + 1)]
            def th(pair=pair):
                bk = bankrot[state["bi"] % 4]
                state["bi"] += 1
                ev = None
                for half_i, (gi, (pas, b, kind, mc, hb)) in enumerate(pair):
                    g = g0 + gi
                    for kc in range(KC):
                        first = half_i == 0 and kc == 0
                        ev = s.emit("tensor", lambda t, g=g, kc=kc, bk=bk, b=b,
                                    half_i=half_i, first=first: t.matmul(
                                        pb[bk][0:128, half_i * C:(half_i + 1) * C],
                                        ring_ap(ring, g, kc * 128, 128),
                                        xn[b][:, kc * C:(kc + 1) * C],
                                        start=first,
                                        stop=(half_i == 1 and kc == KC - 1)),
                                    wait=W(ring_wait(g), ev1[("xn", 2 * ck + b)],
                                           ev_bank[bk] if first else None) if kc == 0 else ())
                    ring_release(g, ev)
                    state["last_mm"][b] = ev
                gi, (pas, b, kind, mc, hb) = pair[0]
                ts = 2 * ck + b
                tsp = ts - 2
                silu = False
                if kind == "k":
                    dst, mcc, fw = kbn[b], mc, ("fkv", tsp, hb)
                elif kind == "v":
                    dst, mcc, fw = vb[b], mc, ("fkv", tsp, hb)
                elif kind == "g":
                    dst, mcc, fw = sgb[b], mc, ("fsgb", tsp, hb)
                    silu = True
                elif kind == "ar":
                    dst, mcc, fw = arb[b], mc, ("farb", tsp, hb)
                elif kind == "ai":
                    dst, mcc, fw = aib[b], mc - 8, ("faib", tsp, hb)
                else:
                    dst, mcc, fw = qb[b], mc, ("y", tsp, hb)
                wfree = ev1.get(fw)
                assert wfree is not None or tsp < 0, fw
                if silu:
                    evv = s.emit("scalar", lambda sc, dst=dst, mcc=mcc, bk=bk: sc.activation(
                        xf_slice(dst, mcc, nt=2 * C), pb[bk][0:128, 0:2 * C], AF.Silu),
                        wait=W(ev, wfree))
                else:
                    evv = s.emit("scalar", lambda sc, dst=dst, mcc=mcc, bk=bk: sc.copy(
                        xf_slice(dst, mcc, nt=2 * C), pb[bk][0:128, 0:2 * C]),
                        wait=W(ev, wfree))
                ev_bank[bk] = evv
                state["last_evac"][b] = evv
            thunks.append(th)

        def fin():
            for b in range(2):
                ev1[("PpairB", ck, b)] = state["last_mm"][b]
                ev1[("evP", 2 * ck + b)] = state["last_evac"][b]
        return thunks, fin

    def stream_zip(sa, sb):
        out = []
        for i in range(max(len(sa), len(sb))):
            if i < len(sa):
                out.append(sa[i])
            if i < len(sb):
                out.append(sb[i])
        return out

    def run_interleaved(steps, thunks):
        n = len(steps)
        bounds = ([4] * 12 + [16] * 4 + [32] * 4 + [n] * 4) if n else [0] * 24
        feeds = {}
        pos = 0
        for j, th in enumerate(thunks):
            want = 4 + (j * max(n - 6, 1)) // max(len(thunks) - 1, 1)
            pos = min(max(pos, bounds[j] if j < len(bounds) else n, want), n)
            feeds.setdefault(pos, []).append(th)
        for i, stp in enumerate(steps):
            stp()
            for th in feeds.get(i, []):
                th()
        for th in feeds.get(n, []):
            th()

    def emit_W1pair(ck):
        g0 = attn_wg0[ck]
        evr = {}
        for oc in range(2):
            evmm = {}
            for kcp in range(4):
                g = g0 + oc * 4 + kcp
                for jj in range(2):
                    kc = kcp * 2 + jj
                    for b in range(2):
                        for t_ in range(TT):
                            bk = 4 + b * 2 + t_
                            wv = []
                            if jj == 0 and b == 0 and t_ == 0:
                                wv.append(ring_wait(g))
                            if kc == 0:
                                wv += [ev1[("y", 2 * ck + b, 0)], ev_bank[bk]]
                            if kc == 4:
                                wv.append(ev1[("y", 2 * ck + b, 1)])
                            evmm[(b, t_)] = s.emit(
                                "tensor", lambda t, g=g, kc=kc, jj=jj, t_=t_, bk=bk, b=b: t.matmul(
                                    pb[bk][0:128, 0:512],
                                    yb[b][:, kc * C + t_ * 128:kc * C + (t_ + 1) * 128],
                                    ring_ap(ring, g, jj * 512, 512),
                                    start=(kc == 0), stop=(kc == KC - 1)),
                                wait=W(*wv))
                            if jj == 1 and b == 1 and t_ == TT - 1:
                                ring_release(g, evmm[(b, t_)])
            for b in range(2):
                slot = (2 * ck + b) % 4
                for t_ in range(TT):
                    bk = 4 + b * 2 + t_
                    s.emit("vector", lambda v, bk=bk, b=b, oc=oc: v.tensor_tensor(
                        psi[:, 0:512], pb[bk][0:128, 0:512],
                        ga_rep[:, b * D + oc * 512:b * D + (oc + 1) * 512], OP.mult),
                        wait=W(evmm[(b, t_)], ga_ready))
                    evr[(b, t_)] = s.emit("vector", lambda v, slot=slot, t_=t_, oc=oc: v.tensor_tensor(
                        stage[slot][:, t_ * 1024 + oc * 512:t_ * 1024 + (oc + 1) * 512],
                        stage[slot][:, t_ * 1024 + oc * 512:t_ * 1024 + (oc + 1) * 512],
                        psi[:, 0:512], OP.add))
                    ev_bank[bk] = evr[(b, t_)]
        for b in range(2):
            ts = 2 * ck + b
            slot = ts % 4
            dstm = dxmid.ap()[b].rearrange("(ck tt p) d -> ck p tt d", p=128, tt=TT)[ck]
            ev1[("store", ts)] = s.emit("sync", lambda sy, slot=slot, dstm=dstm: sy.dma_start(
                out=dstm, in_=stage[slot][:]), wait=W(evr[(b, TT - 1)]), sig=s_sos[slot], amt=16)

    if build_stage >= 1:
        s.emit("sync", lambda sy: sy.dma_start(out=ga_rep[:], in_=dws_grep.ap()[:, 0:B * D]),
               wait=W(bar1, ev_grs), sig=s_gr, amt=16)
        ga_ready = (s_gr, s.cnt["s_gr"])
        for b_ in range(B):
            s.emit("vector", lambda v, b_=b_: v.tensor_tensor(
                sagam[:, b_ * KC:(b_ + 1) * KC], m_sa[:, b_ * KC:(b_ + 1) * KC],
                gam[:, 0:KC], OP.mult), wait=W(bar1, modv_done) if b_ == 0 else ())
        for b0 in range(2):
            emit_A1_tp(b0)
            emit_A1_stats(b0)
            emit_S21(b0)
            emit_XN(b0)
        for it in range(NCH + 1):
            th96, finb = p1b_thunks(it) if it < NCH else ([], None)
            for pas in (0, 1):
                if it >= 1:
                    if pas == 0:
                        sa = chain_stream(2 * it - 2, 0, ("y", 2 * it - 3, 0))
                        sb = chain_stream(2 * it - 1, 1, ("y", 2 * it - 4, 1))
                    else:
                        sa = chain_stream(2 * it - 2, 1, ("y", 2 * it - 1, 1))
                        sb = chain_stream(2 * it - 1, 0, ("y", 2 * it - 2, 0))
                    steps = stream_zip(sa, sb)
                else:
                    steps = []
                run_interleaved(steps, th96[pas * 24:(pas + 1) * 24])
            if finb is not None:
                finb()
            if it >= 1:
                emit_W1pair(it - 1)
            if it + 1 < NCH:
                for ss2 in (2 * it + 2, 2 * it + 3):
                    emit_A1_tp(ss2)
                    emit_A1_stats(ss2)
                    emit_S21(ss2)
                    emit_XN(ss2)

    s.emit("gpsimd", lambda e: None, sig=s_bar)

    ffn_m1g0 = {}
    ffn_m2g0 = {}

    def queue_mm1_groups(ss, extra=()):
        ffn_m1g0[ss] = ring_state["g"]
        for hc in range(HC):
            srcw = dws_w1t.ap()[:, (hc * KC) * 128:(hc * KC + 8) * 128]
            ring_load(ringf, [(srcw, 0, 1024)], wait_extra=extra)
            extra = ()

    def queue_mm2_groups(ts):
        ffn_m2g0[ts] = ring_state["g"]
        for oc in range(2):
            for kcp in range(16):
                reqs = []
                for jj in range(2):
                    kc = kcp * 2 + jj
                    reqs.append((dws_w2t.ap()[kc * 128:(kc + 1) * 128,
                                              oc * 512:(oc + 1) * 512], jj * 512, 512))
                ring_load(ringf, reqs)

    if build_stage >= 2:
        queue_mm1_groups(0, extra=tt_done)
        for ss in range(1, NST):
            queue_mm1_groups(ss)
            queue_mm2_groups(ss - 1)
        queue_mm2_groups(NST - 1)

    p1.close()
    for eng in ENGINES:
        if eng != "gpsimd":
            s.emit(eng, lambda e: None, sig=s_bar)
    bar2 = (s_bar, s.cnt["s_bar"])

    p2 = contextlib.ExitStack()
    P2 = p2.enter_context
    HHALF = HC // 2
    x2fm = P2(nc.sbuf_tensor("x2fm", [128, KC * C], F32))
    xh2 = P2(nc.sbuf_tensor("xh2", [128, KC * C], F32))
    w2d = P2(nc.sbuf_tensor("w2d", [128, 4 * C], F32))
    m2scr = P2(nc.sbuf_tensor("m2scr", [128, 512], F32))
    hsq1 = P2(nc.sbuf_tensor("hsq1", [128, KC * C], BF16))
    xq1 = [P2(nc.sbuf_tensor(f"xq1_{i}", [128, KC * C], BF16)) for i in range(2)]
    Rb = P2(nc.sbuf_tensor("Rb", [128, HC * C], F32))
    B2 = P2(nc.sbuf_tensor("B2", [128, HC * C], F32))
    h2b = P2(nc.sbuf_tensor("h2b", [128, HC * C], BF16))
    xq2 = P2(nc.sbuf_tensor("xq2", [128, HC * C], BF16))
    gb_rep = P2(nc.sbuf_tensor("gb_rep", [128, B * D], BF16))
    mrow2 = P2(nc.sbuf_tensor("mrow2", [1, 2 * C], F32))
    srow2 = P2(nc.sbuf_tensor("srow2", [1, C], F32))
    arow = P2(nc.sbuf_tensor("arow", [1, C], F32))
    growA = P2(nc.sbuf_tensor("growA", [1, C], F32))
    growB = P2(nc.sbuf_tensor("growB", [1, C], F32))
    hrow = P2(nc.sbuf_tensor("hrow", [1, C], F32))
    hrowB = P2(nc.sbuf_tensor("hrowB", [1, C], F32))
    mrep2 = P2(nc.sbuf_tensor("mrep2", [128, 2 * C], F32))
    g1rep = P2(nc.sbuf_tensor("g1rep", [128, 4 * C], F32))
    g2rep = P2(nc.sbuf_tensor("g2rep", [128, 4 * C], F32))
    a2rep = [P2(nc.sbuf_tensor(f"a2rep{i}", [128, 2 * C], F32)) for i in range(2)]
    mvst2 = P2(nc.sbuf_tensor("mvst2", [128, 4], F32))
    varr2 = P2(nc.sbuf_tensor("varr2", [128, 2], F32))
    msq22 = P2(nc.sbuf_tensor("msq22", [128, 2], F32))
    mx2 = P2(nc.sbuf_tensor("mx2", [128, 2], F32))
    sqmx = P2(nc.sbuf_tensor("sqmx", [128, 2], F32))
    f2tp = P2(nc.sbuf_tensor("f2tp", [128, 2], F32))
    f2tok = P2(nc.sbuf_tensor("f2tok", [128, 2], F32))
    abn2 = P2(nc.sbuf_tensor("abn2", [128, HC], F32))
    sbn1 = P2(nc.sbuf_tensor("sbn1", [128, B * KC], F32))

    def emit_xl2(ss):
        slot = ss % 4
        lastp1 = {0: NST - 4, 1: NST - 3, 2: NST - 2, 3: NST - 1}
        wv = ([ev1[("store", ss)], ev1.get(("store", lastp1[ss]))] if ss < 4
              else [ev1.get(("ostore", ss - 4)), ev1[("store", ss)]])
        b, ck = ss % 2, ss // 2
        src = dxmid.ap()[b].rearrange("(ck tt p) d -> ck p tt d", p=128, tt=TT)[ck]
        ev1[("xl2", ss)] = s.emit("sync", lambda sy, slot=slot, src=src: sy.dma_start(
            out=stage[slot][:], in_=src), wait=W(*wv), sig=s_xls[slot], amt=16)

    def emit_A2_tp(ss):
        slot = ss % 4
        last = None
        for tt in range(TT):
            for kcg in range(2):
                bk = kcg
                ev = None
                for i2 in range(4):
                    kc = kcg * 4 + i2
                    ev = s.emit("tensor", lambda t, slot=slot, tt=tt, kc=kc, i2=i2, bk=bk: t.matmul(
                        pb[bk][0:128, i2 * 128:(i2 + 1) * 128],
                        stage[slot][:, tt * 1024 + kc * 128:tt * 1024 + (kc + 1) * 128],
                        identf[:], start=(i2 == 0), stop=(i2 == 3), is_transpose=True),
                        wait=W(ev1[("xl2", ss)], ev_bank[bk],
                               ev1.get(("xh2", ss - 1)), bar2 if ss == 0 else None) if i2 == 0 else ())

                def evacx(sc, tt=tt, kcg=kcg, bk=bk):
                    dst = x2fm[:].rearrange("p (kc c) -> p kc c", c=C)[
                        :, kcg * 4:(kcg + 1) * 4, tt * 128:(tt + 1) * 128]
                    return sc.copy(dst, pb[bk][0:128, 0:512])
                last = ev_bank[bk] = s.emit("scalar", evacx, wait=W(ev))
        ev1[("xfm2", ss)] = last

    def emit_stats_a_v(ss):
        slot = ss % 4
        ev = None
        for tt in range(TT):
            s.emit("vector", lambda v, slot=slot, tt=tt: v.tensor_scalar(
                hsq1[:].bitcast(F32)[:, 0:1024], stage[slot][:, tt * 1024:(tt + 1) * 1024],
                1.0 / D, None, OP.mult, OP.add, accum_out=mvst2[:, tt:tt + 1]),
                wait=W(ev1[("xl2", ss)], ev1.get(("mrow2", ss - 1)),
                       ev1.get(("sqabs", ss - 1))) if tt == 0 else ())
            ev = s.emit("vector", lambda v, slot=slot, tt=tt: v.scalar_tensor_tensor(
                hsq1[:].bitcast(F32)[:, 0:1024], stage[slot][:, tt * 1024:(tt + 1) * 1024], 1.0,
                stage[slot][:, tt * 1024:(tt + 1) * 1024], OP.mult, OP.mult,
                accum_out=mvst2[:, 2 + tt:3 + tt]))
        ev = s.emit("vector", lambda v: v.tensor_scalar(
            varr2[:], mvst2[:, 2:4], 1.0 / D, None, OP.mult), wait=W(ev))
        ev = s.emit("vector", lambda v: v.tensor_tensor(
            msq22[:], mvst2[:, 0:2], mvst2[:, 0:2], OP.mult))
        ev = s.emit("vector", lambda v: v.tensor_tensor(varr2[:], varr2[:], msq22[:], OP.subtract))
        ev1[("acc", ss)] = ev
        ev = s.emit("scalar", lambda sc: sc.activation(varr2[:], varr2[:], AF.Ln,
                                                       bias=cvec[:, CEPS5:CEPS5 + 1]), wait=W(ev))
        ev1[("rstd", ss)] = s.emit("scalar", lambda sc: sc.activation(
            varr2[:], varr2[:], AF.Exp, scale=-0.5))

    def emit_stats_a_pe(ss):
        ev = None
        for i2, (buf_, col) in enumerate([(mvst2, 0), (mvst2, 1), (varr2, 0), (varr2, 1)]):
            ev = s.emit("tensor", lambda t, buf_=buf_, col=col, i2=i2: t.matmul(
                pb[3][0:1, i2 * 128:(i2 + 1) * 128], buf_[:, col:col + 1], identf[:],
                start=(i2 == 0), stop=(i2 == 3), is_transpose=True),
                wait=W(ev1[("rstd", ss)], ev_bank[3]) if i2 == 0 else ())
        ev_mrow = s.emit("scalar", lambda sc: sc.copy(mrow2[:], pb[3][0:1, 0:512]), wait=W(ev))
        ev_bank[3] = ev_mrow
        ev1[("mrow2", ss)] = ev_mrow
        ev = s.emit("tensor", lambda t: t.matmul(pb[2][0:128, 0:C], onesf[0:1, :],
                                                 mrow2[0:1, 0:C], start=True, stop=False),
                    wait=W(ev_mrow, ev_bank[2], ev1.get(("xh2", ss - 1))))
        ev = s.emit("tensor", lambda t: t.matmul(pb[2][0:128, C:2 * C], onesf[0:1, :],
                                                 mrow2[0:1, C:2 * C], start=False, stop=True))
        ev_mrep = s.emit("scalar", lambda sc: sc.copy(mrep2[:], pb[2][0:128, 0:2 * C]), wait=W(ev))
        ev_bank[2] = ev_mrep
        ev1[("mrep", ss)] = ev_mrep

    def emit_stats_b(ss):
        b = ss % 2
        ev = None
        for kc in range(KC):
            s.emit("vector", lambda v, kc=kc: v.tensor_tensor(
                xf_slice(xh2, kc), xf_slice(x2fm, kc), mrep2[:, 0:C], OP.subtract),
                wait=W(ev1[("xfm2", ss)], ev1[("mrep", ss)],
                       ev1.get(("xq1", ss - 1))) if kc == 0 else ())
            ev = s.emit("vector", lambda v, kc=kc, b=b: v.scalar_tensor_tensor(
                xf_slice(xh2, kc), xf_slice(xh2, kc), m_cb1[:, b * KC + kc:b * KC + kc + 1],
                mrep2[:, C:2 * C], OP.mult, OP.mult))
            ev1[("xh2k", ss, kc)] = ev
        ev1[("xh2", ss)] = ev

    def emit_SQ_a_s(ss):
        b = ss % 2
        for kc in range(KC):
            ev1[("hsqk", ss, kc)] = s.emit("scalar", lambda sc, kc=kc, b=b: sc.activation(
                xf_slice(hsq1, kc), xf_slice(xh2, kc), AF.Square,
                bias=m_sb[:, b * KC + kc:b * KC + kc + 1]),
                wait=W(ev1[("xh2k", ss, kc)], ev1[("acc", ss)] if kc == 0 else None))
        ev = None
        for kc in range(KC):
            ev = s.emit("scalar", lambda sc, kc=kc, b=b: sc.activation(
                xf_slice(xh2, kc), xf_slice(xh2, kc), AF.Identity,
                scale=n1v[:, kc:kc + 1], bias=sbn1[:, b * KC + kc:b * KC + kc + 1]))
        ev1[("xn1", ss)] = ev

    def emit_SQ_row(ss):
        ev = s.emit("scalar", lambda sc: sc.copy(srow2[:], pb[3][0:1, 0:C]),
                    wait=[lambda ss=ss: ev1[("ssum1", ss)]])
        ev1[("srowev", ss)] = ev
        ev = s.emit("scalar", lambda sc: sc.activation(srow2[:], srow2[:], AF.Ln, scale=1.0 / D,
                                                       bias=cvec[0:1, CEPS8:CEPS8 + 1]))
        ev1[("r1", ss)] = s.emit("scalar", lambda sc: sc.activation(
            srow2[:], srow2[:], AF.Exp, scale=-0.5))
        ev1[("sqabs", ss)] = s.emit("scalar", lambda sc: sc.activation(hsq1[:], xh2[:], AF.Abs),
                                    wait=[lambda ss=ss: ev1[("ssum1", ss)]] + W(ev1[("xn1", ss)]))

    def emit_SQ_sums(ss):
        ev = None
        for kc in range(KC):
            ev = s.emit("tensor", lambda t, kc=kc: t.matmul(
                pb[3][0:1, 0:C], onesb[:], xf_slice(hsq1, kc),
                start=(kc == 0), stop=(kc == KC - 1)),
                wait=W(ev1[("hsqk", ss, kc)], ev_bank[3] if kc == 0 else None))
        ev1[("ssum1", ss)] = ev
        ev_bank[3] = ev1[("srowev", ss)]

    def emit_SQ_b(ss):
        ev = None
        for sz in [4, 2, 1]:
            ev = s.emit("vector", lambda v, sz=sz: v.tensor_tensor(
                hsq1[:, 0:sz * C], hsq1[:, 0:sz * C], hsq1[:, sz * C:2 * sz * C], OP.max),
                wait=W(ev1[("sqabs", ss)]) if sz == 4 else ())
        ev_tr = s.emit("vector", lambda v: v.tensor_copy(w2d[:, 0:C], hsq1[:, 0:C]),
                       wait=W(ev, ev1.get(("p2tp", ss - 2))))
        evm = None
        for tt in range(TT):
            ev = s.emit("tensor", lambda t, tt=tt: t.matmul(
                pb[0][0:128, 0:128], w2d[:, tt * 128:(tt + 1) * 128], identf[:],
                start=True, stop=True, is_transpose=True),
                wait=W(ev_tr, ev_bank[0]))
            ev = s.emit("scalar", lambda sc, tt=tt: sc.copy(
                w2d[:, C + tt * 128:C + (tt + 1) * 128], pb[0][0:128, 0:128]), wait=W(ev))
            ev_bank[0] = ev
            evm = s.emit("vector", lambda v, tt=tt: v.tensor_reduce(
                sqmx[:, tt:tt + 1], w2d[:, C + tt * 128:C + (tt + 1) * 128], AX.X, OP.max),
                wait=W(ev))
        ev1[("sqtp", ss)] = ev_bank[0]
        evc = s.emit("vector", lambda v: v.tensor_scalar(
            sqmx[:], sqmx[:], 1e-5, None, OP.max), wait=W(evm))
        ev = None
        for tt in range(TT):
            ev = s.emit("tensor", lambda t, tt=tt: t.matmul(
                pb[1][0:1, tt * 128:(tt + 1) * 128], sqmx[:, tt:tt + 1], identf[:],
                start=(tt == 0), stop=(tt == TT - 1), is_transpose=True),
                wait=W(evc, ev_bank[1]) if tt == 0 else ())
        ev = s.emit("scalar", lambda sc: sc.copy(growA[:], pb[1][0:1, 0:C]), wait=W(ev))
        ev_bank[1] = ev
        ev = s.emit("vector", lambda v: v.tensor_tensor(growA[:], growA[:], srow2[:], OP.mult),
                    wait=W(ev, ev1[("r1", ss)]))
        ev = s.emit("vector", lambda v: v.tensor_scalar(growA[:], growA[:], 1e-5, None, OP.max))
        ev = s.emit("scalar", lambda sc: sc.activation(growB[:], growA[:], AF.Ln), wait=W(ev))
        ev_s1 = s.emit("scalar", lambda sc: sc.activation(growB[:], growB[:], AF.Exp, scale=-1.0,
                                                          bias=cvec[0:1, CL127:CL127 + 1]))
        ev_g1 = s.emit("vector", lambda v: v.tensor_tensor(growB[:], growB[:], srow2[:], OP.mult),
                       wait=W(ev_s1))
        ev_a = s.emit("vector", lambda v: v.tensor_scalar(
            arow[:], growA[:], w1sr[:], 1.0 / 127.0, OP.mult, OP.mult), wait=W(ev))
        ev_a = s.emit("scalar", lambda sc: sc.activation(arow[:], arow[:], AF.Square),
                      wait=W(ev_a))
        ev_a = s.emit("vector", lambda v: v.tensor_scalar(
            arow[:], arow[:], ascr[:], None, OP.mult), wait=W(ev_a))
        ev = s.emit("tensor", lambda t: t.matmul(pb[3][0:128, 0:C], onesf[0:1, :], growB[0:1, :],
                                                 start=True, stop=True),
                    wait=W(ev_g1, ev_bank[3], ev1.get(("q1", ss - 1))))
        evg = s.emit("scalar", lambda sc: sc.copy(g1rep[:, 0:C], pb[3][0:128, 0:C]), wait=W(ev))
        ev_bank[3] = evg
        evg = s.emit("vector", lambda v: v.tensor_copy(g1rep[:, C:2 * C], g1rep[:, 0:C]),
                     wait=W(evg))
        evg = s.emit("vector", lambda v: v.tensor_copy(g1rep[:, 2 * C:4 * C], g1rep[:, 0:2 * C]))
        ev1[("g1rep", ss)] = evg
        ev = s.emit("tensor", lambda t: t.matmul(pb[3][0:128, 0:C], onesf[0:1, :], arow[0:1, :],
                                                 start=True, stop=True),
                    wait=W(ev_a, ev_bank[3], ev1.get(("p1a", ss - 2))))
        eva = s.emit("scalar", lambda sc, ss=ss: sc.copy(a2rep[ss % 2][:, 0:C], pb[3][0:128, 0:C]),
                     wait=W(ev))
        ev_bank[3] = eva
        eva = s.emit("vector", lambda v, ss=ss: v.tensor_copy(
            a2rep[ss % 2][:, C:2 * C], a2rep[ss % 2][:, 0:C]), wait=W(eva))
        ev1[("a2rep", ss)] = eva

    def emit_Q1(ss):
        p = ss % 2
        ev = None
        for j in range(2):
            ev = s.emit("vector", lambda v, j=j: v.tensor_tensor(
                xh2[:, j * 4 * C:(j + 1) * 4 * C], xh2[:, j * 4 * C:(j + 1) * 4 * C],
                g1rep[:], OP.mult),
                wait=W(ev1[("g1rep", ss)], ev1[("xn1", ss)],
                       ev1.get(("M1done", ss - 2))) if j == 0 else ())
        ev = s.emit("vector", lambda v, p=p: v.tensor_scalar(
            xq1[p][:], xh2[:], RMAGIC, RMAGIC, OP.add, OP.subtract), wait=W(ev))
        ev1[("xq1", ss)] = ev
        ev1[("q1", ss)] = ev

    def emit_M1(ss, half, q=None):
        p = ss % 2
        g0 = ffn_m1g0[ss]
        ev_evac = None
        evlast = None
        hcxr = range(0, HHALF, 2) if q is None else range(q * 8, q * 8 + 8, 2)
        for hcx in hcxr:
            hc0 = half * HHALF + hcx
            bk = 4 + ((hcx // 2 + 2) % 4)
            ev = None
            for hi in range(2):
                hc = hc0 + hi
                g = g0 + hc
                for kc in range(KC):
                    first = hi == 0 and kc == 0
                    ev = s.emit("tensor", lambda t, g=g, kc=kc, bk=bk, p=p, hi=hi,
                                first=first: t.matmul(
                                    pb[bk][0:128, hi * C:(hi + 1) * C],
                                    ring_ap(ringf, g, kc * 128, 128),
                                    xq1[p][:, kc * C:(kc + 1) * C],
                                    start=first, stop=(hi == 1 and kc == KC - 1)),
                        wait=W(ring_wait(g), ev1[("xq1", ss)],
                               ev_bank[bk] if first else None) if kc == 0 else ())
                ring_release(g, ev)
            evlast = ev
            ev_evac = ev_bank[bk] = s.emit("scalar", lambda sc, hc0=hc0, bk=bk: sc.activation(
                Rb[:, hc0 * C:(hc0 + 2) * C], pb[bk][0:128, 0:2 * C], AF.Relu),
                wait=W(ev, ev1.get(("Rfree", ss - 1, half))))
        if q is None or q == 1:
            ev1[("Rh", ss, half)] = ev_evac
        if half == 1 and q == 1:
            ev1[("M1done", ss)] = evlast

    def hf_slice(buf, hc):
        return buf[:, hc * C:(hc + 1) * C]

    def emit_p1(ts):
        for half in range(2):
            h0 = half * HHALF
            ev = s.emit("vector", lambda v, h0=h0: v.tensor_tensor(
                Rb[:, h0 * C:(h0 + HHALF) * C], Rb[:, h0 * C:(h0 + HHALF) * C],
                Rb[:, h0 * C:(h0 + HHALF) * C], OP.mult),
                wait=W(ev1[("Rh", ts, half)]))
            for q in range(8):
                ev = s.emit("vector", lambda v, h0=h0, q=q, ts=ts: v.tensor_tensor(
                    Rb[:, (h0 + 2 * q) * C:(h0 + 2 * q + 2) * C],
                    Rb[:, (h0 + 2 * q) * C:(h0 + 2 * q + 2) * C],
                    a2rep[ts % 2][:], OP.mult),
                    wait=W(ev1[("a2rep", ts)]) if q == 0 else ())
            ev1[("p1a", ts, half)] = ev
            evw = None
            for hcx in range(HHALF):
                hc = h0 + hcx
                evw = s.emit("scalar", lambda sc, hc=hc: sc.activation(
                    hf_slice(B2, hc), hf_slice(Rb, hc), AF.Identity,
                    scale=n2v[:, hc:hc + 1], bias=abn2[:, hc:hc + 1]),
                    wait=W(ev, ev1.get(("xq2", ts - 1))) if hcx == 0 else ())
            evsq = s.emit("scalar", lambda sc, h0=h0: sc.activation(
                h2b[:, h0 * C:(h0 + HHALF) * C], Rb[:, h0 * C:(h0 + HHALF) * C],
                AF.Square, bias=abi[:]),
                wait=W(evw, ev1.get(("treedone", ts - 1))))
            ev1[("Rfree", ts, half)] = evsq
            ev1[("h2b", ts, half)] = evsq
        ev1[("p1a", ts)] = ev1[("p1a", ts, 1)]

    def emit_p2row_sums(ts):
        ev = None
        for hc in range(HC):
            ev = s.emit("tensor", lambda t, hc=hc: t.matmul(
                pb[3][0:1, 0:C], onesb[:], hf_slice(h2b, hc),
                start=(hc == 0), stop=(hc == HC - 1)),
                wait=W(ev1[("h2b", ts, 0)], ev1[("h2b", ts, 1)],
                       ev_bank[3]) if hc == 0 else ())
        ev1[("ssum2", ts)] = ev
        ev = s.emit("scalar", lambda sc: sc.copy(hrow[:], pb[3][0:1, 0:C]), wait=W(ev))
        ev_bank[3] = ev
        ev = s.emit("scalar", lambda sc: sc.activation(hrow[:], hrow[:], AF.Ln, scale=1.0 / H,
                                                       bias=cvec[0:1, CEPS8:CEPS8 + 1]))
        ev1[("r2", ts)] = s.emit("scalar", lambda sc: sc.activation(
            hrow[:], hrow[:], AF.Exp, scale=-0.5))

    def emit_p2row_abs(ts):
        ev = s.emit("scalar", lambda sc: sc.activation(h2b[:], B2[:], AF.Abs),
                    wait=W(ev1[("ssum2", ts)]))
        ev1[("absdone", ts)] = ev
        for sz in [16, 8, 4, 2, 1]:
            ev = s.emit("vector", lambda v, sz=sz: v.tensor_tensor(
                h2b[:, 0:sz * C], h2b[:, 0:sz * C], h2b[:, sz * C:2 * sz * C], OP.max),
                wait=W(ev) if sz == 16 else ())
        ev = s.emit("vector", lambda v: v.tensor_copy(w2d[:, 0:C], h2b[:, 0:C]),
                    wait=W(ev, ev1.get(("sqtp", ts + 1))))
        ev1[("treedone", ts)] = ev

    def emit_p2row_rows(ts):
        evm = None
        for tt in range(TT):
            ev2 = s.emit("tensor", lambda t, tt=tt: t.matmul(
                pb[0][0:128, 0:128], w2d[:, tt * 128:(tt + 1) * 128], identf[:],
                start=True, stop=True, is_transpose=True),
                wait=W(ev1[("treedone", ts)], ev_bank[0]))
            ev2 = s.emit("scalar", lambda sc, tt=tt: sc.copy(
                w2d[:, C + tt * 128:C + (tt + 1) * 128], pb[0][0:128, 0:128]), wait=W(ev2))
            ev_bank[0] = ev2
            evm = s.emit("vector", lambda v, tt=tt: v.tensor_reduce(
                mx2[:, tt:tt + 1], w2d[:, C + tt * 128:C + (tt + 1) * 128], AX.X, OP.max),
                wait=W(ev2))
        ev1[("p2tp", ts)] = ev_bank[0]
        evc = s.emit("vector", lambda v: v.tensor_scalar(mx2[:], mx2[:], 0.0, None, OP.max),
                     wait=W(evm))
        ev1[("mx2", ts)] = evc
        ev = None
        for tt in range(TT):
            ev = s.emit("tensor", lambda t, tt=tt: t.matmul(
                pb[1][0:1, tt * 128:(tt + 1) * 128], mx2[:, tt:tt + 1], identf[:],
                start=(tt == 0), stop=(tt == TT - 1), is_transpose=True),
                wait=W(evc, ev_bank[1]) if tt == 0 else ())
        ev = s.emit("scalar", lambda sc: sc.copy(hrowB[:], pb[1][0:1, 0:C]), wait=W(ev))
        ev_bank[1] = ev
        ev = s.emit("vector", lambda v: v.tensor_tensor(hrowB[:], hrowB[:], hrow[:], OP.mult),
                    wait=W(ev, ev1[("r2", ts)]))
        ev = s.emit("vector", lambda v: v.tensor_scalar(hrowB[:], hrowB[:], 1e-5, None, OP.max))
        ev = s.emit("scalar", lambda sc: sc.activation(hrowB[:], hrowB[:], AF.Ln), wait=W(ev))
        ev = s.emit("scalar", lambda sc: sc.activation(hrowB[:], hrowB[:], AF.Exp, scale=-1.0,
                                                       bias=cvec[0:1, CL127:CL127 + 1]))
        ev_g2 = s.emit("vector", lambda v: v.tensor_tensor(hrowB[:], hrowB[:], hrow[:], OP.mult),
                       wait=W(ev))
        ev = s.emit("tensor", lambda t: t.matmul(pb[3][0:128, 0:C], onesf[0:1, :], hrowB[0:1, :],
                                                 start=True, stop=True),
                    wait=W(ev_g2, ev_bank[3], ev1.get(("xq2", ts - 1))))
        evg = s.emit("scalar", lambda sc: sc.copy(g2rep[:, 0:C], pb[3][0:128, 0:C]), wait=W(ev))
        ev_bank[3] = evg
        evg = s.emit("vector", lambda v: v.tensor_copy(g2rep[:, C:2 * C], g2rep[:, 0:C]),
                     wait=W(evg))
        evg = s.emit("vector", lambda v: v.tensor_copy(g2rep[:, 2 * C:4 * C], g2rep[:, 0:2 * C]))
        ev1[("g2rep", ts)] = evg
        evt = None
        for tt in range(TT):
            evt = s.emit("tensor", lambda t, tt=tt: t.matmul(
                pb[1][0:128, tt:tt + 1], hrow[0:1, tt * 128:(tt + 1) * 128],
                onesf[0:1, 0:1], start=(tt == 0), stop=(tt == TT - 1), is_transpose=True),
                wait=W(ev1[("r2", ts)], ev_bank[1]) if tt == 0 else ())
        evt = s.emit("scalar", lambda sc: sc.copy(f2tp[:, 0:TT], pb[1][0:128, 0:TT]),
                     wait=W(evt))
        ev_bank[1] = evt
        evf = s.emit("vector", lambda v: v.tensor_tensor(
            f2tok[:], mx2[:], f2tp[:, 0:TT], OP.mult), wait=W(evt, ev1[("mx2", ts)]))
        evf = s.emit("vector", lambda v: v.tensor_scalar(
            f2tok[:], f2tok[:], 1e-5, None, OP.max))
        evf = s.emit("vector", lambda v: v.tensor_scalar(
            f2tok[:], f2tok[:], 1.0 / 127.0, w2sv[:], OP.mult, OP.mult))
        ev1[("f2", ts)] = evf

    def emit_quant2(ts):
        ev = None
        for j in range(8):
            ev = s.emit("vector", lambda v, j=j: v.tensor_tensor(
                B2[:, j * 4 * C:(j + 1) * 4 * C], B2[:, j * 4 * C:(j + 1) * 4 * C],
                g2rep[:], OP.mult),
                wait=W(ev1[("g2rep", ts)], ev1[("absdone", ts)],
                       ev1.get(("M2done", ts - 1))) if j == 0 else ())
        ev = s.emit("vector", lambda v: v.tensor_scalar(
            xq2[:], B2[:], RMAGIC, RMAGIC, OP.add, OP.subtract), wait=W(ev))
        ev1[("xq2", ts)] = ev

    def emit_M2(ts, oc, part=0):
        b, ck = ts % 2, ts // 2
        g0 = ffn_m2g0[ts]
        slot = ts % 4
        if oc == 0:
            kr = range(0, 4) if part == 0 else range(4, 16)
        else:
            kr = range(0, 8) if part == 0 else range(8, 16)
        bks = (6, 7) if oc == 0 else (4, 5)
        for kcp in kr:
            g = g0 + oc * 16 + kcp
            for jj in range(2):
                kc = kcp * 2 + jj
                for t_ in range(TT):
                    bk = bks[t_]
                    wv = []
                    if jj == 0 and t_ == 0:
                        wv.append(ring_wait(g))
                    if kc == 0:
                        wv += [ev1[("xq2", ts)], ev_bank[bk]]
                    ev1[("m2mm", ts, oc, t_)] = s.emit(
                        "tensor", lambda t, g=g, kc=kc, jj=jj, t_=t_, bk=bk: t.matmul(
                            pb[bk][0:128, 0:512],
                            xq2[:, kc * C + t_ * 128:kc * C + (t_ + 1) * 128],
                            ring_ap(ringf, g, jj * 512, 512),
                            start=(kc == 0), stop=(kc == HC - 1)),
                        wait=W(*wv))
                    if jj == 1 and t_ == TT - 1:
                        ring_release(g, ev1[("m2mm", ts, oc, t_)])
        if part == 1:
            evr = ev1.get(("m2evr", ts))
            for t_ in range(TT):
                bk = bks[t_]
                ev = s.emit("vector", lambda v, bk=bk, b=b, oc=oc, t_=t_: v.scalar_tensor_tensor(
                    m2scr[:], pb[bk][0:128, 0:512], f2tok[:, t_:t_ + 1],
                    gb_rep[:, b * D + oc * 512:b * D + (oc + 1) * 512], OP.mult, OP.mult),
                    wait=W(ev1[("m2mm", ts, oc, t_)], ev1[("f2", ts)], gb_ready))
                evr = s.emit("vector", lambda v, slot=slot, t_=t_, oc=oc: v.tensor_tensor(
                    stage[slot][:, t_ * 1024 + oc * 512:t_ * 1024 + (oc + 1) * 512],
                    stage[slot][:, t_ * 1024 + oc * 512:t_ * 1024 + (oc + 1) * 512],
                    m2scr[:], OP.add))
                ev_bank[bk] = evr
            ev1[("m2evr", ts)] = evr
        if oc == 1 and part == 1:
            ev1[("M2done", ts)] = ev1[("m2mm", ts, 1, TT - 1)]
            dsto = dout.ap()[b].rearrange("(ck tt p) d -> ck p tt d", p=128, tt=TT)[ck]
            ev1[("ostore", ts)] = s.emit("sync", lambda sy, slot=slot, dsto=dsto: sy.dma_start(
                out=dsto, in_=stage[slot][:]), wait=W(ev1[("m2evr", ts)]),
                sig=s_sos[slot], amt=16)

    if build_stage >= 2:
        s.emit("sync", lambda sy: sy.dma_start(out=gb_rep[:], in_=dws_grep.ap()[:, B * D:]),
               wait=W(bar2, ev_grs), sig=s_gr, amt=16)
        gb_ready = (s_gr, s.cnt["s_gr"])
        s.emit("vector", lambda v: v.tensor_scalar(
            abn2[:], n2v[:], abi[:], None, OP.mult), wait=W(bar2))
        for b_ in range(B):
            s.emit("vector", lambda v, b_=b_: v.tensor_tensor(
                sbn1[:, b_ * KC:(b_ + 1) * KC], m_sb[:, b_ * KC:(b_ + 1) * KC],
                n1v[:, 0:KC], OP.mult))
        emit_xl2(0)
        emit_xl2(1)
        emit_A2_tp(0)
        emit_stats_a_v(0)
        emit_stats_a_pe(0)
        emit_stats_b(0)
        emit_SQ_a_s(0)
        emit_SQ_row(0)
        emit_SQ_sums(0)
        emit_SQ_b(0)
        emit_Q1(0)
        for it in range(NST + 1):
            nss = it + 1
            if nss + 1 < NST:
                emit_xl2(nss + 1)
            if it >= 1:
                emit_p2row_sums(it - 1)
                emit_p2row_abs(it - 1)
            if it < NST:
                emit_M1(it, 0)
            if it >= 1:
                emit_p2row_rows(it - 1)
            if nss < NST:
                emit_stats_a_v(nss)
            if it >= 1:
                emit_quant2(it - 1)
            if nss < NST:
                emit_A2_tp(nss)
            if it < NST:
                emit_M1(it, 1, 0)
            if nss < NST:
                emit_stats_a_pe(nss)
            if it < NST:
                emit_M1(it, 1, 1)
            if nss < NST:
                emit_stats_b(nss)
                emit_SQ_a_s(nss)
                emit_SQ_row(nss)
            if it < NST:
                emit_p1(it)
            if it >= 1:
                emit_M2(it - 1, 0, 0)
            if nss < NST:
                emit_SQ_sums(nss)
            if it >= 1:
                emit_M2(it - 1, 0, 1)
                emit_M2(it - 1, 1, 0)
            if nss < NST:
                emit_SQ_b(nss)
                emit_Q1(nss)
            if it >= 1:
                emit_M2(it - 1, 1, 1)
    p2.close()

    tap_src = {"xmid": dxmid, "mod": dws_mod, "w1t": dws_w1t, "w2t": dws_w2t,
               "attn": dws_attn, "wo": dws_wo}
    all_done = [(s_sos[i], s.cnt[f"s_so{i}"]) for i in range(4)] + sc_done_l
    prev_tap = None
    for name in taps:
        srcd = tap_src[name]
        prev_tap = s.emit("gpsimd", lambda sy, a=srcd, d=tap_d[name]: sy.dma_start(
            out=d.ap(), in_=a.ap()), wait=all_done + W(prev_tap), sig=s_st, amt=16)
        all_done = []
    fin = [(s_sos[i], s.cnt[f"s_so{i}"]) for i in range(4)] + [(s_st, s.cnt["s_st"])]
    s.emit("sync", lambda sy: None, wait=fin)

    block = es.enter_context(nc.Block())
    s.install(block, self_waits=debug_sync)
    es.close()
    return nc


_CACHE = {}


def kernel(L0_attn_x, L0_attn_gamma, L0_attn_wq, L0_attn_wk, L0_attn_wv,
           L0_attn_wa, L0_attn_wg, L0_attn_wo, L0_ffn_norm1_w, L0_ffn_w1,
           L0_ffn_act_scale, L0_ffn_act_bias, L0_ffn_norm2_w, L0_ffn_w2,
           L0_c, L0_wmod):
    if "nc" not in _CACHE:
        _CACHE["nc"] = build_program()
    nc = _CACHE["nc"]
    cc = np.ascontiguousarray
    shared = {
        "wq": cc(L0_attn_wq), "wk": cc(L0_attn_wk), "wv": cc(L0_attn_wv),
        "wa": cc(L0_attn_wa), "wg": cc(L0_attn_wg), "wo": cc(L0_attn_wo),
        "gamma": cc(L0_attn_gamma), "n1w": cc(L0_ffn_norm1_w),
        "n2w": cc(L0_ffn_norm2_w), "w1": cc(L0_ffn_w1), "w2": cc(L0_ffn_w2),
        "ascale": cc(L0_ffn_act_scale), "abias": cc(L0_ffn_act_bias),
        "wmod": cc(L0_wmod),
    }
    in_maps = []
    for i in range(NCORE):
        m = dict(shared)
        m["x"] = cc(L0_attn_x[i * B:(i + 1) * B])
        m["c"] = cc(L0_c[i * B:(i + 1) * B])
        in_maps.append(m)
    res = run_bass_kernel_spmd(nc, in_maps, core_ids=list(range(NCORE)))
    return np.concatenate([r["out"] for r in res.results], axis=0)


# version 26
# speedup vs baseline: 1.0282x; 1.0021x over previous
import contextlib
import math

import numpy as np
import concourse.bass as bass
import concourse.mybir as mybir
from concourse.bass_utils import run_bass_kernel_spmd

F32 = mybir.dt.float32
BF16 = mybir.dt.bfloat16
FP8 = mybir.dt.float8e4
I32 = mybir.dt.int32
AF = mybir.ActivationFunctionType
OP = mybir.AluOpType
AX = mybir.AxisListType

NCORE = 8
B = 2
N = 2048
D = 1024
H = 4096
KC = D // 128
HC = H // 128
C = 256
TT = C // 128
NCH = N // C
NST = B * NCH
SQD = float(math.sqrt(D))
RMAGIC = 12582912.0
FORCE_ZERO_STATE = False
PI = float(np.pi)

ENGINES = ("sync", "vector", "scalar", "tensor", "gpsimd")


COMPUTE_ENGINES = ("vector", "scalar", "tensor", "gpsimd")


class Sched:

    def __init__(self, nc, es):
        self.nc = nc
        self.q = {e: [] for e in ENGINES}
        self.cnt = {}
        self._names = {}
        self.es = es
        self.esem = {}
        self.ecnt = {e: 0 for e in ENGINES}

    def sem(self, name):
        h = self.es.enter_context(self.nc.semaphore(name))
        self._names[id(h)] = name
        self.cnt[name] = 0
        return h

    def init_engine_sems(self):
        for e in COMPUTE_ENGINES:
            self.esem[e] = self.sem("se_" + e)

    def emit(self, eng, fn, wait=(), sig=None, amt=1):
        wait = tuple(w for w in wait if w is not None and (callable(w) or w[0] is not None))
        prior = self.ecnt[eng] if eng in COMPUTE_ENGINES else 0
        ev = None
        if sig is not None:
            nm = self._names[id(sig)]
            self.cnt[nm] += amt
            ev = (sig, self.cnt[nm])
        elif eng in COMPUTE_ENGINES:
            self.ecnt[eng] += 1
            ev = (self.esem[eng], self.ecnt[eng])
        self.q[eng].append((wait, fn, ev, amt, prior))
        return ev

    def install(self, block, self_waits=False):
        for eng in ENGINES:
            items = self.q[eng]
            dec = getattr(block, eng)
            es_ = self.esem.get(eng)
            sw = self_waits or eng in ("vector", "scalar", "gpsimd")

            def make(items, es_, sw=sw):
                def body(e):
                    for waits, fn, ev, amt, prior in items:
                        if sw and es_ is not None and prior > 0:
                            e.wait_ge(es_, prior)
                        for w in waits:
                            s_, v = w() if callable(w) else w
                            e.wait_ge(s_, v)
                        instr = fn(e)
                        if ev is not None:
                            if instr is None:
                                e.sem_inc(ev[0], amt)
                            else:
                                instr.then_inc(ev[0], amt)
                return body

            dec(make(items, es_))


def build_program(taps=None, build_stage=2, debug_sync=False):
    taps = taps if taps is not None else {}
    nc = bass.Bass("TRN2", target_bir_lowering=False,
                   detect_race_conditions=debug_sync)

    dx = nc.dram_tensor("x", [B, N, D], F32, kind="ExternalInput")
    dwq = nc.dram_tensor("wq", [D, D], F32, kind="ExternalInput")
    dwk = nc.dram_tensor("wk", [D, D], F32, kind="ExternalInput")
    dwv = nc.dram_tensor("wv", [D, D], F32, kind="ExternalInput")
    dwa = nc.dram_tensor("wa", [D, 2 * D], F32, kind="ExternalInput")
    dwg = nc.dram_tensor("wg", [D, D], F32, kind="ExternalInput")
    dwo = nc.dram_tensor("wo", [D, D], F32, kind="ExternalInput")
    dgamma = nc.dram_tensor("gamma", [D], F32, kind="ExternalInput")
    dn1w = nc.dram_tensor("n1w", [D], F32, kind="ExternalInput")
    dn2w = nc.dram_tensor("n2w", [H], F32, kind="ExternalInput")
    dw1 = nc.dram_tensor("w1", [H, D], F32, kind="ExternalInput")
    dw2 = nc.dram_tensor("w2", [D, H], F32, kind="ExternalInput")
    dascale = nc.dram_tensor("ascale", [1], F32, kind="ExternalInput")
    dabias = nc.dram_tensor("abias", [1], F32, kind="ExternalInput")
    dc = nc.dram_tensor("c", [B, D], F32, kind="ExternalInput")
    dwmod = nc.dram_tensor("wmod", [D, 6 * D], F32, kind="ExternalInput")
    dout = nc.dram_tensor("out", [B, N, D], F32, kind="ExternalOutput")

    NBLK_ATTN = 4 * 64 + 16 * 8
    dws_attn = nc.dram_tensor("ws_attn", [128, NBLK_ATTN * 128], BF16, kind="Internal")
    dws_wo = nc.dram_tensor("ws_wo", [D, D], BF16, kind="Internal")
    dws_w1t = nc.dram_tensor("ws_w1t", [128, HC * KC * 128], FP8, kind="Internal")
    dws_w2t = nc.dram_tensor("ws_w2t", [H, D], FP8, kind="Internal")
    dws_mod = nc.dram_tensor("ws_mod", [B, 6 * D], F32, kind="Internal")
    dws_grep = nc.dram_tensor("ws_grep", [128, 2 * B * D], BF16, kind="Internal")
    dxmid = nc.dram_tensor("xmid", [B, N, D], F32, kind="Internal")

    tap_d = {}
    for name, shape in taps.items():
        tap_d[name] = nc.dram_tensor("tap_" + name, list(shape), F32, kind="ExternalOutput")

    es = contextlib.ExitStack()
    E = es.enter_context
    s = Sched(nc, es)

    s.init_engine_sems()
    s_ld = s.sem("s_ld")
    s_wrs = [s.sem(f"s_wr{i}") for i in range(6)]
    s_wfr = [s.sem(f"s_wfr{i}") for i in range(6)]
    s_xls = [s.sem(f"s_xl{i}") for i in range(4)]
    s_xlh = [s.sem(f"s_xlh{i}") for i in range(4)]
    s_sos = [s.sem(f"s_so{i}") for i in range(4)]
    s_wm = [s.sem(f"s_wm{i}") for i in range(4)]
    s_wg = [s.sem(f"s_wg{i}") for i in range(3)]
    s_ws = [s.sem(f"s_ws{i}") for i in range(3)]
    s_wf = [s.sem(f"s_wf{i}") for i in range(4)]
    s_tt = [s.sem(f"s_tt{i}") for i in range(4)]
    s_ms = s.sem("s_ms")
    s_msb = [s.sem(f"s_msb{i}") for i in range(2)]
    s_mv = s.sem("s_mv")
    s_gr = s.sem("s_gr")
    s_st = s.sem("s_st")
    s_bar = s.sem("s_bar")

    identf = E(nc.sbuf_tensor("identf", [128, 128], F32))
    identb = E(nc.sbuf_tensor("identb", [128, 128], BF16))
    onesf = E(nc.sbuf_tensor("onesf", [128, 128], F32))
    onesb = E(nc.sbuf_tensor("onesb", [128, 1], BF16))
    zeroC = E(nc.sbuf_tensor("zeroC", [128, C], F32))
    cvec = E(nc.sbuf_tensor("cvec", [128, 10], F32))
    CEPS5, CEPS8, CPI2, CLSD, C19, C18, CL127, CZERO, CONE, CNEG1 = range(10)
    gam = E(nc.sbuf_tensor("gam", [128, KC], F32))
    n1v = E(nc.sbuf_tensor("n1v", [128, KC], F32))
    n2v = E(nc.sbuf_tensor("n2v", [128, HC], F32))
    asc = E(nc.sbuf_tensor("asc", [128, 1], F32))
    abi = E(nc.sbuf_tensor("abi", [128, 1], F32))
    ascr = E(nc.sbuf_tensor("ascr", [1, 1], F32))
    abir = E(nc.sbuf_tensor("abir", [1, 1], F32))
    m_sa = E(nc.sbuf_tensor("m_sa", [128, B * KC], F32))
    m_ca1 = E(nc.sbuf_tensor("m_ca1", [128, B * KC], F32))
    m_ga = E(nc.sbuf_tensor("m_ga", [128, B * KC], F32))
    m_sb = E(nc.sbuf_tensor("m_sb", [128, B * KC], F32))
    m_cb1 = E(nc.sbuf_tensor("m_cb1", [128, B * KC], F32))
    m_gb = E(nc.sbuf_tensor("m_gb", [128, B * KC], F32))
    w1sv = E(nc.sbuf_tensor("w1sv", [128, 1], F32))
    w2sv = E(nc.sbuf_tensor("w2sv", [128, 1], F32))
    w1sr = E(nc.sbuf_tensor("w1sr", [1, 1], F32))
    w2sr = E(nc.sbuf_tensor("w2sr", [1, 1], F32))
    stage = [E(nc.sbuf_tensor(f"stage{i}", [128, TT * 1024], F32)) for i in range(4)]
    NSLOT = 4
    NSLOTF = 6
    ring = E(nc.sbuf_tensor("ring", [128, NSLOT * 1024], BF16))
    ringf = E(nc.sbuf_tensor("ringf", [128, NSLOTF * 1024], FP8))

    pb = [E(nc.psum_tensor(f"pb{i}", [128, 512], F32)) for i in range(8)]
    ev_bank = [None] * 8

    W = lambda *evs: [e for e in evs if e is not None]

    ev_gp0 = None
    s.emit("gpsimd", lambda g: g.iota(identi[:], pattern=[[1, 128]], base=0,
                                      channel_multiplier=-1))
    s.emit("gpsimd", lambda g: g.memset(onesf[:], 1.0))
    s.emit("gpsimd", lambda g: g.memset(zeroC[:], 0.0))
    for j, valc in enumerate([1e-5, 1e-8, PI / 2, math.log(32.0), 1e-19, 1e-18,
                              math.log(127.0), 0.0, 1.0, -1.0]):
        ev_gp0 = s.emit("gpsimd", lambda g, j=j, v=valc: g.memset(cvec[:, j:j + 1], v))
    ev_idf = s.emit("vector", lambda v: v.tensor_scalar(identf[:], identi[:], 0.0, None, OP.is_equal),
                    wait=W(ev_gp0))
    ev_idb = s.emit("vector", lambda v: v.tensor_scalar(identb[:], identi[:], 0.0, None, OP.is_equal))
    ev_onb = s.emit("vector", lambda v: v.tensor_copy(onesb[:], onesf[:, 0:1]),
                    wait=W(ev_gp0))

    p0 = contextlib.ExitStack()
    P0 = p0.enter_context
    identi = P0(nc.sbuf_tensor("identi", [128, 128], I32))
    csb = P0(nc.sbuf_tensor("csb", [B, D], F32))

    def featvec_load(dst, dram, nk):
        ap = dram.ap().rearrange("(kc p) -> p kc", p=128)
        return s.emit("sync", lambda sy: sy.dma_start(out=dst[:, 0:nk], in_=ap,
                                                      allow_slow_non_contiguous=True),
                      sig=s_ld, amt=16)

    featvec_load(gam, dgamma, KC)
    featvec_load(n1v, dn1w, HC if False else KC)
    featvec_load(n2v, dn2w, HC)
    tiny = P0(nc.sbuf_tensor("tiny", [1, 2], F32))
    s.emit("sync", lambda sy: sy.dma_start(out=tiny[0:1, 0:1], in_=dascale.ap().unsqueeze(0)),
           sig=s_ld, amt=16)
    ev_t2 = s.emit("sync", lambda sy: sy.dma_start(out=tiny[0:1, 1:2], in_=dabias.ap().unsqueeze(0)),
                   sig=s_ld, amt=16)
    ev_cl = s.emit("sync", lambda sy: sy.dma_start(out=csb[:], in_=dc.ap()), sig=s_ld, amt=16)
    ev_misc = ev_cl
    ev_gs = s.emit("vector", lambda v: v.tensor_scalar(gam[:, 0:KC], gam[:, 0:KC], SQD, None, OP.mult),
                   wait=W(ev_misc))
    ev_t1 = ev_misc
    ev_t2 = ev_misc
    ev_mm = s.emit("tensor", lambda t: t.matmul(pb[3][0:128, 0:2], onesf[0:1, :], tiny[:],
                                                start=True, stop=True),
                   wait=W(ev_t1, ev_t2, ev_gp0))
    ev_as = s.emit("scalar", lambda sc: sc.copy(asc[:], pb[3][0:128, 0:1]), wait=W(ev_mm))
    ev_ab = s.emit("scalar", lambda sc: sc.copy(abi[:], pb[3][0:128, 1:2]))
    ev_asr = s.emit("vector", lambda v: v.tensor_copy(ascr[:], tiny[0:1, 0:1]), wait=W(ev_t1))
    ev_abr = s.emit("vector", lambda v: v.tensor_copy(abir[:], tiny[0:1, 1:2]), wait=W(ev_t2))
    ev_bank[3] = ev_ab

    csil = P0(nc.sbuf_tensor("csil", [B, D], F32))
    csilT = P0(nc.sbuf_tensor("csilT", [128, KC * B], BF16))
    mbuf = [P0(nc.sbuf_tensor(f"mbuf{i}", [B, 512], F32)) for i in range(2)]
    mmsk = P0(nc.sbuf_tensor("mmsk", [B, 512], F32))
    ebs = [P0(nc.sbuf_tensor(f"ebs{b}", [B, 128], F32)) for b in range(B)]
    grstg = P0(nc.sbuf_tensor("grstg", [128, 2 * B * D], BF16))
    ev_sil = s.emit("scalar", lambda sc: sc.activation(csil[:], csb[:], AF.Silu),
                    wait=W(ev_misc))
    for kc in range(KC):
        ev = s.emit("tensor", lambda t, kc=kc: t.matmul(
            pb[0][0:128, kc * B:(kc + 1) * B], csil[:, kc * 128:(kc + 1) * 128],
            identf[0:B, 0:B], start=(kc == 0), stop=(kc == KC - 1), is_transpose=True),
            wait=W(ev_sil, ev_idf, ev_bank[0]) if kc == 0 else ())
    ev_csT = s.emit("scalar", lambda sc: sc.copy(csilT[:], pb[0][0:128, 0:KC * B]),
                    wait=W(ev))
    ev_bank[0] = ev_csT
    NWM = 4
    wmb = [P0(nc.sbuf_tensor(f"wmb{i}", [128, 512], F32)) for i in range(NWM)]
    wmbb = [P0(nc.sbuf_tensor(f"wmbb{i}", [128, 512], BF16)) for i in range(NWM)]
    ev_cast = [None] * NWM
    ev_mmod = [None] * NWM
    ev = s.emit("gpsimd", lambda g: g.memset(ebs[0][0:B, :], 0.0))
    ev = s.emit("gpsimd", lambda g: g.memset(ebs[0][0:1, :], 1.0), wait=W(ev))
    ev = s.emit("gpsimd", lambda g: g.memset(ebs[1][0:B, :], 1.0), wait=W(ev))
    ev_eb = s.emit("gpsimd", lambda g: g.memset(ebs[1][0:1, :], 0.0), wait=W(ev))
    ev_mb = [None, None]
    GKIND = {2: 0, 5: 2 * D}

    def mod_tile(oc, kc):
        i = (oc * KC + kc) % NWM
        evl = s.emit("sync", lambda sy, oc=oc, kc=kc, i=i: sy.dma_start(
            out=wmb[i][:], in_=dwmod.ap()[kc * 128:(kc + 1) * 128, oc * 512:(oc + 1) * 512]),
            wait=W(ev_cast[i]), sig=s_wm[i], amt=16)
        ev_cast[i] = s.emit("scalar", lambda sc, i=i: sc.activation(wmbb[i][:], wmb[i][:], AF.Copy),
                            wait=W(evl, ev_mmod[i]))
        ev_mmod[i] = s.emit("tensor", lambda t, kc=kc, i=i: t.matmul(
            pb[1][0:B, 0:512], csilT[:, kc * B:(kc + 1) * B], wmbb[i][:],
            start=(kc == 0), stop=(kc == KC - 1)),
            wait=W(ev_cast[i], ev_csT if oc == 0 and kc == 0 else None,
                   ev_bank[1] if kc == 0 else None))
        if kc != KC - 1:
            return
        mi = oc % 2
        kind = oc // 2
        ev = s.emit("scalar", lambda sc, mi=mi: sc.copy(mbuf[mi][:], pb[1][0:B, 0:512]),
                    wait=W(ev_mmod[(oc * KC + KC - 1) % NWM], ev_mb[mi]))
        ev_bank[1] = ev
        if kind in (1, 4):
            ev = s.emit("vector", lambda v, mi=mi: v.tensor_scalar(
                mbuf[mi][:], mbuf[mi][:], 1.0, None, OP.add), wait=W(ev))
        elif kind in (2, 5):
            ev2 = s.emit("vector", lambda v, mi=mi: v.tensor_scalar(
                mmsk[:], mbuf[mi][:], 0.0, None, OP.is_ge), wait=W(ev))
            ev2 = s.emit("vector", lambda v: v.tensor_scalar(
                mmsk[:], mmsk[:], 2.0, 1.0, OP.mult, OP.subtract))
            ev3 = s.emit("scalar", lambda sc, mi=mi: sc.activation(mbuf[mi][:], mbuf[mi][:], AF.Abs),
                         wait=W(ev2))
            ev3 = s.emit("vector", lambda v, mi=mi: v.tensor_scalar(
                mbuf[mi][:], mbuf[mi][:], 0.25, None, OP.max), wait=W(ev3))
            ev = s.emit("vector", lambda v, mi=mi: v.tensor_tensor(
                mbuf[mi][:], mbuf[mi][:], mmsk[:], OP.mult))
            for b in range(B):
                evr = s.emit("tensor", lambda t, b=b, mi=mi: t.matmul(
                    pb[2][0:128, 0:512], ebs[b][:], mbuf[mi][:],
                    start=True, stop=True),
                    wait=W(ev, ev_eb, ev_bank[2]))
                ev_bank[2] = s.emit("scalar", lambda sc, b=b, oc=oc, kind=kind: sc.copy(
                    grstg[:, GKIND[kind] + b * D + (oc % 2) * 512:
                          GKIND[kind] + b * D + (oc % 2) * 512 + 512],
                    pb[2][0:128, 0:512]), wait=W(evr))
        ev_mb[mi] = s.emit("sync", lambda sy, oc=oc, mi=mi: sy.dma_start(
            out=dws_mod.ap()[:, oc * 512:(oc + 1) * 512], in_=mbuf[mi][:]),
            wait=W(ev, ev_bank[2] if kind in (2, 5) else None), sig=s_msb[mi], amt=16)

    NWS = 3
    wstg = [P0(nc.sbuf_tensor(f"wstg{i}", [128, 2048], F32)) for i in range(NWS)]
    wstgb = [P0(nc.sbuf_tensor(f"wstgb{i}", [128, 2048], BF16)) for i in range(NWS)]
    ev_wst = [None] * NWS

    def attn_tile(dram, nmc, blk0, kc, wo=False):
        i = attn_tile.n % NWS
        attn_tile.n += 1
        ncol = nmc * 128
        evl = s.emit("sync", lambda sy, dram=dram, kc=kc, i=i, ncol=ncol: sy.dma_start(
            out=wstg[i][:, 0:ncol], in_=dram.ap()[kc * 128:(kc + 1) * 128, :]),
            wait=W(ev_wst[i]), sig=s_wg[i], amt=16)
        if attn_tile.n % 2:
            evc = s.emit("scalar", lambda sc, i=i, ncol=ncol: sc.activation(
                wstgb[i][:, 0:ncol], wstg[i][:, 0:ncol], AF.Copy), wait=W(evl))
        else:
            evc = s.emit("vector", lambda v, i=i, ncol=ncol: v.tensor_copy(
                wstgb[i][:, 0:ncol], wstg[i][:, 0:ncol]), wait=W(evl))
        if wo:
            ev_wst[i] = s.emit("scalar", lambda sc, kc=kc, i=i: sc.dma_start(
                out=dws_wo.ap()[kc * 128:(kc + 1) * 128, :], in_=wstgb[i][:, 0:1024]),
                wait=W(evc), sig=s_ws[i], amt=16)
            return

        def store_blk(sc, i=i, nmc=nmc, blk0=blk0, kc=kc):
            src = wstgb[i][:, 0:nmc * 128].rearrange("p (mc j) -> p mc j", j=128)
            dst = dws_attn.ap().rearrange("p (blk j) -> p blk j", j=128)
            return sc.dma_start(out=dst[:, blk0 + kc:blk0 + nmc * KC:KC, :], in_=src)
        ev_wst[i] = s.emit("scalar", store_blk, wait=W(evc), sig=s_ws[i], amt=16)
    attn_tile.n = 0

    accb = P0(nc.sbuf_tensor("accb", [128, HC + KC], F32))
    acc1 = P0(nc.sbuf_tensor("acc1", [128, 2], F32))
    scal = P0(nc.sbuf_tensor("scal", [1, 4], F32))
    NWF = 4
    wstgF = [P0(nc.sbuf_tensor(f"wstgF{i}", [128, 4096], F32)) for i in range(NWF)]
    NTTL = 4
    ttlb = [P0(nc.sbuf_tensor(f"ttlb{i}", [128, 4096], FP8)) for i in range(NTTL)]
    wsb_ = [P0(nc.sbuf_tensor(f"wsb{i}", [128, 1], F32)) for i in range(2)]
    ev_wstF = [None] * NWF
    ev_ttlb = [None] * NTTL
    FFN_W = [(dw1, HC, w1sv, w1sr, 0), (dw2, KC, w2sv, w2sr, HC)]

    def ffn_scale_tile(wi, ti):
        dram, ntile, sv, sr, acc0 = FFN_W[wi]
        cols = int(dram.shape[1])
        i = ffn_scale_tile.n % NWF
        ffn_scale_tile.n += 1
        evl = s.emit("gpsimd", lambda sy, dram=dram, ti=ti, i=i, cols=cols: sy.dma_start(
            out=wstgF[i][:, 0:cols], in_=dram.ap()[ti * 128:(ti + 1) * 128, :]),
            wait=W(ev_wstF[i]), sig=s_wf[i], amt=16)
        ev_wstF[i] = s.emit("vector", lambda v, ti=ti, i=i, cols=cols, acc0=acc0:
                            v.scalar_tensor_tensor(
                                wstgF[i][:, 0:cols], wstgF[i][:, 0:cols], -1.0,
                                wstgF[i][:, 0:cols], OP.mult, OP.max,
                                accum_out=accb[:, acc0 + ti:acc0 + ti + 1]),
                            wait=W(evl))
        return ev_wstF[i]

    ffn_scale_tile.n = 0

    def ffn_finalize(wi, ev_acc):
        dram, ntile, sv, sr, acc0 = FFN_W[wi]
        cols = int(dram.shape[1])
        ev = s.emit("vector", lambda v, ntile=ntile, acc0=acc0: v.tensor_reduce(
            acc1[:, wi:wi + 1], accb[:, acc0:acc0 + ntile], AX.X, OP.add), wait=W(ev_acc))
        ev = s.emit("tensor", lambda t: t.matmul(pb[3][0:1, 0:1], acc1[:, wi:wi + 1],
                                                 onesf[:, 0:1], start=True, stop=True),
                    wait=W(ev, ev_bank[3]))
        nelem = float(128 * ntile * cols)
        ev = s.emit("scalar", lambda sc, nelem=nelem: sc.activation(
            scal[0:1, 0:1], pb[3][0:1, 0:1], AF.Copy, scale=1.0 / nelem), wait=W(ev))
        ev_bank[3] = ev
        ev = s.emit("vector", lambda v: v.tensor_scalar(scal[0:1, 0:1], scal[0:1, 0:1],
                                                        1e-5, None, OP.max), wait=W(ev))
        ev = s.emit("vector", lambda v: v.reciprocal(scal[0:1, 1:2], scal[0:1, 0:1]))
        s.emit("vector", lambda v, sr=sr: v.tensor_copy(sr[:], scal[0:1, 0:1]))
        ev = s.emit("tensor", lambda t: t.matmul(pb[3][0:128, 0:2], onesf[0:1, :], scal[0:1, 0:2],
                                                 start=True, stop=True),
                    wait=W(ev, ev_bank[3]))
        s.emit("scalar", lambda sc, sv=sv: sc.copy(sv[:], pb[3][0:128, 0:1]), wait=W(ev))
        ev_ws = s.emit("scalar", lambda sc, wi=wi: sc.copy(wsb_[wi][:], pb[3][0:128, 1:2]))
        ev_bank[3] = ev_ws
        return ev_ws

    def ffn_quant_tile(wi, ti, ev_ws):
        dram, ntile, sv, sr, acc0 = FFN_W[wi]
        cols = int(dram.shape[1])
        i = ffn_quant_tile.n % NWF
        ffn_quant_tile.n += 1
        evl = s.emit("gpsimd", lambda sy, dram=dram, ti=ti, i=i, cols=cols: sy.dma_start(
            out=wstgF[i][:, 0:cols], in_=dram.ap()[ti * 128:(ti + 1) * 128, :]),
            wait=W(ev_wstF[i], ffn_quant_tile.tp[i], ev_ws), sig=s_wf[i], amt=16)
        ev_wstF[i] = None
        ev = s.emit("vector", lambda v, i=i, cols=cols, wi=wi: v.tensor_scalar(
            wstgF[i][:, 0:cols], wstgF[i][:, 0:cols], wsb_[wi][:], None, OP.mult),
            wait=W(evl))
        ev = s.emit("vector", lambda v, i=i, cols=cols: v.tensor_scalar(
            wstgF[i][:, 0:cols], wstgF[i][:, 0:cols], RMAGIC, RMAGIC,
            OP.add, OP.subtract))
        ev_q = s.emit("vector", lambda v, i=i, cols=cols: v.tensor_scalar(
            wstgF[i][:, 0:cols], wstgF[i][:, 0:cols], 1.0, -1.0, OP.min, OP.max))
        i2 = ti % NTTL
        evc = None
        for jb in range(cols // 512):
            bk = 4 + ((ti + jb) % 4)
            ev = None
            for j4 in range(4):
                j = jb * 4 + j4
                ev = s.emit("tensor", lambda t, i=i, j=j, j4=j4, bk=bk: t.matmul(
                    pb[bk][0:128, j4 * 128:(j4 + 1) * 128],
                    wstgF[i][:, j * 128:(j + 1) * 128], identf[:],
                    start=(j4 == 0), stop=(j4 == 3), is_transpose=True),
                    wait=W(ev_q, ev_idf, ev_bank[bk],
                           ev_ttlb[i2] if j == 0 else None) if j4 == 0 else ())
            ffn_quant_tile.tp[i] = ev
            evc = s.emit("scalar", lambda sc, i2=i2, jb=jb, bk=bk: sc.copy(
                ttlb[i2][:, jb * 512:(jb + 1) * 512], pb[bk][0:128, 0:512]),
                wait=W(ev))
            ev_bank[bk] = evc
        if wi == 0:
            def stw(sy, ti=ti, i2=i2, cols=cols):
                return sy.dma_start(
                    out=dws_w1t.ap()[:, ti * KC * 128:(ti + 1) * KC * 128],
                    in_=ttlb[i2][:, 0:cols])
        else:
            def stw(sy, ti=ti, i2=i2, cols=cols):
                dst = dws_w2t.ap().rearrange("(j p) o -> p j o", p=128)[
                    :, :, ti * 128:(ti + 1) * 128]
                srcv = ttlb[i2][:, 0:cols].rearrange("p (j c) -> p j c", c=128)
                return sy.dma_start(out=dst, in_=srcv,
                                    allow_slow_non_contiguous=True)
        ev_ttlb[i2] = s.emit("sync", stw, wait=W(evc), sig=s_tt[i2], amt=16)
    ffn_quant_tile.n = 0
    ffn_quant_tile.tp = [None] * NWF

    attn_seq = []
    for dram, nmc, blk0 in [(dwq, 8, 0), (dwk, 8, 64), (dwv, 8, 128),
                            (dwg, 8, 192), (dwa, 16, 256)]:
        for kc in range(KC):
            attn_seq.append((dram, nmc, blk0, kc, False))
    for kc in range(KC):
        attn_seq.append((dwo, 8, 0, kc, True))
    mod_seq = [(oc, kc) for oc in range(12) for kc in range(KC)]
    scale_seq = [(0, ti) for ti in range(HC)] + [(1, ti) for ti in range(KC)]
    ev_acc = {0: None, 1: None}
    ws1 = None
    qpend = []
    nrounds = max(len(attn_seq), len(mod_seq), len(scale_seq))
    for r in range(nrounds):
        if r < len(mod_seq):
            mod_tile(*mod_seq[r])
        if r < len(attn_seq):
            attn_tile(*attn_seq[r])
        if r < len(scale_seq):
            wi, ti = scale_seq[r]
            ev_acc[wi] = ffn_scale_tile(wi, ti)
            if wi == 0 and ti == HC - 1:
                ws1 = ffn_finalize(0, ev_acc[0])
        elif ws1 is not None and qpend is not None and len(qpend) < HC:
            ffn_quant_tile(0, len(qpend), ws1)
            qpend.append(None)
    ev_garep = ev_bank[2]
    ev_grs = s.emit("sync", lambda sy: sy.dma_start(out=dws_grep.ap(), in_=grstg[:]),
                    wait=W(ev_garep), sig=s_gr, amt=16)
    ev_mv = {}
    for kind, dstv in [(0, m_sa), (1, m_ca1), (2, m_ga), (3, m_sb), (4, m_cb1), (5, m_gb)]:
        for b in range(B):
            srcap = dws_mod.ap()[b].rearrange("(kind kc p) -> p kc kind", p=128, kind=6)[:, :, kind]
            ev_mv[kind] = s.emit("sync", lambda sy, dstv=dstv, srcap=srcap, b=b: sy.dma_start(
                out=dstv[:, b * KC:(b + 1) * KC], in_=srcap, allow_slow_non_contiguous=True),
                wait=W(ev_mb[0], ev_mb[1]), sig=s_mv, amt=16)
    if ws1 is None:
        ws1 = ffn_finalize(0, ev_acc[0])
    ws2 = ffn_finalize(1, ev_acc[1])
    for ti in range(len(qpend), HC):
        ffn_quant_tile(0, ti, ws1)
    for ti in range(KC):
        ffn_quant_tile(1, ti, ws2)

    p0.close()
    for eng in ENGINES:
        s.emit(eng, lambda e: None, sig=s_bar)
    bar1 = (s_bar, s.cnt["s_bar"])
    ws_done = [(s_ws[i], s.cnt[f"s_ws{i}"]) for i in range(3)]
    tt_done = [(s_tt[i], s.cnt[f"s_tt{i}"]) for i in range(4)]
    sc_done_l = ws_done + tt_done
    sc_done_l += [(s_msb[0], s.cnt["s_msb0"]), (s_msb[1], s.cnt["s_msb1"])]
    p0_done = (sc_done_l + [(s_wg[i], s.cnt[f"s_wg{i}"]) for i in range(3)]
               + [(s_wf[i], s.cnt[f"s_wf{i}"]) for i in range(4)]
               + [(s_wm[i], s.cnt[f"s_wm{i}"]) for i in range(4)]
               + [(s_gr, s.cnt["s_gr"])])
    modv_done = (s_mv, s.cnt["s_mv"])

    ring_state = {"g": 0, "loaded": [], "release": {}, "slot": {},
                  "meta": {id(ring): {"n": 0, "nslot": NSLOT, "g_of": [],
                                      "sems": s_wrs},
                           id(ringf): {"n": 0, "nslot": NSLOTF, "g_of": [],
                                       "sems": s_wfr}}}

    def ring_release(g, ev):
        ring_state["release"][g] = ev

    def ring_load(buf, reqs, wait_extra=()):
        g = ring_state["g"]
        st = ring_state["meta"][id(buf)]
        li = st["n"]
        st["n"] += 1
        st["g_of"].append(g)
        slot = li % st["nslot"]
        ring_state["slot"][g] = slot
        waits = list(wait_extra)
        if li >= st["nslot"]:
            gprev = st["g_of"][li - st["nslot"]]
            waits.append(lambda gprev=gprev: ring_state["release"][gprev])
        ev = None
        for (src, c0, ncols) in reqs:
            ev = s.emit("gpsimd", lambda gp, buf=buf, slot=slot, c0=c0, ncols=ncols, src=src:
                        gp.dma_start(out=buf[:, slot * 1024 + c0:slot * 1024 + c0 + ncols],
                                     in_=src),
                        wait=waits, sig=st["sems"][slot], amt=16)
            waits = []
        ring_state["loaded"].append(ev)
        ring_state["g"] += 1
        return g

    def ring_wait(g):
        return ring_state["loaded"][g]

    def ring_ap(buf, g, c0, ncols):
        slot = ring_state["slot"][g]
        return buf[:, slot * 1024 + c0:slot * 1024 + c0 + ncols]

    attn_pg0 = {}
    attn_wg0 = {}

    PROJ_BLK = {"k": 64, "v": 128, "g": 192, "ar": 256, "ai": 256, "q": 0}

    def proj_desc():
        desc = []
        for pas in (0, 1):
            for kind in ("k", "v", "g", "ar", "ai", "q"):
                for b in (0, 1):
                    hb = pas if b == 0 else 1 - pas
                    for mcx in range(4):
                        mc = (8 if kind == "ai" else 0) + hb * 4 + mcx
                        desc.append((pas, b, kind, mc, hb))
        return desc

    def queue_proj_groups(ck, extra=()):
        attn_pg0[ck] = ring_state["g"]
        for (pas, b, kind, mc, hb) in proj_desc():
            blk0 = PROJ_BLK[kind]
            srcw = dws_attn.ap()[:, (blk0 + mc * KC) * 128:(blk0 + mc * KC + 8) * 128]
            ring_load(ring, [(srcw, 0, 1024)], wait_extra=extra)
            extra = ()

    def queue_wo_groups(ck):
        attn_wg0[ck] = ring_state["g"]
        for oc in range(2):
            for kcp in range(4):
                reqs = []
                for jj in range(2):
                    kc = kcp * 2 + jj
                    reqs.append((dws_wo.ap()[kc * 128:(kc + 1) * 128, oc * 512:(oc + 1) * 512],
                                 jj * 512, 512))
                ring_load(ring, reqs)

    if build_stage >= 1:
        queue_proj_groups(0, extra=[bar1] + sc_done_l)
        for ck in range(1, NCH):
            queue_proj_groups(ck)
            queue_wo_groups(ck - 1)
        queue_wo_groups(NCH - 1)

    p1 = contextlib.ExitStack()
    P1 = p1.enter_context
    x_fm = P1(nc.sbuf_tensor("x_fm", [128, KC * C], F32))
    xh = P1(nc.sbuf_tensor("xh", [128, KC * C], F32))
    hsq = P1(nc.sbuf_tensor("hsq", [128, KC * C], BF16))
    xn = [P1(nc.sbuf_tensor(f"xn{i}", [128, KC * C], BF16)) for i in range(2)]
    yb = [P1(nc.sbuf_tensor(f"yb{i}", [128, KC * C], BF16)) for i in range(2)]
    w7 = P1(nc.sbuf_tensor("w7", [128, KC * C], F32))
    w8 = P1(nc.sbuf_tensor("w8", [128, KC * C], F32))
    w9 = P1(nc.sbuf_tensor("w9", [128, KC * C], F32))
    w10 = P1(nc.sbuf_tensor("w10", [128, KC * C], F32))
    w11 = P1(nc.sbuf_tensor("w11", [128, KC * C], F32))
    psi = P1(nc.sbuf_tensor("psi", [128, KC * C], F32))
    qb = [P1(nc.sbuf_tensor(f"qb{i}", [128, KC * C], BF16)) for i in range(2)]
    kbn = [P1(nc.sbuf_tensor(f"kbn{i}", [128, KC * C], BF16)) for i in range(2)]
    vb = [P1(nc.sbuf_tensor(f"vb{i}", [128, KC * C], BF16)) for i in range(2)]
    sgb = [P1(nc.sbuf_tensor(f"sgb{i}", [128, KC * C], BF16)) for i in range(2)]
    arb = [P1(nc.sbuf_tensor(f"arb{i}", [128, KC * C], F32)) for i in range(2)]
    aib = [P1(nc.sbuf_tensor(f"aib{i}", [128, KC * C], F32)) for i in range(2)]
    mrow = P1(nc.sbuf_tensor("mrow", [1, 2 * C], F32))
    srow = P1(nc.sbuf_tensor("srow", [1, C], F32))
    mrep = P1(nc.sbuf_tensor("mrep", [128, 2 * C], F32))
    grep = P1(nc.sbuf_tensor("grep", [128, C], F32))
    mvst = P1(nc.sbuf_tensor("mvst", [128, 4], F32))
    varr = P1(nc.sbuf_tensor("varr", [128, 2], F32))
    msq2 = P1(nc.sbuf_tensor("msq2", [128, 2], F32))
    sagam = P1(nc.sbuf_tensor("sagam", [128, B * KC], F32))
    ga_rep = P1(nc.sbuf_tensor("ga_rep", [128, B * D], BF16))
    st_psi = [P1(nc.sbuf_tensor(f"stpsi{b}", [128, KC], F32)) for b in range(B)]
    st_zr = [P1(nc.sbuf_tensor(f"stzr{b}", [128, KC], F32)) for b in range(B)]
    st_zi = [P1(nc.sbuf_tensor(f"stzi{b}", [128, KC], F32)) for b in range(B)]

    ev1 = {}

    def xf_slice(buf, kc, t0=0, nt=C):
        return buf[:, kc * C + t0:kc * C + t0 + nt]

    def lastcol(buf):
        return buf[:].rearrange("p (kc c) -> p kc c", c=C)[:, :, C - 1]

    def emit_A1_tp(ss):
        b, ck = ss % 2, ss // 2
        slot = ss % 4
        wv = [] if ss < 4 else [ev1.get(("store", ss - 4))]
        src = dx.ap()[b].rearrange("(ck tt p) d -> ck p tt d", p=128, tt=TT)[ck]
        ev1[("xlh", ss)] = s.emit("sync", lambda sy, slot=slot, src=src: sy.dma_start(
            out=stage[slot][:, 0:1024], in_=src[:, 0:1, :]),
            wait=W(*wv), sig=s_xlh[slot], amt=16)
        ev1[("xl", ss)] = s.emit("sync", lambda sy, slot=slot, src=src: sy.dma_start(
            out=stage[slot][:, 1024:2048], in_=src[:, 1:2, :]), sig=s_xls[slot], amt=16)
        last = None
        for tt in range(TT):
            for kcg in range(2):
                bk = kcg
                ev = None
                for i2 in range(4):
                    kc = kcg * 4 + i2
                    ev = s.emit("tensor", lambda t, slot=slot, tt=tt, kc=kc, i2=i2, bk=bk: t.matmul(
                        pb[bk][0:128, i2 * 128:(i2 + 1) * 128],
                        stage[slot][:, tt * 1024 + kc * 128:tt * 1024 + (kc + 1) * 128],
                        identf[:], start=(i2 == 0), stop=(i2 == 3), is_transpose=True),
                        wait=W(ev1[("xlh", ss)], ev1[("xl", ss)], ev_bank[bk],
                               ev1.get(("xh", ss - 1))) if i2 == 0 else ())

                def evacx(sc, tt=tt, kcg=kcg, bk=bk):
                    dst = x_fm[:].rearrange("p (kc c) -> p kc c", c=C)[
                        :, kcg * 4:(kcg + 1) * 4, tt * 128:(tt + 1) * 128]
                    return sc.copy(dst, pb[bk][0:128, 0:512])
                last = ev_bank[bk] = s.emit(
                    "scalar", evacx,
                    wait=W(ev) + (p0_done if ss < 2 else []))
        ev1[("xfm", ss)] = last

    def emit_A1_stats(ss):
        b, ck = ss % 2, ss // 2
        slot = ss % 4
        ev = None
        for tt in range(TT):
            s.emit("vector", lambda v, slot=slot, tt=tt: v.tensor_scalar(
                hsq[:].bitcast(F32)[:, 0:1024], stage[slot][:, tt * 1024:(tt + 1) * 1024],
                1.0 / D, None, OP.mult, OP.add, accum_out=mvst[:, tt:tt + 1]),
                wait=(W(ev1[("xlh", ss)], ev1.get(("mrow", ss - 1)), ev1.get(("hsq", ss - 1)),
                        modv_done if ss == 0 else None)
                      + (p0_done if ss < 2 else [])) if tt == 0 else
                     W(ev1[("xl", ss)]) if tt == 1 else ())
            ev = s.emit("vector", lambda v, slot=slot, tt=tt: v.scalar_tensor_tensor(
                hsq[:].bitcast(F32)[:, 0:1024], stage[slot][:, tt * 1024:(tt + 1) * 1024], 1.0,
                stage[slot][:, tt * 1024:(tt + 1) * 1024], OP.mult, OP.mult,
                accum_out=mvst[:, 2 + tt:3 + tt]))
        ev = s.emit("vector", lambda v: v.tensor_scalar(
            varr[:], mvst[:, 2:4], 1.0 / D, None, OP.mult), wait=W(ev))
        ev = s.emit("vector", lambda v: v.tensor_tensor(
            msq2[:], mvst[:, 0:2], mvst[:, 0:2], OP.mult))
        ev = s.emit("vector", lambda v: v.tensor_tensor(varr[:], varr[:], msq2[:], OP.subtract))
        ev = s.emit("scalar", lambda sc: sc.activation(varr[:], varr[:], AF.Ln,
                                                       bias=cvec[:, CEPS5:CEPS5 + 1]),
                    wait=W(ev))
        ev_rstd = s.emit("scalar", lambda sc: sc.activation(varr[:], varr[:], AF.Exp, scale=-0.5))
        ev = None
        for i2, (buf_, col) in enumerate([(mvst, 0), (mvst, 1), (varr, 0), (varr, 1)]):
            ev = s.emit("tensor", lambda t, buf_=buf_, col=col, i2=i2: t.matmul(
                pb[3][0:1, i2 * 128:(i2 + 1) * 128], buf_[:, col:col + 1], identf[:],
                start=(i2 == 0), stop=(i2 == 3), is_transpose=True),
                wait=W(ev_rstd, ev_bank[3]) if i2 == 0 else ())
        ev_mrow = s.emit("scalar", lambda sc: sc.copy(mrow[:], pb[3][0:1, 0:512]),
                         wait=W(ev))
        ev_bank[3] = ev_mrow
        ev1[("mrow", ss)] = ev_mrow
        ev = s.emit("tensor", lambda t: t.matmul(pb[2][0:128, 0:C], onesf[0:1, :],
                                                 mrow[0:1, 0:C], start=True, stop=False),
                    wait=W(ev_mrow, ev_bank[2], ev1.get(("xh", ss - 1)),
                           ev1.get(("hsq", ss - 1))))
        ev = s.emit("tensor", lambda t: t.matmul(pb[2][0:128, C:2 * C], onesf[0:1, :],
                                                 mrow[0:1, C:2 * C], start=False, stop=True))
        ev_mrep = s.emit("scalar", lambda sc: sc.copy(mrep[:], pb[2][0:128, 0:2 * C]),
                         wait=W(ev))
        ev_bank[2] = ev_mrep
        ev = None
        for kc in range(KC):
            s.emit("vector", lambda v, kc=kc: v.tensor_tensor(
                xf_slice(xh, kc), xf_slice(x_fm, kc), mrep[:, 0:C], OP.subtract),
                wait=W(ev1[("xfm", ss)], ev_mrep) if kc == 0 else ())
            ev = s.emit("vector", lambda v, kc=kc, b=b: v.scalar_tensor_tensor(
                xf_slice(xh, kc), xf_slice(xh, kc), m_ca1[:, b * KC + kc:b * KC + kc + 1],
                mrep[:, C:2 * C], OP.mult, OP.mult))
            ev1[("xhk", ss, kc)] = ev
        ev1[("xh", ss)] = ev

    def emit_S21(ss):
        b = ss % 2
        ev_h = None
        for kc in range(KC):
            ev_h = s.emit("scalar", lambda sc, kc=kc, b=b: sc.activation(
                xf_slice(hsq, kc), xf_slice(xh, kc), AF.Square,
                bias=m_sa[:, b * KC + kc:b * KC + kc + 1]),
                wait=W(ev1[("xhk", ss, kc)]))
        ev1[("hsq", ss)] = ev_h
        ev = None
        for kc in range(KC):
            ev = s.emit("tensor", lambda t, kc=kc: t.matmul(
                pb[3][0:1, 0:C], onesb[:], xf_slice(hsq, kc),
                start=(kc == 0), stop=(kc == KC - 1)),
                wait=W(ev_h, ev_bank[3]) if kc == 0 else ())
        ev = s.emit("scalar", lambda sc: sc.copy(srow[:], pb[3][0:1, 0:C]), wait=W(ev))
        ev = s.emit("scalar", lambda sc: sc.activation(srow[:], srow[:], AF.Ln,
                                                       bias=cvec[0:1, CEPS5:CEPS5 + 1]))
        ev = s.emit("scalar", lambda sc: sc.activation(srow[:], srow[:], AF.Exp, scale=-0.5))
        evm = s.emit("tensor", lambda t: t.matmul(pb[3][0:128, 0:C], onesf[0:1, :],
                                                  srow[0:1, :], start=True, stop=True),
                     wait=W(ev, ev1.get(("xn", ss - 1))))
        ev = s.emit("scalar", lambda sc: sc.copy(grep[:], pb[3][0:128, 0:C]),
                    wait=W(evm))
        ev_bank[3] = ev
        ev1[("grep", ss)] = ev

    def emit_XN(ss):
        b = ss % 2
        ck2 = ss // 2
        ids = []
        for kc in range(KC):
            ids.append(s.emit("scalar", lambda sc, kc=kc, b=b: sc.activation(
                xf_slice(xh, kc), xf_slice(xh, kc), AF.Identity,
                scale=gam[:, kc:kc + 1], bias=sagam[:, b * KC + kc:b * KC + kc + 1])))
        ev = None
        for kc in range(KC):
            ev = s.emit("vector", lambda v, kc=kc, b=b: v.tensor_tensor(
                xf_slice(xn[b], kc), xf_slice(xh, kc), grep[:], OP.mult),
                wait=W(ev1[("grep", ss)], ids[kc],
                       ev1.get(("PpairB", ck2 - 1, b)) if kc == 0 else None))
        ev1[("xn", ss)] = ev

    def chain_stream(ts, hs, wprev_key):
        b, ck = ts % 2, ts // 2
        p = ts % 2
        HW = 4 * C

        def sl(buf):
            return buf[:, hs * HW:(hs + 1) * HW]
        wprev = ev1.get(wprev_key) if wprev_key else None
        evP_ = ev1[("evP", ts)]
        h = {}
        steps = []

        def op(eng, fn, wk=(), name=None, free_tag=None):
            def th(eng=eng, fn=fn, wk=wk, name=name, free_tag=free_tag):
                ev = s.emit(eng, fn, wait=W(*[h[k] if isinstance(k, str) else k for k in wk]))
                if name:
                    h[name] = ev
                if free_tag:
                    ev1[(free_tag, ts, hs)] = ev
            steps.append(th)

        op("vector", lambda v: v.tensor_tensor(
            sl(qb[p]), sl(qb[p]), sl(sgb[p]), OP.mult),
            wk=(evP_, wprev), name="qg", free_tag="fsgb")
        op("vector", lambda v: v.tensor_tensor(
            sl(w11), sl(kbn[p]), sl(vb[p]), OP.mult),
            wk=(evP_, wprev), name="kv", free_tag="fkv")
        op("scalar", lambda sc: sc.activation(sl(w8), sl(arb[p]), AF.Square),
           wk=(evP_, wprev), name="e1")
        op("scalar", lambda sc: sc.activation(sl(w9), sl(aib[p]), AF.Square), name="e2")
        op("vector", lambda v: v.tensor_tensor(sl(w8), sl(w8), sl(w9), OP.add),
           wk=("e2",), name="e3")
        op("scalar", lambda sc: sc.activation(sl(w9), sl(w8), AF.Ln,
                                              bias=cvec[:, C18:C18 + 1]),
           wk=("e3",), name="e4")
        op("scalar", lambda sc: sc.activation(sl(w9), sl(w9), AF.Exp, scale=0.5),
           name="e5")
        op("vector", lambda v: v.tensor_tensor(sl(w10), sl(w9), sl(arb[p]), OP.add),
           wk=("e5",), name="e6a", free_tag="farb")
        op("scalar", lambda sc: sc.activation(sl(w10), sl(w10), AF.Relu),
           wk=("e6a",), name="e6b")
        op("vector", lambda v: v.tensor_tensor(sl(w8), sl(w9), sl(w10), OP.mult),
           wk=("e6b",), name="e7")
        op("scalar", lambda sc: sc.activation(sl(w8), sl(w8), AF.Ln, scale=2.0,
                                              bias=cvec[:, C18:C18 + 1]),
           wk=("e7",), name="e8")
        op("scalar", lambda sc: sc.activation(sl(w8), sl(w8), AF.Exp, scale=0.5),
           name="e9")
        op("vector", lambda v: v.tensor_tensor(sl(w8), sl(w8), sl(w10), OP.add),
           wk=("e9",), name="e10")
        op("scalar", lambda sc: sc.activation(sl(w8), sl(w8), AF.Ln,
                                              bias=cvec[:, C19:C19 + 1]),
           wk=("e10",), name="e11")
        op("scalar", lambda sc: sc.activation(sl(w8), sl(w8), AF.Exp, scale=-1.0),
           name="e12")
        op("vector", lambda v: v.tensor_tensor(sl(w10), sl(aib[p]), sl(w8), OP.mult),
           wk=("e12",), name="e13a", free_tag="faib")
        op("vector", lambda v: v.tensor_scalar(sl(w10), sl(w10), 1.0, -1.0,
                                               OP.min, OP.max), name="e13b")
        op("scalar", lambda sc: sc.activation(sl(w7), sl(w9), AF.Sigmoid),
           wk=(wprev,), name="e14")
        op("scalar", lambda sc: sc.activation(sl(w10), sl(w10), AF.Arctan),
           wk=("e13b",), name="e15")

        def scanstep(out, in0, in1, state, wname):
            def th(out=out, in0=in0, in1=in1, state=state, wname=wname):
                for kx in range(4):
                    kc = hs * 4 + kx
                    init = 0.0 if ck == 0 or FORCE_ZERO_STATE else state[b][:, kc:kc + 1]
                    s.emit("vector", lambda v, kc=kc, init=init, out=out, in0=in0,
                           in1=in1: v.tensor_tensor_scan(
                               xf_slice(out, kc), xf_slice(in0, kc),
                               xf_slice(in1, kc) if in1 is not zeroC else zeroC[:],
                               init, OP.mult if in1 is not zeroC else OP.add, OP.add),
                           wait=W(h[wname]) if kx == 0 else ())
            steps.append(th)

        def statecopy(state, srcbuf):
            def th(state=state, srcbuf=srcbuf):
                lc = srcbuf[:].rearrange("p (kc c) -> p kc c", c=C)[
                    :, hs * 4:(hs + 1) * 4, C - 1]
                s.emit("vector", lambda v, lc=lc: v.tensor_copy(
                    state[b][:, hs * 4:(hs + 1) * 4], lc))
            steps.append(th)

        scanstep(psi, w10, zeroC, st_psi, "e15")
        statecopy(st_psi, psi)
        op("vector", lambda v: v.tensor_scalar(
            sl(w9), sl(psi), 2.0 / PI, RMAGIC, OP.mult, OP.add), name="w8i")
        op("vector", lambda v: v.tensor_scalar(
            sl(w9), sl(w9), RMAGIC, -PI / 2.0, OP.subtract, OP.mult))
        op("vector", lambda v: v.tensor_tensor(
            sl(w8), sl(psi), sl(w9), OP.add), name="e20")
        op("scalar", lambda sc: sc.activation(sl(w9), sl(w8), AF.Sin, scale=2.0),
           wk=("e20",), name="e21")
        op("scalar", lambda sc: sc.activation(sl(w9), sl(w9), AF.Square), name="e23")
        op("scalar", lambda sc: sc.activation(sl(w10), sl(w9), AF.Identity,
                                              scale=-2.0, bias=cvec[:, CONE:CONE + 1]),
           name="e27")
        op("scalar", lambda sc: sc.activation(sl(w9), sl(w8), AF.Sin, scale=4.0),
           name="e28")
        op("vector", lambda v: v.tensor_tensor(sl(w8), sl(w10), sl(w11), OP.mult),
           wk=("e27", "e28"), name="e29")
        op("vector", lambda v: v.tensor_tensor(sl(w11), sl(w9), sl(w11), OP.mult),
           wk=("e28",), name="e30")
        scanstep(psi, w7, w8, st_zr, "e14")
        statecopy(st_zr, psi)
        scanstep(w8, w7, w11, st_zi, "e14")
        statecopy(st_zi, w8)
        op("vector", lambda v: v.tensor_tensor(sl(w10), sl(w10), sl(psi), OP.mult), name="m1")
        op("vector", lambda v: v.tensor_tensor(sl(w9), sl(w9), sl(w8), OP.mult), name="m2")
        op("vector", lambda v: v.tensor_tensor(sl(w10), sl(w10), sl(w9), OP.add), name="m3")

        def m4():
            ev = s.emit("vector", lambda v: v.tensor_tensor(
                sl(yb[p]), sl(w10), sl(qb[p]), OP.mult))
            ev1[("y", ts, hs)] = ev
        steps.append(m4)
        return steps

    def p1b_thunks(ck):
        g0 = attn_pg0[ck]
        bankrot = [4, 5, 6, 7]
        state = {"bi": 0, "last_evac": {}, "last_mm": {}}
        thunks = []
        desc = proj_desc()
        for pi in range(0, len(desc), 2):
            pair = [(gi, desc[gi]) for gi in (pi, pi + 1)]
            def th(pair=pair):
                bk = bankrot[state["bi"] % 4]
                state["bi"] += 1
                ev = None
                for half_i, (gi, (pas, b, kind, mc, hb)) in enumerate(pair):
                    g = g0 + gi
                    for kc in range(KC):
                        first = half_i == 0 and kc == 0
                        ev = s.emit("tensor", lambda t, g=g, kc=kc, bk=bk, b=b,
                                    half_i=half_i, first=first: t.matmul(
                                        pb[bk][0:128, half_i * C:(half_i + 1) * C],
                                        ring_ap(ring, g, kc * 128, 128),
                                        xn[b][:, kc * C:(kc + 1) * C],
                                        start=first,
                                        stop=(half_i == 1 and kc == KC - 1)),
                                    wait=W(ring_wait(g), ev1[("xn", 2 * ck + b)],
                                           ev_bank[bk] if first else None) if kc == 0 else ())
                    ring_release(g, ev)
                    state["last_mm"][b] = ev
                gi, (pas, b, kind, mc, hb) = pair[0]
                ts = 2 * ck + b
                tsp = ts - 2
                silu = False
                if kind == "k":
                    dst, mcc, fw = kbn[b], mc, ("fkv", tsp, hb)
                elif kind == "v":
                    dst, mcc, fw = vb[b], mc, ("fkv", tsp, hb)
                elif kind == "g":
                    dst, mcc, fw = sgb[b], mc, ("fsgb", tsp, hb)
                    silu = True
                elif kind == "ar":
                    dst, mcc, fw = arb[b], mc, ("farb", tsp, hb)
                elif kind == "ai":
                    dst, mcc, fw = aib[b], mc - 8, ("faib", tsp, hb)
                else:
                    dst, mcc, fw = qb[b], mc, ("y", tsp, hb)
                wfree = ev1.get(fw)
                assert wfree is not None or tsp < 0, fw
                if silu:
                    evv = s.emit("scalar", lambda sc, dst=dst, mcc=mcc, bk=bk: sc.activation(
                        xf_slice(dst, mcc, nt=2 * C), pb[bk][0:128, 0:2 * C], AF.Silu),
                        wait=W(ev, wfree))
                else:
                    evv = s.emit("scalar", lambda sc, dst=dst, mcc=mcc, bk=bk: sc.copy(
                        xf_slice(dst, mcc, nt=2 * C), pb[bk][0:128, 0:2 * C]),
                        wait=W(ev, wfree))
                ev_bank[bk] = evv
                state["last_evac"][b] = evv
            thunks.append(th)

        def fin():
            for b in range(2):
                ev1[("PpairB", ck, b)] = state["last_mm"][b]
                ev1[("evP", 2 * ck + b)] = state["last_evac"][b]
        return thunks, fin

    def stream_zip(sa, sb):
        out = []
        for i in range(max(len(sa), len(sb))):
            if i < len(sa):
                out.append(sa[i])
            if i < len(sb):
                out.append(sb[i])
        return out

    def run_interleaved(steps, thunks):
        n = len(steps)
        bounds = ([4] * 12 + [16] * 4 + [32] * 4 + [n] * 4) if n else [0] * 24
        feeds = {}
        pos = 0
        for j, th in enumerate(thunks):
            want = 4 + (j * max(n - 6, 1)) // max(len(thunks) - 1, 1)
            pos = min(max(pos, bounds[j] if j < len(bounds) else n, want), n)
            feeds.setdefault(pos, []).append(th)
        for i, stp in enumerate(steps):
            stp()
            for th in feeds.get(i, []):
                th()
        for th in feeds.get(n, []):
            th()

    def emit_W1pair(ck):
        g0 = attn_wg0[ck]
        evr = {}
        for oc in range(2):
            evmm = {}
            for kcp in range(4):
                g = g0 + oc * 4 + kcp
                for jj in range(2):
                    kc = kcp * 2 + jj
                    for b in range(2):
                        for t_ in range(TT):
                            bk = 4 + b * 2 + t_
                            wv = []
                            if jj == 0 and b == 0 and t_ == 0:
                                wv.append(ring_wait(g))
                            if kc == 0:
                                wv += [ev1[("y", 2 * ck + b, 0)], ev_bank[bk]]
                            if kc == 4:
                                wv.append(ev1[("y", 2 * ck + b, 1)])
                            evmm[(b, t_)] = s.emit(
                                "tensor", lambda t, g=g, kc=kc, jj=jj, t_=t_, bk=bk, b=b: t.matmul(
                                    pb[bk][0:128, 0:512],
                                    yb[b][:, kc * C + t_ * 128:kc * C + (t_ + 1) * 128],
                                    ring_ap(ring, g, jj * 512, 512),
                                    start=(kc == 0), stop=(kc == KC - 1)),
                                wait=W(*wv))
                            if jj == 1 and b == 1 and t_ == TT - 1:
                                ring_release(g, evmm[(b, t_)])
            for b in range(2):
                slot = (2 * ck + b) % 4
                for t_ in range(TT):
                    bk = 4 + b * 2 + t_
                    s.emit("vector", lambda v, bk=bk, b=b, oc=oc: v.tensor_tensor(
                        psi[:, 0:512], pb[bk][0:128, 0:512],
                        ga_rep[:, b * D + oc * 512:b * D + (oc + 1) * 512], OP.mult),
                        wait=W(evmm[(b, t_)], ga_ready))
                    evr[(b, t_)] = s.emit("vector", lambda v, slot=slot, t_=t_, oc=oc: v.tensor_tensor(
                        stage[slot][:, t_ * 1024 + oc * 512:t_ * 1024 + (oc + 1) * 512],
                        stage[slot][:, t_ * 1024 + oc * 512:t_ * 1024 + (oc + 1) * 512],
                        psi[:, 0:512], OP.add))
                    ev_bank[bk] = evr[(b, t_)]
        for b in range(2):
            ts = 2 * ck + b
            slot = ts % 4
            dstm = dxmid.ap()[b].rearrange("(ck tt p) d -> ck p tt d", p=128, tt=TT)[ck]
            ev1[("store", ts)] = s.emit("sync", lambda sy, slot=slot, dstm=dstm: sy.dma_start(
                out=dstm, in_=stage[slot][:]), wait=W(evr[(b, TT - 1)]), sig=s_sos[slot], amt=16)

    if build_stage >= 1:
        s.emit("sync", lambda sy: sy.dma_start(out=ga_rep[:], in_=dws_grep.ap()[:, 0:B * D]),
               wait=W(bar1, ev_grs), sig=s_gr, amt=16)
        ga_ready = (s_gr, s.cnt["s_gr"])
        for b_ in range(B):
            s.emit("vector", lambda v, b_=b_: v.tensor_tensor(
                sagam[:, b_ * KC:(b_ + 1) * KC], m_sa[:, b_ * KC:(b_ + 1) * KC],
                gam[:, 0:KC], OP.mult), wait=W(bar1, modv_done) if b_ == 0 else ())
        for b0 in range(2):
            emit_A1_tp(b0)
            emit_A1_stats(b0)
            emit_S21(b0)
            emit_XN(b0)
        for it in range(NCH + 1):
            th96, finb = p1b_thunks(it) if it < NCH else ([], None)
            for pas in (0, 1):
                if it >= 1:
                    if pas == 0:
                        sa = chain_stream(2 * it - 2, 0, ("y", 2 * it - 3, 0))
                        sb = chain_stream(2 * it - 1, 1, ("y", 2 * it - 4, 1))
                    else:
                        sa = chain_stream(2 * it - 2, 1, ("y", 2 * it - 1, 1))
                        sb = chain_stream(2 * it - 1, 0, ("y", 2 * it - 2, 0))
                    steps = stream_zip(sa, sb)
                else:
                    steps = []
                run_interleaved(steps, th96[pas * 24:(pas + 1) * 24])
            if finb is not None:
                finb()
            if it >= 1:
                emit_W1pair(it - 1)
            if it + 1 < NCH:
                for ss2 in (2 * it + 2, 2 * it + 3):
                    emit_A1_tp(ss2)
                    emit_A1_stats(ss2)
                    emit_S21(ss2)
                    emit_XN(ss2)

    s.emit("gpsimd", lambda e: None, sig=s_bar)

    ffn_m1g0 = {}
    ffn_m2g0 = {}

    def queue_mm1_groups(ss, extra=()):
        ffn_m1g0[ss] = ring_state["g"]
        for hc in range(HC):
            srcw = dws_w1t.ap()[:, (hc * KC) * 128:(hc * KC + 8) * 128]
            ring_load(ringf, [(srcw, 0, 1024)], wait_extra=extra)
            extra = ()

    def queue_mm2_groups(ts):
        ffn_m2g0[ts] = ring_state["g"]
        for oc in range(2):
            for kcp in range(16):
                reqs = []
                for jj in range(2):
                    kc = kcp * 2 + jj
                    reqs.append((dws_w2t.ap()[kc * 128:(kc + 1) * 128,
                                              oc * 512:(oc + 1) * 512], jj * 512, 512))
                ring_load(ringf, reqs)

    if build_stage >= 2:
        queue_mm1_groups(0, extra=tt_done)
        for ss in range(1, NST):
            queue_mm1_groups(ss)
            queue_mm2_groups(ss - 1)
        queue_mm2_groups(NST - 1)

    p1.close()
    for eng in ENGINES:
        if eng != "gpsimd":
            s.emit(eng, lambda e: None, sig=s_bar)
    bar2 = (s_bar, s.cnt["s_bar"])

    p2 = contextlib.ExitStack()
    P2 = p2.enter_context
    HHALF = HC // 2
    x2fm = P2(nc.sbuf_tensor("x2fm", [128, KC * C], F32))
    xh2 = P2(nc.sbuf_tensor("xh2", [128, KC * C], F32))
    w2d = P2(nc.sbuf_tensor("w2d", [128, 4 * C], F32))
    m2scr = P2(nc.sbuf_tensor("m2scr", [128, 512], F32))
    hsq1 = P2(nc.sbuf_tensor("hsq1", [128, KC * C], BF16))
    xq1 = [P2(nc.sbuf_tensor(f"xq1_{i}", [128, KC * C], BF16)) for i in range(2)]
    Rb = P2(nc.sbuf_tensor("Rb", [128, HC * C], F32))
    B2 = P2(nc.sbuf_tensor("B2", [128, HC * C], F32))
    h2b = P2(nc.sbuf_tensor("h2b", [128, HC * C], BF16))
    xq2 = P2(nc.sbuf_tensor("xq2", [128, HC * C], BF16))
    gb_rep = P2(nc.sbuf_tensor("gb_rep", [128, B * D], BF16))
    mrow2 = P2(nc.sbuf_tensor("mrow2", [1, 2 * C], F32))
    srow2 = P2(nc.sbuf_tensor("srow2", [1, C], F32))
    arow = P2(nc.sbuf_tensor("arow", [1, C], F32))
    growA = P2(nc.sbuf_tensor("growA", [1, C], F32))
    growB = P2(nc.sbuf_tensor("growB", [1, C], F32))
    hrow = P2(nc.sbuf_tensor("hrow", [1, C], F32))
    hrowB = P2(nc.sbuf_tensor("hrowB", [1, C], F32))
    mrep2 = P2(nc.sbuf_tensor("mrep2", [128, 2 * C], F32))
    g1rep = P2(nc.sbuf_tensor("g1rep", [128, 4 * C], F32))
    g2rep = P2(nc.sbuf_tensor("g2rep", [128, 4 * C], F32))
    a2rep = [P2(nc.sbuf_tensor(f"a2rep{i}", [128, 2 * C], F32)) for i in range(2)]
    mvst2 = P2(nc.sbuf_tensor("mvst2", [128, 4], F32))
    varr2 = P2(nc.sbuf_tensor("varr2", [128, 2], F32))
    msq22 = P2(nc.sbuf_tensor("msq22", [128, 2], F32))
    mx2 = P2(nc.sbuf_tensor("mx2", [128, 2], F32))
    sqmx = P2(nc.sbuf_tensor("sqmx", [128, 2], F32))
    f2tp = P2(nc.sbuf_tensor("f2tp", [128, 2], F32))
    f2tok = P2(nc.sbuf_tensor("f2tok", [128, 2], F32))
    abn2 = P2(nc.sbuf_tensor("abn2", [128, HC], F32))
    sbn1 = P2(nc.sbuf_tensor("sbn1", [128, B * KC], F32))

    def emit_xl2(ss):
        slot = ss % 4
        lastp1 = {0: NST - 4, 1: NST - 3, 2: NST - 2, 3: NST - 1}
        wv = ([ev1[("store", ss)], ev1.get(("store", lastp1[ss]))] if ss < 4
              else [ev1.get(("ostore", ss - 4)), ev1[("store", ss)]])
        b, ck = ss % 2, ss // 2
        src = dxmid.ap()[b].rearrange("(ck tt p) d -> ck p tt d", p=128, tt=TT)[ck]
        ev1[("xl2", ss)] = s.emit("sync", lambda sy, slot=slot, src=src: sy.dma_start(
            out=stage[slot][:], in_=src), wait=W(*wv), sig=s_xls[slot], amt=16)

    def emit_A2_tp(ss):
        slot = ss % 4
        last = None
        for tt in range(TT):
            for kcg in range(2):
                bk = kcg
                ev = None
                for i2 in range(4):
                    kc = kcg * 4 + i2
                    ev = s.emit("tensor", lambda t, slot=slot, tt=tt, kc=kc, i2=i2, bk=bk: t.matmul(
                        pb[bk][0:128, i2 * 128:(i2 + 1) * 128],
                        stage[slot][:, tt * 1024 + kc * 128:tt * 1024 + (kc + 1) * 128],
                        identf[:], start=(i2 == 0), stop=(i2 == 3), is_transpose=True),
                        wait=W(ev1[("xl2", ss)], ev_bank[bk],
                               ev1.get(("xh2", ss - 1)), bar2 if ss == 0 else None) if i2 == 0 else ())

                def evacx(sc, tt=tt, kcg=kcg, bk=bk):
                    dst = x2fm[:].rearrange("p (kc c) -> p kc c", c=C)[
                        :, kcg * 4:(kcg + 1) * 4, tt * 128:(tt + 1) * 128]
                    return sc.copy(dst, pb[bk][0:128, 0:512])
                last = ev_bank[bk] = s.emit("scalar", evacx, wait=W(ev))
        ev1[("xfm2", ss)] = last

    def emit_stats_a_v(ss):
        slot = ss % 4
        ev = None
        for tt in range(TT):
            s.emit("vector", lambda v, slot=slot, tt=tt: v.tensor_scalar(
                hsq1[:].bitcast(F32)[:, 0:1024], stage[slot][:, tt * 1024:(tt + 1) * 1024],
                1.0 / D, None, OP.mult, OP.add, accum_out=mvst2[:, tt:tt + 1]),
                wait=W(ev1[("xl2", ss)], ev1.get(("mrow2", ss - 1)),
                       ev1.get(("sqabs", ss - 1))) if tt == 0 else ())
            ev = s.emit("vector", lambda v, slot=slot, tt=tt: v.scalar_tensor_tensor(
                hsq1[:].bitcast(F32)[:, 0:1024], stage[slot][:, tt * 1024:(tt + 1) * 1024], 1.0,
                stage[slot][:, tt * 1024:(tt + 1) * 1024], OP.mult, OP.mult,
                accum_out=mvst2[:, 2 + tt:3 + tt]))
        ev = s.emit("vector", lambda v: v.tensor_scalar(
            varr2[:], mvst2[:, 2:4], 1.0 / D, None, OP.mult), wait=W(ev))
        ev = s.emit("vector", lambda v: v.tensor_tensor(
            msq22[:], mvst2[:, 0:2], mvst2[:, 0:2], OP.mult))
        ev = s.emit("vector", lambda v: v.tensor_tensor(varr2[:], varr2[:], msq22[:], OP.subtract))
        ev1[("acc", ss)] = ev
        ev = s.emit("scalar", lambda sc: sc.activation(varr2[:], varr2[:], AF.Ln,
                                                       bias=cvec[:, CEPS5:CEPS5 + 1]), wait=W(ev))
        ev1[("rstd", ss)] = s.emit("scalar", lambda sc: sc.activation(
            varr2[:], varr2[:], AF.Exp, scale=-0.5))

    def emit_stats_a_pe(ss):
        ev = None
        for i2, (buf_, col) in enumerate([(mvst2, 0), (mvst2, 1), (varr2, 0), (varr2, 1)]):
            ev = s.emit("tensor", lambda t, buf_=buf_, col=col, i2=i2: t.matmul(
                pb[3][0:1, i2 * 128:(i2 + 1) * 128], buf_[:, col:col + 1], identf[:],
                start=(i2 == 0), stop=(i2 == 3), is_transpose=True),
                wait=W(ev1[("rstd", ss)], ev_bank[3]) if i2 == 0 else ())
        ev_mrow = s.emit("scalar", lambda sc: sc.copy(mrow2[:], pb[3][0:1, 0:512]), wait=W(ev))
        ev_bank[3] = ev_mrow
        ev1[("mrow2", ss)] = ev_mrow
        ev = s.emit("tensor", lambda t: t.matmul(pb[2][0:128, 0:C], onesf[0:1, :],
                                                 mrow2[0:1, 0:C], start=True, stop=False),
                    wait=W(ev_mrow, ev_bank[2], ev1.get(("xh2", ss - 1))))
        ev = s.emit("tensor", lambda t: t.matmul(pb[2][0:128, C:2 * C], onesf[0:1, :],
                                                 mrow2[0:1, C:2 * C], start=False, stop=True))
        ev_mrep = s.emit("scalar", lambda sc: sc.copy(mrep2[:], pb[2][0:128, 0:2 * C]), wait=W(ev))
        ev_bank[2] = ev_mrep
        ev1[("mrep", ss)] = ev_mrep

    def emit_stats_b(ss):
        b = ss % 2
        ev = None
        for kc in range(KC):
            s.emit("vector", lambda v, kc=kc: v.tensor_tensor(
                xf_slice(xh2, kc), xf_slice(x2fm, kc), mrep2[:, 0:C], OP.subtract),
                wait=W(ev1[("xfm2", ss)], ev1[("mrep", ss)],
                       ev1.get(("xq1", ss - 1))) if kc == 0 else ())
            ev = s.emit("vector", lambda v, kc=kc, b=b: v.scalar_tensor_tensor(
                xf_slice(xh2, kc), xf_slice(xh2, kc), m_cb1[:, b * KC + kc:b * KC + kc + 1],
                mrep2[:, C:2 * C], OP.mult, OP.mult))
            ev1[("xh2k", ss, kc)] = ev
        ev1[("xh2", ss)] = ev

    def emit_SQ_a_s(ss):
        b = ss % 2
        for kc in range(KC):
            ev1[("hsqk", ss, kc)] = s.emit("scalar", lambda sc, kc=kc, b=b: sc.activation(
                xf_slice(hsq1, kc), xf_slice(xh2, kc), AF.Square,
                bias=m_sb[:, b * KC + kc:b * KC + kc + 1]),
                wait=W(ev1[("xh2k", ss, kc)], ev1[("acc", ss)] if kc == 0 else None))
        ev = None
        for kc in range(KC):
            ev = s.emit("scalar", lambda sc, kc=kc, b=b: sc.activation(
                xf_slice(xh2, kc), xf_slice(xh2, kc), AF.Identity,
                scale=n1v[:, kc:kc + 1], bias=sbn1[:, b * KC + kc:b * KC + kc + 1]))
        ev1[("xn1", ss)] = ev

    def emit_SQ_row(ss):
        ev = s.emit("scalar", lambda sc: sc.copy(srow2[:], pb[3][0:1, 0:C]),
                    wait=[lambda ss=ss: ev1[("ssum1", ss)]])
        ev1[("srowev", ss)] = ev
        ev = s.emit("scalar", lambda sc: sc.activation(srow2[:], srow2[:], AF.Ln, scale=1.0 / D,
                                                       bias=cvec[0:1, CEPS8:CEPS8 + 1]))
        ev1[("r1", ss)] = s.emit("scalar", lambda sc: sc.activation(
            srow2[:], srow2[:], AF.Exp, scale=-0.5))
        ev1[("sqabs", ss)] = s.emit("scalar", lambda sc: sc.activation(hsq1[:], xh2[:], AF.Abs),
                                    wait=[lambda ss=ss: ev1[("ssum1", ss)]] + W(ev1[("xn1", ss)]))

    def emit_SQ_sums(ss):
        ev = None
        for kc in range(KC):
            ev = s.emit("tensor", lambda t, kc=kc: t.matmul(
                pb[3][0:1, 0:C], onesb[:], xf_slice(hsq1, kc),
                start=(kc == 0), stop=(kc == KC - 1)),
                wait=W(ev1[("hsqk", ss, kc)], ev_bank[3] if kc == 0 else None))
        ev1[("ssum1", ss)] = ev
        ev_bank[3] = ev1[("srowev", ss)]

    def emit_SQ_b(ss):
        ev = None
        for sz in [4, 2, 1]:
            ev = s.emit("vector", lambda v, sz=sz: v.tensor_tensor(
                hsq1[:, 0:sz * C], hsq1[:, 0:sz * C], hsq1[:, sz * C:2 * sz * C], OP.max),
                wait=W(ev1[("sqabs", ss)]) if sz == 4 else ())
        ev_tr = s.emit("vector", lambda v: v.tensor_copy(w2d[:, 0:C], hsq1[:, 0:C]),
                       wait=W(ev, ev1.get(("p2tp", ss - 2))))
        evm = None
        for tt in range(TT):
            ev = s.emit("tensor", lambda t, tt=tt: t.matmul(
                pb[0][0:128, 0:128], w2d[:, tt * 128:(tt + 1) * 128], identf[:],
                start=True, stop=True, is_transpose=True),
                wait=W(ev_tr, ev_bank[0]))
            ev = s.emit("scalar", lambda sc, tt=tt: sc.copy(
                w2d[:, C + tt * 128:C + (tt + 1) * 128], pb[0][0:128, 0:128]), wait=W(ev))
            ev_bank[0] = ev
            evm = s.emit("vector", lambda v, tt=tt: v.tensor_reduce(
                sqmx[:, tt:tt + 1], w2d[:, C + tt * 128:C + (tt + 1) * 128], AX.X, OP.max),
                wait=W(ev))
        ev1[("sqtp", ss)] = ev_bank[0]
        evc = s.emit("vector", lambda v: v.tensor_scalar(
            sqmx[:], sqmx[:], 1e-5, None, OP.max), wait=W(evm))
        ev = None
        for tt in range(TT):
            ev = s.emit("tensor", lambda t, tt=tt: t.matmul(
                pb[1][0:1, tt * 128:(tt + 1) * 128], sqmx[:, tt:tt + 1], identf[:],
                start=(tt == 0), stop=(tt == TT - 1), is_transpose=True),
                wait=W(evc, ev_bank[1]) if tt == 0 else ())
        ev = s.emit("scalar", lambda sc: sc.copy(growA[:], pb[1][0:1, 0:C]), wait=W(ev))
        ev_bank[1] = ev
        ev = s.emit("vector", lambda v: v.tensor_tensor(growA[:], growA[:], srow2[:], OP.mult),
                    wait=W(ev, ev1[("r1", ss)]))
        ev = s.emit("vector", lambda v: v.tensor_scalar(growA[:], growA[:], 1e-5, None, OP.max))
        ev = s.emit("scalar", lambda sc: sc.activation(growB[:], growA[:], AF.Ln), wait=W(ev))
        ev_s1 = s.emit("scalar", lambda sc: sc.activation(growB[:], growB[:], AF.Exp, scale=-1.0,
                                                          bias=cvec[0:1, CL127:CL127 + 1]))
        ev_g1 = s.emit("vector", lambda v: v.tensor_tensor(growB[:], growB[:], srow2[:], OP.mult),
                       wait=W(ev_s1))
        ev_a = s.emit("vector", lambda v: v.tensor_scalar(
            arow[:], growA[:], w1sr[:], 1.0 / 127.0, OP.mult, OP.mult), wait=W(ev))
        ev_a = s.emit("scalar", lambda sc: sc.activation(arow[:], arow[:], AF.Square),
                      wait=W(ev_a))
        ev_a = s.emit("vector", lambda v: v.tensor_scalar(
            arow[:], arow[:], ascr[:], None, OP.mult), wait=W(ev_a))
        ev = s.emit("tensor", lambda t: t.matmul(pb[3][0:128, 0:C], onesf[0:1, :], growB[0:1, :],
                                                 start=True, stop=True),
                    wait=W(ev_g1, ev_bank[3], ev1.get(("q1", ss - 1))))
        evg = s.emit("scalar", lambda sc: sc.copy(g1rep[:, 0:C], pb[3][0:128, 0:C]), wait=W(ev))
        ev_bank[3] = evg
        evg = s.emit("vector", lambda v: v.tensor_copy(g1rep[:, C:2 * C], g1rep[:, 0:C]),
                     wait=W(evg))
        evg = s.emit("vector", lambda v: v.tensor_copy(g1rep[:, 2 * C:4 * C], g1rep[:, 0:2 * C]))
        ev1[("g1rep", ss)] = evg
        ev = s.emit("tensor", lambda t: t.matmul(pb[3][0:128, 0:C], onesf[0:1, :], arow[0:1, :],
                                                 start=True, stop=True),
                    wait=W(ev_a, ev_bank[3], ev1.get(("p1a", ss - 2))))
        eva = s.emit("scalar", lambda sc, ss=ss: sc.copy(a2rep[ss % 2][:, 0:C], pb[3][0:128, 0:C]),
                     wait=W(ev))
        ev_bank[3] = eva
        eva = s.emit("vector", lambda v, ss=ss: v.tensor_copy(
            a2rep[ss % 2][:, C:2 * C], a2rep[ss % 2][:, 0:C]), wait=W(eva))
        ev1[("a2rep", ss)] = eva

    def emit_Q1(ss):
        p = ss % 2
        ev = None
        for j in range(2):
            ev = s.emit("vector", lambda v, j=j: v.tensor_tensor(
                xh2[:, j * 4 * C:(j + 1) * 4 * C], xh2[:, j * 4 * C:(j + 1) * 4 * C],
                g1rep[:], OP.mult),
                wait=W(ev1[("g1rep", ss)], ev1[("xn1", ss)],
                       ev1.get(("M1done", ss - 2))) if j == 0 else ())
        ev = s.emit("vector", lambda v, p=p: v.tensor_scalar(
            xq1[p][:], xh2[:], RMAGIC, RMAGIC, OP.add, OP.subtract), wait=W(ev))
        ev1[("xq1", ss)] = ev
        ev1[("q1", ss)] = ev

    def emit_M1(ss, half, q=None):
        p = ss % 2
        g0 = ffn_m1g0[ss]
        ev_evac = None
        evlast = None
        hcxr = range(0, HHALF, 2) if q is None else range(q * 8, q * 8 + 8, 2)
        for hcx in hcxr:
            hc0 = half * HHALF + hcx
            bk = 4 + ((hcx // 2 + 2) % 4)
            ev = None
            for hi in range(2):
                hc = hc0 + hi
                g = g0 + hc
                for kc in range(KC):
                    first = hi == 0 and kc == 0
                    ev = s.emit("tensor", lambda t, g=g, kc=kc, bk=bk, p=p, hi=hi,
                                first=first: t.matmul(
                                    pb[bk][0:128, hi * C:(hi + 1) * C],
                                    ring_ap(ringf, g, kc * 128, 128),
                                    xq1[p][:, kc * C:(kc + 1) * C],
                                    start=first, stop=(hi == 1 and kc == KC - 1)),
                        wait=W(ring_wait(g), ev1[("xq1", ss)],
                               ev_bank[bk] if first else None) if kc == 0 else ())
                ring_release(g, ev)
            evlast = ev
            ev_evac = ev_bank[bk] = s.emit("scalar", lambda sc, hc0=hc0, bk=bk: sc.activation(
                Rb[:, hc0 * C:(hc0 + 2) * C], pb[bk][0:128, 0:2 * C], AF.Relu),
                wait=W(ev, ev1.get(("Rfree", ss - 1, half))))
        if q is None or q == 1:
            ev1[("Rh", ss, half)] = ev_evac
        if half == 1 and q == 1:
            ev1[("M1done", ss)] = evlast

    def hf_slice(buf, hc):
        return buf[:, hc * C:(hc + 1) * C]

    def emit_p1(ts):
        for half in range(2):
            h0 = half * HHALF
            ev = s.emit("vector", lambda v, h0=h0: v.tensor_tensor(
                Rb[:, h0 * C:(h0 + HHALF) * C], Rb[:, h0 * C:(h0 + HHALF) * C],
                Rb[:, h0 * C:(h0 + HHALF) * C], OP.mult),
                wait=W(ev1[("Rh", ts, half)]))
            for q in range(8):
                ev = s.emit("vector", lambda v, h0=h0, q=q, ts=ts: v.tensor_tensor(
                    Rb[:, (h0 + 2 * q) * C:(h0 + 2 * q + 2) * C],
                    Rb[:, (h0 + 2 * q) * C:(h0 + 2 * q + 2) * C],
                    a2rep[ts % 2][:], OP.mult),
                    wait=W(ev1[("a2rep", ts)]) if q == 0 else ())
            ev1[("p1a", ts, half)] = ev
            evw = None
            for hcx in range(HHALF):
                hc = h0 + hcx
                evw = s.emit("scalar", lambda sc, hc=hc: sc.activation(
                    hf_slice(B2, hc), hf_slice(Rb, hc), AF.Identity,
                    scale=n2v[:, hc:hc + 1], bias=abn2[:, hc:hc + 1]),
                    wait=W(ev, ev1.get(("xq2", ts - 1))) if hcx == 0 else ())
            evsq = s.emit("scalar", lambda sc, h0=h0: sc.activation(
                h2b[:, h0 * C:(h0 + HHALF) * C], Rb[:, h0 * C:(h0 + HHALF) * C],
                AF.Square, bias=abi[:]),
                wait=W(evw, ev1.get(("treedone", ts - 1))))
            ev1[("Rfree", ts, half)] = evsq
            ev1[("h2b", ts, half)] = evsq
        ev1[("p1a", ts)] = ev1[("p1a", ts, 1)]

    def emit_p2row_sums(ts):
        ev = None
        for hc in range(HC):
            ev = s.emit("tensor", lambda t, hc=hc: t.matmul(
                pb[3][0:1, 0:C], onesb[:], hf_slice(h2b, hc),
                start=(hc == 0), stop=(hc == HC - 1)),
                wait=W(ev1[("h2b", ts, 0)], ev1[("h2b", ts, 1)],
                       ev_bank[3]) if hc == 0 else ())
        ev1[("ssum2", ts)] = ev
        ev = s.emit("scalar", lambda sc: sc.copy(hrow[:], pb[3][0:1, 0:C]), wait=W(ev))
        ev_bank[3] = ev
        ev = s.emit("scalar", lambda sc: sc.activation(hrow[:], hrow[:], AF.Ln, scale=1.0 / H,
                                                       bias=cvec[0:1, CEPS8:CEPS8 + 1]))
        ev1[("r2", ts)] = s.emit("scalar", lambda sc: sc.activation(
            hrow[:], hrow[:], AF.Exp, scale=-0.5))

    def emit_p2row_abs(ts):
        ev = s.emit("scalar", lambda sc: sc.activation(h2b[:], B2[:], AF.Abs),
                    wait=W(ev1[("ssum2", ts)]))
        ev1[("absdone", ts)] = ev
        for sz in [16, 8, 4, 2, 1]:
            ev = s.emit("vector", lambda v, sz=sz: v.tensor_tensor(
                h2b[:, 0:sz * C], h2b[:, 0:sz * C], h2b[:, sz * C:2 * sz * C], OP.max),
                wait=W(ev) if sz == 16 else ())
        ev = s.emit("vector", lambda v: v.tensor_copy(w2d[:, 0:C], h2b[:, 0:C]),
                    wait=W(ev, ev1.get(("sqtp", ts + 1))))
        ev1[("treedone", ts)] = ev

    def emit_p2row_rows(ts):
        evm = None
        for tt in range(TT):
            ev2 = s.emit("tensor", lambda t, tt=tt: t.matmul(
                pb[0][0:128, 0:128], w2d[:, tt * 128:(tt + 1) * 128], identf[:],
                start=True, stop=True, is_transpose=True),
                wait=W(ev1[("treedone", ts)], ev_bank[0]))
            ev2 = s.emit("scalar", lambda sc, tt=tt: sc.copy(
                w2d[:, C + tt * 128:C + (tt + 1) * 128], pb[0][0:128, 0:128]), wait=W(ev2))
            ev_bank[0] = ev2
            evm = s.emit("vector", lambda v, tt=tt: v.tensor_reduce(
                mx2[:, tt:tt + 1], w2d[:, C + tt * 128:C + (tt + 1) * 128], AX.X, OP.max),
                wait=W(ev2))
        ev1[("p2tp", ts)] = ev_bank[0]
        evc = s.emit("vector", lambda v: v.tensor_scalar(mx2[:], mx2[:], 0.0, None, OP.max),
                     wait=W(evm))
        ev1[("mx2", ts)] = evc
        ev = None
        for tt in range(TT):
            ev = s.emit("tensor", lambda t, tt=tt: t.matmul(
                pb[1][0:1, tt * 128:(tt + 1) * 128], mx2[:, tt:tt + 1], identf[:],
                start=(tt == 0), stop=(tt == TT - 1), is_transpose=True),
                wait=W(evc, ev_bank[1]) if tt == 0 else ())
        ev = s.emit("scalar", lambda sc: sc.copy(hrowB[:], pb[1][0:1, 0:C]), wait=W(ev))
        ev_bank[1] = ev
        ev = s.emit("vector", lambda v: v.tensor_tensor(hrowB[:], hrowB[:], hrow[:], OP.mult),
                    wait=W(ev, ev1[("r2", ts)]))
        ev = s.emit("vector", lambda v: v.tensor_scalar(hrowB[:], hrowB[:], 1e-5, None, OP.max))
        ev = s.emit("scalar", lambda sc: sc.activation(hrowB[:], hrowB[:], AF.Ln), wait=W(ev))
        ev = s.emit("scalar", lambda sc: sc.activation(hrowB[:], hrowB[:], AF.Exp, scale=-1.0,
                                                       bias=cvec[0:1, CL127:CL127 + 1]))
        ev_g2 = s.emit("vector", lambda v: v.tensor_tensor(hrowB[:], hrowB[:], hrow[:], OP.mult),
                       wait=W(ev))
        ev = s.emit("tensor", lambda t: t.matmul(pb[3][0:128, 0:C], onesf[0:1, :], hrowB[0:1, :],
                                                 start=True, stop=True),
                    wait=W(ev_g2, ev_bank[3], ev1.get(("xq2", ts - 1))))
        evg = s.emit("scalar", lambda sc: sc.copy(g2rep[:, 0:C], pb[3][0:128, 0:C]), wait=W(ev))
        ev_bank[3] = evg
        evg = s.emit("vector", lambda v: v.tensor_copy(g2rep[:, C:2 * C], g2rep[:, 0:C]),
                     wait=W(evg))
        evg = s.emit("vector", lambda v: v.tensor_copy(g2rep[:, 2 * C:4 * C], g2rep[:, 0:2 * C]))
        ev1[("g2rep", ts)] = evg
        evt = None
        for tt in range(TT):
            evt = s.emit("tensor", lambda t, tt=tt: t.matmul(
                pb[1][0:128, tt:tt + 1], hrow[0:1, tt * 128:(tt + 1) * 128],
                onesf[0:1, 0:1], start=(tt == 0), stop=(tt == TT - 1), is_transpose=True),
                wait=W(ev1[("r2", ts)], ev_bank[1]) if tt == 0 else ())
        evt = s.emit("scalar", lambda sc: sc.copy(f2tp[:, 0:TT], pb[1][0:128, 0:TT]),
                     wait=W(evt))
        ev_bank[1] = evt
        evf = s.emit("vector", lambda v: v.tensor_tensor(
            f2tok[:], mx2[:], f2tp[:, 0:TT], OP.mult), wait=W(evt, ev1[("mx2", ts)]))
        evf = s.emit("vector", lambda v: v.tensor_scalar(
            f2tok[:], f2tok[:], 1e-5, None, OP.max))
        evf = s.emit("vector", lambda v: v.tensor_scalar(
            f2tok[:], f2tok[:], 1.0 / 127.0, w2sv[:], OP.mult, OP.mult))
        ev1[("f2", ts)] = evf

    def emit_quant2(ts):
        ev = None
        for j in range(8):
            ev = s.emit("vector", lambda v, j=j: v.tensor_tensor(
                B2[:, j * 4 * C:(j + 1) * 4 * C], B2[:, j * 4 * C:(j + 1) * 4 * C],
                g2rep[:], OP.mult),
                wait=W(ev1[("g2rep", ts)], ev1[("absdone", ts)],
                       ev1.get(("M2done", ts - 1))) if j == 0 else ())
        ev = s.emit("vector", lambda v: v.tensor_scalar(
            xq2[:], B2[:], RMAGIC, RMAGIC, OP.add, OP.subtract), wait=W(ev))
        ev1[("xq2", ts)] = ev

    def emit_M2(ts, oc, part=0):
        b, ck = ts % 2, ts // 2
        g0 = ffn_m2g0[ts]
        slot = ts % 4
        if oc == 0:
            kr = range(0, 4) if part == 0 else range(4, 16)
        else:
            kr = range(0, 8) if part == 0 else range(8, 16)
        bks = (6, 7) if oc == 0 else (4, 5)
        for kcp in kr:
            g = g0 + oc * 16 + kcp
            for jj in range(2):
                kc = kcp * 2 + jj
                for t_ in range(TT):
                    bk = bks[t_]
                    wv = []
                    if jj == 0 and t_ == 0:
                        wv.append(ring_wait(g))
                    if kc == 0:
                        wv += [ev1[("xq2", ts)], ev_bank[bk]]
                    ev1[("m2mm", ts, oc, t_)] = s.emit(
                        "tensor", lambda t, g=g, kc=kc, jj=jj, t_=t_, bk=bk: t.matmul(
                            pb[bk][0:128, 0:512],
                            xq2[:, kc * C + t_ * 128:kc * C + (t_ + 1) * 128],
                            ring_ap(ringf, g, jj * 512, 512),
                            start=(kc == 0), stop=(kc == HC - 1)),
                        wait=W(*wv))
                    if jj == 1 and t_ == TT - 1:
                        ring_release(g, ev1[("m2mm", ts, oc, t_)])
        if part == 1:
            evr = ev1.get(("m2evr", ts))
            for t_ in range(TT):
                bk = bks[t_]
                ev = s.emit("vector", lambda v, bk=bk, b=b, oc=oc, t_=t_: v.scalar_tensor_tensor(
                    m2scr[:], pb[bk][0:128, 0:512], f2tok[:, t_:t_ + 1],
                    gb_rep[:, b * D + oc * 512:b * D + (oc + 1) * 512], OP.mult, OP.mult),
                    wait=W(ev1[("m2mm", ts, oc, t_)], ev1[("f2", ts)], gb_ready))
                evr = s.emit("vector", lambda v, slot=slot, t_=t_, oc=oc: v.tensor_tensor(
                    stage[slot][:, t_ * 1024 + oc * 512:t_ * 1024 + (oc + 1) * 512],
                    stage[slot][:, t_ * 1024 + oc * 512:t_ * 1024 + (oc + 1) * 512],
                    m2scr[:], OP.add))
                ev_bank[bk] = evr
            ev1[("m2evr", ts)] = evr
        if oc == 1 and part == 1:
            ev1[("M2done", ts)] = ev1[("m2mm", ts, 1, TT - 1)]
            dsto = dout.ap()[b].rearrange("(ck tt p) d -> ck p tt d", p=128, tt=TT)[ck]
            ev1[("ostore", ts)] = s.emit("sync", lambda sy, slot=slot, dsto=dsto: sy.dma_start(
                out=dsto, in_=stage[slot][:]), wait=W(ev1[("m2evr", ts)]),
                sig=s_sos[slot], amt=16)

    if build_stage >= 2:
        s.emit("sync", lambda sy: sy.dma_start(out=gb_rep[:], in_=dws_grep.ap()[:, B * D:]),
               wait=W(bar2, ev_grs), sig=s_gr, amt=16)
        gb_ready = (s_gr, s.cnt["s_gr"])
        s.emit("vector", lambda v: v.tensor_scalar(
            abn2[:], n2v[:], abi[:], None, OP.mult), wait=W(bar2))
        for b_ in range(B):
            s.emit("vector", lambda v, b_=b_: v.tensor_tensor(
                sbn1[:, b_ * KC:(b_ + 1) * KC], m_sb[:, b_ * KC:(b_ + 1) * KC],
                n1v[:, 0:KC], OP.mult))
        emit_xl2(0)
        emit_xl2(1)
        emit_A2_tp(0)
        emit_stats_a_v(0)
        emit_stats_a_pe(0)
        emit_stats_b(0)
        emit_SQ_a_s(0)
        emit_SQ_row(0)
        emit_SQ_sums(0)
        emit_SQ_b(0)
        emit_Q1(0)
        for it in range(NST + 1):
            nss = it + 1
            if nss + 1 < NST:
                emit_xl2(nss + 1)
            if it >= 1:
                emit_p2row_sums(it - 1)
                emit_p2row_abs(it - 1)
            if it < NST:
                emit_M1(it, 0)
            if it >= 1:
                emit_p2row_rows(it - 1)
            if nss < NST:
                emit_stats_a_v(nss)
            if it >= 1:
                emit_quant2(it - 1)
            if nss < NST:
                emit_A2_tp(nss)
            if it < NST:
                emit_M1(it, 1, 0)
            if nss < NST:
                emit_stats_a_pe(nss)
            if it < NST:
                emit_M1(it, 1, 1)
            if nss < NST:
                emit_stats_b(nss)
                emit_SQ_a_s(nss)
                emit_SQ_row(nss)
            if it < NST:
                emit_p1(it)
            if it >= 1:
                emit_M2(it - 1, 0, 0)
            if nss < NST:
                emit_SQ_sums(nss)
            if it >= 1:
                emit_M2(it - 1, 0, 1)
                emit_M2(it - 1, 1, 0)
            if nss < NST:
                emit_SQ_b(nss)
                emit_Q1(nss)
            if it >= 1:
                emit_M2(it - 1, 1, 1)
    p2.close()

    tap_src = {"xmid": dxmid, "mod": dws_mod, "w1t": dws_w1t, "w2t": dws_w2t,
               "attn": dws_attn, "wo": dws_wo}
    all_done = [(s_sos[i], s.cnt[f"s_so{i}"]) for i in range(4)] + sc_done_l
    prev_tap = None
    for name in taps:
        srcd = tap_src[name]
        prev_tap = s.emit("gpsimd", lambda sy, a=srcd, d=tap_d[name]: sy.dma_start(
            out=d.ap(), in_=a.ap()), wait=all_done + W(prev_tap), sig=s_st, amt=16)
        all_done = []
    fin = [(s_sos[i], s.cnt[f"s_so{i}"]) for i in range(4)] + [(s_st, s.cnt["s_st"])]
    s.emit("sync", lambda sy: None, wait=fin)

    block = es.enter_context(nc.Block())
    s.install(block, self_waits=debug_sync)
    es.close()
    return nc


_CACHE = {}


def kernel(L0_attn_x, L0_attn_gamma, L0_attn_wq, L0_attn_wk, L0_attn_wv,
           L0_attn_wa, L0_attn_wg, L0_attn_wo, L0_ffn_norm1_w, L0_ffn_w1,
           L0_ffn_act_scale, L0_ffn_act_bias, L0_ffn_norm2_w, L0_ffn_w2,
           L0_c, L0_wmod):
    if "nc" not in _CACHE:
        _CACHE["nc"] = build_program()
    nc = _CACHE["nc"]
    cc = np.ascontiguousarray
    shared = {
        "wq": cc(L0_attn_wq), "wk": cc(L0_attn_wk), "wv": cc(L0_attn_wv),
        "wa": cc(L0_attn_wa), "wg": cc(L0_attn_wg), "wo": cc(L0_attn_wo),
        "gamma": cc(L0_attn_gamma), "n1w": cc(L0_ffn_norm1_w),
        "n2w": cc(L0_ffn_norm2_w), "w1": cc(L0_ffn_w1), "w2": cc(L0_ffn_w2),
        "ascale": cc(L0_ffn_act_scale), "abias": cc(L0_ffn_act_bias),
        "wmod": cc(L0_wmod),
    }
    in_maps = []
    for i in range(NCORE):
        m = dict(shared)
        m["x"] = cc(L0_attn_x[i * B:(i + 1) * B])
        m["c"] = cc(L0_c[i * B:(i + 1) * B])
        in_maps.append(m)
    res = run_bass_kernel_spmd(nc, in_maps, core_ids=list(range(NCORE)))
    return np.concatenate([r["out"] for r in res.results], axis=0)


# version 27
# speedup vs baseline: 1.0724x; 1.0430x over previous
import contextlib
import math

import numpy as np
import concourse.bass as bass
import concourse.mybir as mybir
from concourse.bass_utils import run_bass_kernel_spmd

F32 = mybir.dt.float32
BF16 = mybir.dt.bfloat16
FP8 = mybir.dt.float8e4
I32 = mybir.dt.int32
AF = mybir.ActivationFunctionType
OP = mybir.AluOpType
AX = mybir.AxisListType

NCORE = 8
B = 2
N = 2048
D = 1024
H = 4096
KC = D // 128
HC = H // 128
C = 256
TT = C // 128
NCH = N // C
NST = B * NCH
SQD = float(math.sqrt(D))
RMAGIC = 12582912.0
FORCE_ZERO_STATE = False
PI = float(np.pi)

ENGINES = ("sync", "vector", "scalar", "tensor", "gpsimd")


COMPUTE_ENGINES = ("vector", "scalar", "tensor", "gpsimd")


class Sched:

    def __init__(self, nc, es):
        self.nc = nc
        self.q = {e: [] for e in ENGINES}
        self.cnt = {}
        self._names = {}
        self.es = es
        self.esem = {}
        self.ecnt = {e: 0 for e in ENGINES}

    def sem(self, name):
        h = self.es.enter_context(self.nc.semaphore(name))
        self._names[id(h)] = name
        self.cnt[name] = 0
        return h

    def init_engine_sems(self):
        for e in COMPUTE_ENGINES:
            self.esem[e] = self.sem("se_" + e)

    def emit(self, eng, fn, wait=(), sig=None, amt=1):
        wait = tuple(w for w in wait if w is not None and (callable(w) or w[0] is not None))
        prior = self.ecnt[eng] if eng in COMPUTE_ENGINES else 0
        ev = None
        if sig is not None:
            nm = self._names[id(sig)]
            self.cnt[nm] += amt
            ev = (sig, self.cnt[nm])
        elif eng in COMPUTE_ENGINES:
            self.ecnt[eng] += 1
            ev = (self.esem[eng], self.ecnt[eng])
        self.q[eng].append((wait, fn, ev, amt, prior))
        return ev

    def install(self, block, self_waits=False):
        for eng in ENGINES:
            items = self.q[eng]
            dec = getattr(block, eng)
            es_ = self.esem.get(eng)
            sw = self_waits or eng in ("vector", "scalar", "gpsimd")

            def make(items, es_, sw=sw):
                def body(e):
                    for waits, fn, ev, amt, prior in items:
                        if sw and es_ is not None and prior > 0:
                            e.wait_ge(es_, prior)
                        for w in waits:
                            s_, v = w() if callable(w) else w
                            e.wait_ge(s_, v)
                        instr = fn(e)
                        if ev is not None:
                            if instr is None:
                                e.sem_inc(ev[0], amt)
                            else:
                                instr.then_inc(ev[0], amt)
                return body

            dec(make(items, es_))


def build_program(taps=None, build_stage=2, debug_sync=False):
    taps = taps if taps is not None else {}
    nc = bass.Bass("TRN2", target_bir_lowering=False,
                   detect_race_conditions=debug_sync)

    dx = nc.dram_tensor("x", [B, N, D], F32, kind="ExternalInput")
    dwq = nc.dram_tensor("wq", [D, D], F32, kind="ExternalInput")
    dwk = nc.dram_tensor("wk", [D, D], F32, kind="ExternalInput")
    dwv = nc.dram_tensor("wv", [D, D], F32, kind="ExternalInput")
    dwa = nc.dram_tensor("wa", [D, 2 * D], F32, kind="ExternalInput")
    dwg = nc.dram_tensor("wg", [D, D], F32, kind="ExternalInput")
    dwo = nc.dram_tensor("wo", [D, D], F32, kind="ExternalInput")
    dgamma = nc.dram_tensor("gamma", [D], F32, kind="ExternalInput")
    dn1w = nc.dram_tensor("n1w", [D], F32, kind="ExternalInput")
    dn2w = nc.dram_tensor("n2w", [H], F32, kind="ExternalInput")
    dw1 = nc.dram_tensor("w1", [H, D], F32, kind="ExternalInput")
    dw2 = nc.dram_tensor("w2", [D, H], F32, kind="ExternalInput")
    dascale = nc.dram_tensor("ascale", [1], F32, kind="ExternalInput")
    dabias = nc.dram_tensor("abias", [1], F32, kind="ExternalInput")
    dc = nc.dram_tensor("c", [B, D], F32, kind="ExternalInput")
    dwmod = nc.dram_tensor("wmod", [D, 6 * D], F32, kind="ExternalInput")
    dout = nc.dram_tensor("out", [B, N, D], F32, kind="ExternalOutput")

    NBLK_ATTN = 4 * 64 + 16 * 8
    dws_attn = nc.dram_tensor("ws_attn", [128, NBLK_ATTN * 128], BF16, kind="Internal")
    dws_wo = nc.dram_tensor("ws_wo", [D, D], BF16, kind="Internal")
    dws_w1t = nc.dram_tensor("ws_w1t", [128, HC * KC * 128], FP8, kind="Internal")
    dws_w2t = nc.dram_tensor("ws_w2t", [H, D], FP8, kind="Internal")
    dws_mod = nc.dram_tensor("ws_mod", [B, 6 * D], F32, kind="Internal")
    dws_grep = nc.dram_tensor("ws_grep", [128, 2 * B * D], BF16, kind="Internal")
    dxmid = nc.dram_tensor("xmid", [B, N, D], F32, kind="Internal")

    tap_d = {}
    for name, shape in taps.items():
        tap_d[name] = nc.dram_tensor("tap_" + name, list(shape), F32, kind="ExternalOutput")

    es = contextlib.ExitStack()
    E = es.enter_context
    s = Sched(nc, es)

    s.init_engine_sems()
    s_ld = s.sem("s_ld")
    s_wrs = [s.sem(f"s_wr{i}") for i in range(6)]
    s_wfr = [s.sem(f"s_wfr{i}") for i in range(6)]
    s_xls = [s.sem(f"s_xl{i}") for i in range(4)]
    s_xlh = [s.sem(f"s_xlh{i}") for i in range(4)]
    s_sos = [s.sem(f"s_so{i}") for i in range(4)]
    s_wm = [s.sem(f"s_wm{i}") for i in range(4)]
    s_wg = [s.sem(f"s_wg{i}") for i in range(3)]
    s_ws = [s.sem(f"s_ws{i}") for i in range(3)]
    s_wf = [s.sem(f"s_wf{i}") for i in range(4)]
    s_tt = [s.sem(f"s_tt{i}") for i in range(4)]
    s_ms = s.sem("s_ms")
    s_msb = [s.sem(f"s_msb{i}") for i in range(2)]
    s_mv = s.sem("s_mv")
    s_gr = s.sem("s_gr")
    s_st = s.sem("s_st")
    s_bar = s.sem("s_bar")

    identf = E(nc.sbuf_tensor("identf", [128, 128], F32))
    identb = E(nc.sbuf_tensor("identb", [128, 128], BF16))
    onesf = E(nc.sbuf_tensor("onesf", [128, 128], F32))
    onesb = E(nc.sbuf_tensor("onesb", [128, 1], BF16))
    zeroC = E(nc.sbuf_tensor("zeroC", [128, C], F32))
    cvec = E(nc.sbuf_tensor("cvec", [128, 10], F32))
    CEPS5, CEPS8, CPI2, CLSD, C19, C18, CL127, CZERO, CONE, CNEG1 = range(10)
    gam = E(nc.sbuf_tensor("gam", [128, KC], F32))
    n1v = E(nc.sbuf_tensor("n1v", [128, KC], F32))
    n2v = E(nc.sbuf_tensor("n2v", [128, HC], F32))
    asc = E(nc.sbuf_tensor("asc", [128, 1], F32))
    abi = E(nc.sbuf_tensor("abi", [128, 1], F32))
    ascr = E(nc.sbuf_tensor("ascr", [1, 1], F32))
    abir = E(nc.sbuf_tensor("abir", [1, 1], F32))
    m_sa = E(nc.sbuf_tensor("m_sa", [128, B * KC], F32))
    m_ca1 = E(nc.sbuf_tensor("m_ca1", [128, B * KC], F32))
    m_ga = E(nc.sbuf_tensor("m_ga", [128, B * KC], F32))
    m_sb = E(nc.sbuf_tensor("m_sb", [128, B * KC], F32))
    m_cb1 = E(nc.sbuf_tensor("m_cb1", [128, B * KC], F32))
    m_gb = E(nc.sbuf_tensor("m_gb", [128, B * KC], F32))
    w1sv = E(nc.sbuf_tensor("w1sv", [128, 1], F32))
    w2sv = E(nc.sbuf_tensor("w2sv", [128, 1], F32))
    w1sr = E(nc.sbuf_tensor("w1sr", [1, 1], F32))
    w2sr = E(nc.sbuf_tensor("w2sr", [1, 1], F32))
    stage = [E(nc.sbuf_tensor(f"stage{i}", [128, TT * 1024], F32)) for i in range(4)]
    NSLOT = 4
    NSLOTF = 6
    ring = E(nc.sbuf_tensor("ring", [128, NSLOT * 1024], BF16))
    ringf = E(nc.sbuf_tensor("ringf", [128, NSLOTF * 1024], FP8))

    pb = [E(nc.psum_tensor(f"pb{i}", [128, 512], F32)) for i in range(8)]
    ev_bank = [None] * 8

    W = lambda *evs: [e for e in evs if e is not None]

    ev_gp0 = None
    s.emit("gpsimd", lambda g: g.iota(identi[:], pattern=[[1, 128]], base=0,
                                      channel_multiplier=-1))
    s.emit("gpsimd", lambda g: g.memset(onesf[:], 1.0))
    s.emit("gpsimd", lambda g: g.memset(zeroC[:], 0.0))
    for j, valc in enumerate([1e-5, 1e-8, PI / 2, math.log(32.0), 1e-19, 1e-18,
                              math.log(127.0), 0.0, 1.0, -1.0]):
        ev_gp0 = s.emit("gpsimd", lambda g, j=j, v=valc: g.memset(cvec[:, j:j + 1], v))
    ev_idf = s.emit("vector", lambda v: v.tensor_scalar(identf[:], identi[:], 0.0, None, OP.is_equal),
                    wait=W(ev_gp0))
    ev_idb = s.emit("vector", lambda v: v.tensor_scalar(identb[:], identi[:], 0.0, None, OP.is_equal))
    ev_onb = s.emit("vector", lambda v: v.tensor_copy(onesb[:], onesf[:, 0:1]),
                    wait=W(ev_gp0))

    p0 = contextlib.ExitStack()
    P0 = p0.enter_context
    identi = P0(nc.sbuf_tensor("identi", [128, 128], I32))
    csb = P0(nc.sbuf_tensor("csb", [B, D], F32))

    def featvec_load(dst, dram, nk):
        ap = dram.ap().rearrange("(kc p) -> p kc", p=128)
        return s.emit("sync", lambda sy: sy.dma_start(out=dst[:, 0:nk], in_=ap,
                                                      allow_slow_non_contiguous=True),
                      sig=s_ld, amt=16)

    featvec_load(gam, dgamma, KC)
    featvec_load(n1v, dn1w, HC if False else KC)
    featvec_load(n2v, dn2w, HC)
    tiny = P0(nc.sbuf_tensor("tiny", [1, 2], F32))
    s.emit("sync", lambda sy: sy.dma_start(out=tiny[0:1, 0:1], in_=dascale.ap().unsqueeze(0)),
           sig=s_ld, amt=16)
    ev_t2 = s.emit("sync", lambda sy: sy.dma_start(out=tiny[0:1, 1:2], in_=dabias.ap().unsqueeze(0)),
                   sig=s_ld, amt=16)
    ev_cl = s.emit("sync", lambda sy: sy.dma_start(out=csb[:], in_=dc.ap()), sig=s_ld, amt=16)
    ev_misc = ev_cl
    ev_gs = s.emit("vector", lambda v: v.tensor_scalar(gam[:, 0:KC], gam[:, 0:KC], SQD, None, OP.mult),
                   wait=W(ev_misc))
    ev_t1 = ev_misc
    ev_t2 = ev_misc
    ev_mm = s.emit("tensor", lambda t: t.matmul(pb[3][0:128, 0:2], onesf[0:1, :], tiny[:],
                                                start=True, stop=True),
                   wait=W(ev_t1, ev_t2, ev_gp0))
    ev_as = s.emit("scalar", lambda sc: sc.copy(asc[:], pb[3][0:128, 0:1]), wait=W(ev_mm))
    ev_ab = s.emit("scalar", lambda sc: sc.copy(abi[:], pb[3][0:128, 1:2]))
    ev_asr = s.emit("vector", lambda v: v.tensor_copy(ascr[:], tiny[0:1, 0:1]), wait=W(ev_t1))
    ev_abr = s.emit("vector", lambda v: v.tensor_copy(abir[:], tiny[0:1, 1:2]), wait=W(ev_t2))
    ev_bank[3] = ev_ab

    csil = P0(nc.sbuf_tensor("csil", [B, D], F32))
    csilT = P0(nc.sbuf_tensor("csilT", [128, KC * B], BF16))
    mbuf = [P0(nc.sbuf_tensor(f"mbuf{i}", [B, 512], F32)) for i in range(2)]
    mmsk = P0(nc.sbuf_tensor("mmsk", [B, 512], F32))
    ebs = [P0(nc.sbuf_tensor(f"ebs{b}", [B, 128], F32)) for b in range(B)]
    grstg = P0(nc.sbuf_tensor("grstg", [128, 2 * B * D], BF16))
    ev_sil = s.emit("scalar", lambda sc: sc.activation(csil[:], csb[:], AF.Silu),
                    wait=W(ev_misc))
    for kc in range(KC):
        ev = s.emit("tensor", lambda t, kc=kc: t.matmul(
            pb[0][0:128, kc * B:(kc + 1) * B], csil[:, kc * 128:(kc + 1) * 128],
            identf[0:B, 0:B], start=(kc == 0), stop=(kc == KC - 1), is_transpose=True),
            wait=W(ev_sil, ev_idf, ev_bank[0]) if kc == 0 else ())
    ev_csT = s.emit("scalar", lambda sc: sc.copy(csilT[:], pb[0][0:128, 0:KC * B]),
                    wait=W(ev))
    ev_bank[0] = ev_csT
    NWM = 4
    wmb = [P0(nc.sbuf_tensor(f"wmb{i}", [128, 512], F32)) for i in range(NWM)]
    wmbb = [P0(nc.sbuf_tensor(f"wmbb{i}", [128, 512], BF16)) for i in range(NWM)]
    ev_cast = [None] * NWM
    ev_mmod = [None] * NWM
    ev = s.emit("gpsimd", lambda g: g.memset(ebs[0][0:B, :], 0.0))
    ev = s.emit("gpsimd", lambda g: g.memset(ebs[0][0:1, :], 1.0), wait=W(ev))
    ev = s.emit("gpsimd", lambda g: g.memset(ebs[1][0:B, :], 1.0), wait=W(ev))
    ev_eb = s.emit("gpsimd", lambda g: g.memset(ebs[1][0:1, :], 0.0), wait=W(ev))
    ev_mb = [None, None]
    GKIND = {2: 0, 5: 2 * D}

    def mod_tile(oc, kc):
        i = (oc * KC + kc) % NWM
        evl = s.emit("sync", lambda sy, oc=oc, kc=kc, i=i: sy.dma_start(
            out=wmb[i][:], in_=dwmod.ap()[kc * 128:(kc + 1) * 128, oc * 512:(oc + 1) * 512]),
            wait=W(ev_cast[i]), sig=s_wm[i], amt=16)
        ev_cast[i] = s.emit("scalar", lambda sc, i=i: sc.activation(wmbb[i][:], wmb[i][:], AF.Copy),
                            wait=W(evl, ev_mmod[i]))
        ev_mmod[i] = s.emit("tensor", lambda t, kc=kc, i=i: t.matmul(
            pb[1][0:B, 0:512], csilT[:, kc * B:(kc + 1) * B], wmbb[i][:],
            start=(kc == 0), stop=(kc == KC - 1)),
            wait=W(ev_cast[i], ev_csT if oc == 0 and kc == 0 else None,
                   ev_bank[1] if kc == 0 else None))
        if kc != KC - 1:
            return
        mi = oc % 2
        kind = oc // 2
        ev = s.emit("scalar", lambda sc, mi=mi: sc.copy(mbuf[mi][:], pb[1][0:B, 0:512]),
                    wait=W(ev_mmod[(oc * KC + KC - 1) % NWM], ev_mb[mi]))
        ev_bank[1] = ev
        if kind in (1, 4):
            ev = s.emit("vector", lambda v, mi=mi: v.tensor_scalar(
                mbuf[mi][:], mbuf[mi][:], 1.0, None, OP.add), wait=W(ev))
        elif kind in (2, 5):
            ev2 = s.emit("vector", lambda v, mi=mi: v.tensor_scalar(
                mmsk[:], mbuf[mi][:], 0.0, None, OP.is_ge), wait=W(ev))
            ev2 = s.emit("vector", lambda v: v.tensor_scalar(
                mmsk[:], mmsk[:], 2.0, 1.0, OP.mult, OP.subtract))
            ev3 = s.emit("scalar", lambda sc, mi=mi: sc.activation(mbuf[mi][:], mbuf[mi][:], AF.Abs),
                         wait=W(ev2))
            ev3 = s.emit("vector", lambda v, mi=mi: v.tensor_scalar(
                mbuf[mi][:], mbuf[mi][:], 0.25, None, OP.max), wait=W(ev3))
            ev = s.emit("vector", lambda v, mi=mi: v.tensor_tensor(
                mbuf[mi][:], mbuf[mi][:], mmsk[:], OP.mult))
            for b in range(B):
                evr = s.emit("tensor", lambda t, b=b, mi=mi: t.matmul(
                    pb[2][0:128, 0:512], ebs[b][:], mbuf[mi][:],
                    start=True, stop=True),
                    wait=W(ev, ev_eb, ev_bank[2]))
                ev_bank[2] = s.emit("scalar", lambda sc, b=b, oc=oc, kind=kind: sc.copy(
                    grstg[:, GKIND[kind] + b * D + (oc % 2) * 512:
                          GKIND[kind] + b * D + (oc % 2) * 512 + 512],
                    pb[2][0:128, 0:512]), wait=W(evr))
        ev_mb[mi] = s.emit("sync", lambda sy, oc=oc, mi=mi: sy.dma_start(
            out=dws_mod.ap()[:, oc * 512:(oc + 1) * 512], in_=mbuf[mi][:]),
            wait=W(ev, ev_bank[2] if kind in (2, 5) else None), sig=s_msb[mi], amt=16)

    NWS = 3
    wstg = [P0(nc.sbuf_tensor(f"wstg{i}", [128, 2048], F32)) for i in range(NWS)]
    wstgb = [P0(nc.sbuf_tensor(f"wstgb{i}", [128, 2048], BF16)) for i in range(NWS)]
    ev_wst = [None] * NWS

    def attn_tile(dram, nmc, blk0, kc, wo=False):
        i = attn_tile.n % NWS
        attn_tile.n += 1
        ncol = nmc * 128
        evl = s.emit("sync", lambda sy, dram=dram, kc=kc, i=i, ncol=ncol: sy.dma_start(
            out=wstg[i][:, 0:ncol], in_=dram.ap()[kc * 128:(kc + 1) * 128, :]),
            wait=W(ev_wst[i]), sig=s_wg[i], amt=16)
        if attn_tile.n % 2:
            evc = s.emit("scalar", lambda sc, i=i, ncol=ncol: sc.activation(
                wstgb[i][:, 0:ncol], wstg[i][:, 0:ncol], AF.Copy), wait=W(evl))
        else:
            evc = s.emit("vector", lambda v, i=i, ncol=ncol: v.tensor_copy(
                wstgb[i][:, 0:ncol], wstg[i][:, 0:ncol]), wait=W(evl))
        if wo:
            ev_wst[i] = s.emit("scalar", lambda sc, kc=kc, i=i: sc.dma_start(
                out=dws_wo.ap()[kc * 128:(kc + 1) * 128, :], in_=wstgb[i][:, 0:1024]),
                wait=W(evc), sig=s_ws[i], amt=16)
            return

        def store_blk(sc, i=i, nmc=nmc, blk0=blk0, kc=kc):
            src = wstgb[i][:, 0:nmc * 128].rearrange("p (mc j) -> p mc j", j=128)
            dst = dws_attn.ap().rearrange("p (blk j) -> p blk j", j=128)
            return sc.dma_start(out=dst[:, blk0 + kc:blk0 + nmc * KC:KC, :], in_=src)
        ev_wst[i] = s.emit("scalar", store_blk, wait=W(evc), sig=s_ws[i], amt=16)
    attn_tile.n = 0

    accb = P0(nc.sbuf_tensor("accb", [128, HC + KC], F32))
    acc1 = P0(nc.sbuf_tensor("acc1", [128, 2], F32))
    scal = P0(nc.sbuf_tensor("scal", [1, 4], F32))
    NWF = 4
    wstgF = [P0(nc.sbuf_tensor(f"wstgF{i}", [128, 4096], F32)) for i in range(NWF)]
    NTTL = 4
    ttlb = [P0(nc.sbuf_tensor(f"ttlb{i}", [128, 4096], FP8)) for i in range(NTTL)]
    wsb_ = [P0(nc.sbuf_tensor(f"wsb{i}", [128, 1], F32)) for i in range(2)]
    ev_wstF = [None] * NWF
    ev_ttlb = [None] * NTTL
    FFN_W = [(dw1, HC, w1sv, w1sr, 0), (dw2, KC, w2sv, w2sr, HC)]

    def ffn_scale_tile(wi, ti):
        dram, ntile, sv, sr, acc0 = FFN_W[wi]
        cols = int(dram.shape[1])
        i = ffn_scale_tile.n % NWF
        ffn_scale_tile.n += 1
        evl = s.emit("gpsimd", lambda sy, dram=dram, ti=ti, i=i, cols=cols: sy.dma_start(
            out=wstgF[i][:, 0:cols], in_=dram.ap()[ti * 128:(ti + 1) * 128, :]),
            wait=W(ev_wstF[i]), sig=s_wf[i], amt=16)
        ev_wstF[i] = s.emit("vector", lambda v, ti=ti, i=i, cols=cols, acc0=acc0:
                            v.scalar_tensor_tensor(
                                wstgF[i][:, 0:cols], wstgF[i][:, 0:cols], -1.0,
                                wstgF[i][:, 0:cols], OP.mult, OP.max,
                                accum_out=accb[:, acc0 + ti:acc0 + ti + 1]),
                            wait=W(evl))
        return ev_wstF[i]

    ffn_scale_tile.n = 0

    def ffn_finalize(wi, ev_acc):
        dram, ntile, sv, sr, acc0 = FFN_W[wi]
        cols = int(dram.shape[1])
        ev = s.emit("vector", lambda v, ntile=ntile, acc0=acc0: v.tensor_reduce(
            acc1[:, wi:wi + 1], accb[:, acc0:acc0 + ntile], AX.X, OP.add), wait=W(ev_acc))
        ev = s.emit("tensor", lambda t: t.matmul(pb[3][0:1, 0:1], acc1[:, wi:wi + 1],
                                                 onesf[:, 0:1], start=True, stop=True),
                    wait=W(ev, ev_bank[3]))
        nelem = float(128 * ntile * cols)
        ev = s.emit("scalar", lambda sc, nelem=nelem: sc.activation(
            scal[0:1, 0:1], pb[3][0:1, 0:1], AF.Copy, scale=1.0 / nelem), wait=W(ev))
        ev_bank[3] = ev
        ev = s.emit("vector", lambda v: v.tensor_scalar(scal[0:1, 0:1], scal[0:1, 0:1],
                                                        1e-5, None, OP.max), wait=W(ev))
        ev = s.emit("vector", lambda v: v.reciprocal(scal[0:1, 1:2], scal[0:1, 0:1]))
        s.emit("vector", lambda v, sr=sr: v.tensor_copy(sr[:], scal[0:1, 0:1]))
        ev = s.emit("tensor", lambda t: t.matmul(pb[3][0:128, 0:2], onesf[0:1, :], scal[0:1, 0:2],
                                                 start=True, stop=True),
                    wait=W(ev, ev_bank[3]))
        s.emit("scalar", lambda sc, sv=sv: sc.copy(sv[:], pb[3][0:128, 0:1]), wait=W(ev))
        ev_ws = s.emit("scalar", lambda sc, wi=wi: sc.copy(wsb_[wi][:], pb[3][0:128, 1:2]))
        ev_bank[3] = ev_ws
        return ev_ws

    def ffn_quant_tile(wi, ti, ev_ws):
        dram, ntile, sv, sr, acc0 = FFN_W[wi]
        cols = int(dram.shape[1])
        i = ffn_quant_tile.n % NWF
        ffn_quant_tile.n += 1
        evl = s.emit("gpsimd", lambda sy, dram=dram, ti=ti, i=i, cols=cols: sy.dma_start(
            out=wstgF[i][:, 0:cols], in_=dram.ap()[ti * 128:(ti + 1) * 128, :]),
            wait=W(ev_wstF[i], ffn_quant_tile.tp[i], ev_ws), sig=s_wf[i], amt=16)
        ev_wstF[i] = None
        ev = s.emit("vector", lambda v, i=i, cols=cols, wi=wi: v.tensor_scalar(
            wstgF[i][:, 0:cols], wstgF[i][:, 0:cols], wsb_[wi][:], None, OP.mult),
            wait=W(evl))
        ev = s.emit("vector", lambda v, i=i, cols=cols: v.tensor_scalar(
            wstgF[i][:, 0:cols], wstgF[i][:, 0:cols], RMAGIC, RMAGIC,
            OP.add, OP.subtract))
        ev_q = s.emit("vector", lambda v, i=i, cols=cols: v.tensor_scalar(
            wstgF[i][:, 0:cols], wstgF[i][:, 0:cols], 1.0, -1.0, OP.min, OP.max))
        i2 = ti % NTTL
        evc = None
        for jb in range(cols // 512):
            bk = 4 + ((ti + jb) % 4)
            ev = None
            for j4 in range(4):
                j = jb * 4 + j4
                ev = s.emit("tensor", lambda t, i=i, j=j, j4=j4, bk=bk: t.matmul(
                    pb[bk][0:128, j4 * 128:(j4 + 1) * 128],
                    wstgF[i][:, j * 128:(j + 1) * 128], identf[:],
                    start=(j4 == 0), stop=(j4 == 3), is_transpose=True),
                    wait=W(ev_q, ev_idf, ev_bank[bk],
                           ev_ttlb[i2] if j == 0 else None) if j4 == 0 else ())
            ffn_quant_tile.tp[i] = ev
            evc = s.emit("scalar", lambda sc, i2=i2, jb=jb, bk=bk: sc.copy(
                ttlb[i2][:, jb * 512:(jb + 1) * 512], pb[bk][0:128, 0:512]),
                wait=W(ev))
            ev_bank[bk] = evc
        if wi == 0:
            def stw(sy, ti=ti, i2=i2, cols=cols):
                return sy.dma_start(
                    out=dws_w1t.ap()[:, ti * KC * 128:(ti + 1) * KC * 128],
                    in_=ttlb[i2][:, 0:cols])
        else:
            def stw(sy, ti=ti, i2=i2, cols=cols):
                dst = dws_w2t.ap().rearrange("(j p) o -> p j o", p=128)[
                    :, :, ti * 128:(ti + 1) * 128]
                srcv = ttlb[i2][:, 0:cols].rearrange("p (j c) -> p j c", c=128)
                return sy.dma_start(out=dst, in_=srcv,
                                    allow_slow_non_contiguous=True)
        ev_ttlb[i2] = s.emit("scalar", stw, wait=W(evc), sig=s_tt[i2], amt=16)
    ffn_quant_tile.n = 0
    ffn_quant_tile.tp = [None] * NWF

    attn_seq = []
    for dram, nmc, blk0 in [(dwq, 8, 0), (dwk, 8, 64), (dwv, 8, 128),
                            (dwg, 8, 192), (dwa, 16, 256)]:
        for kc in range(KC):
            attn_seq.append((dram, nmc, blk0, kc, False))
    for kc in range(KC):
        attn_seq.append((dwo, 8, 0, kc, True))
    mod_seq = [(oc, kc) for oc in range(12) for kc in range(KC)]
    scale_seq = [(0, ti) for ti in range(HC)] + [(1, ti) for ti in range(KC)]
    ev_acc = {0: None, 1: None}
    ws1 = None
    qpend = []
    nrounds = max(len(attn_seq), len(mod_seq), len(scale_seq))
    for r in range(nrounds):
        if r < len(mod_seq):
            mod_tile(*mod_seq[r])
        if r < len(attn_seq):
            attn_tile(*attn_seq[r])
        if r < len(scale_seq):
            wi, ti = scale_seq[r]
            ev_acc[wi] = ffn_scale_tile(wi, ti)
            if wi == 0 and ti == HC - 1:
                ws1 = ffn_finalize(0, ev_acc[0])
        elif ws1 is not None and qpend is not None and len(qpend) < HC:
            ffn_quant_tile(0, len(qpend), ws1)
            qpend.append(None)
    ev_garep = ev_bank[2]
    ev_grs = s.emit("sync", lambda sy: sy.dma_start(out=dws_grep.ap(), in_=grstg[:]),
                    wait=W(ev_garep), sig=s_gr, amt=16)
    ev_mv = {}
    for kind, dstv in [(0, m_sa), (1, m_ca1), (2, m_ga), (3, m_sb), (4, m_cb1), (5, m_gb)]:
        for b in range(B):
            srcap = dws_mod.ap()[b].rearrange("(kind kc p) -> p kc kind", p=128, kind=6)[:, :, kind]
            ev_mv[kind] = s.emit("sync", lambda sy, dstv=dstv, srcap=srcap, b=b: sy.dma_start(
                out=dstv[:, b * KC:(b + 1) * KC], in_=srcap, allow_slow_non_contiguous=True),
                wait=W(ev_mb[0], ev_mb[1]), sig=s_mv, amt=16)
    if ws1 is None:
        ws1 = ffn_finalize(0, ev_acc[0])
    ws2 = ffn_finalize(1, ev_acc[1])
    for ti in range(len(qpend), HC):
        ffn_quant_tile(0, ti, ws1)
    for ti in range(KC):
        ffn_quant_tile(1, ti, ws2)

    p0.close()
    for eng in ENGINES:
        s.emit(eng, lambda e: None, sig=s_bar)
    bar1 = (s_bar, s.cnt["s_bar"])
    ws_done = [(s_ws[i], s.cnt[f"s_ws{i}"]) for i in range(3)]
    tt_done = [(s_tt[i], s.cnt[f"s_tt{i}"]) for i in range(4)]
    sc_done_l = ws_done + tt_done
    sc_done_l += [(s_msb[0], s.cnt["s_msb0"]), (s_msb[1], s.cnt["s_msb1"])]
    p0_done = (sc_done_l + [(s_wg[i], s.cnt[f"s_wg{i}"]) for i in range(3)]
               + [(s_wf[i], s.cnt[f"s_wf{i}"]) for i in range(4)]
               + [(s_wm[i], s.cnt[f"s_wm{i}"]) for i in range(4)]
               + [(s_gr, s.cnt["s_gr"])])
    modv_done = (s_mv, s.cnt["s_mv"])

    ring_state = {"g": 0, "loaded": [], "release": {}, "slot": {},
                  "meta": {id(ring): {"n": 0, "nslot": NSLOT, "g_of": [],
                                      "sems": s_wrs},
                           id(ringf): {"n": 0, "nslot": NSLOTF, "g_of": [],
                                       "sems": s_wfr}}}

    def ring_release(g, ev):
        ring_state["release"][g] = ev

    def ring_load(buf, reqs, wait_extra=()):
        g = ring_state["g"]
        st = ring_state["meta"][id(buf)]
        li = st["n"]
        st["n"] += 1
        st["g_of"].append(g)
        slot = li % st["nslot"]
        ring_state["slot"][g] = slot
        waits = list(wait_extra)
        if li >= st["nslot"]:
            gprev = st["g_of"][li - st["nslot"]]
            waits.append(lambda gprev=gprev: ring_state["release"][gprev])
        ev = None
        for (src, c0, ncols) in reqs:
            ev = s.emit("gpsimd", lambda gp, buf=buf, slot=slot, c0=c0, ncols=ncols, src=src:
                        gp.dma_start(out=buf[:, slot * 1024 + c0:slot * 1024 + c0 + ncols],
                                     in_=src),
                        wait=waits, sig=st["sems"][slot], amt=16)
            waits = []
        ring_state["loaded"].append(ev)
        ring_state["g"] += 1
        return g

    def ring_wait(g):
        return ring_state["loaded"][g]

    def ring_ap(buf, g, c0, ncols):
        slot = ring_state["slot"][g]
        return buf[:, slot * 1024 + c0:slot * 1024 + c0 + ncols]

    attn_pg0 = {}
    attn_wg0 = {}

    PROJ_BLK = {"k": 64, "v": 128, "g": 192, "ar": 256, "ai": 256, "q": 0}

    def proj_desc():
        desc = []
        for pas in (0, 1):
            for kind in ("k", "v", "g", "ar", "ai", "q"):
                for b in (0, 1):
                    hb = pas if b == 0 else 1 - pas
                    for mcx in range(4):
                        mc = (8 if kind == "ai" else 0) + hb * 4 + mcx
                        desc.append((pas, b, kind, mc, hb))
        return desc

    def queue_proj_groups(ck, extra=()):
        attn_pg0[ck] = ring_state["g"]
        for (pas, b, kind, mc, hb) in proj_desc():
            blk0 = PROJ_BLK[kind]
            srcw = dws_attn.ap()[:, (blk0 + mc * KC) * 128:(blk0 + mc * KC + 8) * 128]
            ring_load(ring, [(srcw, 0, 1024)], wait_extra=extra)
            extra = ()

    def queue_wo_groups(ck):
        attn_wg0[ck] = ring_state["g"]
        for oc in range(2):
            for kcp in range(4):
                reqs = []
                for jj in range(2):
                    kc = kcp * 2 + jj
                    reqs.append((dws_wo.ap()[kc * 128:(kc + 1) * 128, oc * 512:(oc + 1) * 512],
                                 jj * 512, 512))
                ring_load(ring, reqs)

    if build_stage >= 1:
        queue_proj_groups(0, extra=[bar1] + sc_done_l)
        for ck in range(1, NCH):
            queue_proj_groups(ck)
            queue_wo_groups(ck - 1)
        queue_wo_groups(NCH - 1)

    p1 = contextlib.ExitStack()
    P1 = p1.enter_context
    x_fm = P1(nc.sbuf_tensor("x_fm", [128, KC * C], F32))
    xh = P1(nc.sbuf_tensor("xh", [128, KC * C], F32))
    hsq = P1(nc.sbuf_tensor("hsq", [128, KC * C], BF16))
    xn = [P1(nc.sbuf_tensor(f"xn{i}", [128, KC * C], BF16)) for i in range(2)]
    yb = [P1(nc.sbuf_tensor(f"yb{i}", [128, KC * C], BF16)) for i in range(2)]
    w7 = P1(nc.sbuf_tensor("w7", [128, KC * C], F32))
    w8 = P1(nc.sbuf_tensor("w8", [128, KC * C], F32))
    w9 = P1(nc.sbuf_tensor("w9", [128, KC * C], F32))
    w10 = P1(nc.sbuf_tensor("w10", [128, KC * C], F32))
    w11 = P1(nc.sbuf_tensor("w11", [128, KC * C], F32))
    psi = P1(nc.sbuf_tensor("psi", [128, KC * C], F32))
    qb = [P1(nc.sbuf_tensor(f"qb{i}", [128, KC * C], BF16)) for i in range(2)]
    kbn = [P1(nc.sbuf_tensor(f"kbn{i}", [128, KC * C], BF16)) for i in range(2)]
    vb = [P1(nc.sbuf_tensor(f"vb{i}", [128, KC * C], BF16)) for i in range(2)]
    sgb = [P1(nc.sbuf_tensor(f"sgb{i}", [128, KC * C], BF16)) for i in range(2)]
    arb = [P1(nc.sbuf_tensor(f"arb{i}", [128, KC * C], F32)) for i in range(2)]
    aib = [P1(nc.sbuf_tensor(f"aib{i}", [128, KC * C], F32)) for i in range(2)]
    mrow = P1(nc.sbuf_tensor("mrow", [1, 2 * C], F32))
    srow = P1(nc.sbuf_tensor("srow", [1, C], F32))
    mrep = P1(nc.sbuf_tensor("mrep", [128, 2 * C], F32))
    grep = P1(nc.sbuf_tensor("grep", [128, C], F32))
    mvst = P1(nc.sbuf_tensor("mvst", [128, 4], F32))
    varr = P1(nc.sbuf_tensor("varr", [128, 2], F32))
    msq2 = P1(nc.sbuf_tensor("msq2", [128, 2], F32))
    sagam = P1(nc.sbuf_tensor("sagam", [128, B * KC], F32))
    ga_rep = P1(nc.sbuf_tensor("ga_rep", [128, B * D], BF16))
    st_psi = [P1(nc.sbuf_tensor(f"stpsi{b}", [128, KC], F32)) for b in range(B)]
    st_zr = [P1(nc.sbuf_tensor(f"stzr{b}", [128, KC], F32)) for b in range(B)]
    st_zi = [P1(nc.sbuf_tensor(f"stzi{b}", [128, KC], F32)) for b in range(B)]

    ev1 = {}

    def xf_slice(buf, kc, t0=0, nt=C):
        return buf[:, kc * C + t0:kc * C + t0 + nt]

    def lastcol(buf):
        return buf[:].rearrange("p (kc c) -> p kc c", c=C)[:, :, C - 1]

    def emit_A1_tp(ss):
        b, ck = ss % 2, ss // 2
        slot = ss % 4
        wv = [] if ss < 4 else [ev1.get(("store", ss - 4))]
        src = dx.ap()[b].rearrange("(ck tt p) d -> ck p tt d", p=128, tt=TT)[ck]
        ev1[("xlh", ss)] = s.emit("sync", lambda sy, slot=slot, src=src: sy.dma_start(
            out=stage[slot][:, 0:1024], in_=src[:, 0:1, :]),
            wait=W(*wv), sig=s_xlh[slot], amt=16)
        ev1[("xl", ss)] = s.emit("sync", lambda sy, slot=slot, src=src: sy.dma_start(
            out=stage[slot][:, 1024:2048], in_=src[:, 1:2, :]), sig=s_xls[slot], amt=16)
        last = None
        for tt in range(TT):
            for kcg in range(2):
                bk = kcg
                ev = None
                for i2 in range(4):
                    kc = kcg * 4 + i2
                    ev = s.emit("tensor", lambda t, slot=slot, tt=tt, kc=kc, i2=i2, bk=bk: t.matmul(
                        pb[bk][0:128, i2 * 128:(i2 + 1) * 128],
                        stage[slot][:, tt * 1024 + kc * 128:tt * 1024 + (kc + 1) * 128],
                        identf[:], start=(i2 == 0), stop=(i2 == 3), is_transpose=True),
                        wait=W(ev1[("xlh", ss)], ev1[("xl", ss)], ev_bank[bk],
                               ev1.get(("xh", ss - 1))) if i2 == 0 else ())

                def evacx(sc, tt=tt, kcg=kcg, bk=bk):
                    dst = x_fm[:].rearrange("p (kc c) -> p kc c", c=C)[
                        :, kcg * 4:(kcg + 1) * 4, tt * 128:(tt + 1) * 128]
                    return sc.copy(dst, pb[bk][0:128, 0:512])
                last = ev_bank[bk] = s.emit(
                    "scalar", evacx,
                    wait=W(ev) + (p0_done if ss < 2 else []))
        ev1[("xfm", ss)] = last

    def emit_A1_stats(ss):
        b, ck = ss % 2, ss // 2
        slot = ss % 4
        ev = None
        for tt in range(TT):
            s.emit("vector", lambda v, slot=slot, tt=tt: v.tensor_scalar(
                hsq[:].bitcast(F32)[:, 0:1024], stage[slot][:, tt * 1024:(tt + 1) * 1024],
                1.0 / D, None, OP.mult, OP.add, accum_out=mvst[:, tt:tt + 1]),
                wait=(W(ev1[("xlh", ss)], ev1.get(("mrow", ss - 1)), ev1.get(("hsq", ss - 1)),
                        modv_done if ss == 0 else None)
                      + (p0_done if ss < 2 else [])) if tt == 0 else
                     W(ev1[("xl", ss)]) if tt == 1 else ())
            ev = s.emit("vector", lambda v, slot=slot, tt=tt: v.scalar_tensor_tensor(
                hsq[:].bitcast(F32)[:, 0:1024], stage[slot][:, tt * 1024:(tt + 1) * 1024], 1.0,
                stage[slot][:, tt * 1024:(tt + 1) * 1024], OP.mult, OP.mult,
                accum_out=mvst[:, 2 + tt:3 + tt]))
        ev = s.emit("vector", lambda v: v.tensor_scalar(
            varr[:], mvst[:, 2:4], 1.0 / D, None, OP.mult), wait=W(ev))
        ev = s.emit("vector", lambda v: v.tensor_tensor(
            msq2[:], mvst[:, 0:2], mvst[:, 0:2], OP.mult))
        ev = s.emit("vector", lambda v: v.tensor_tensor(varr[:], varr[:], msq2[:], OP.subtract))
        ev = s.emit("scalar", lambda sc: sc.activation(varr[:], varr[:], AF.Ln,
                                                       bias=cvec[:, CEPS5:CEPS5 + 1]),
                    wait=W(ev))
        ev_rstd = s.emit("scalar", lambda sc: sc.activation(varr[:], varr[:], AF.Exp, scale=-0.5))
        ev = None
        for i2, (buf_, col) in enumerate([(mvst, 0), (mvst, 1), (varr, 0), (varr, 1)]):
            ev = s.emit("tensor", lambda t, buf_=buf_, col=col, i2=i2: t.matmul(
                pb[3][0:1, i2 * 128:(i2 + 1) * 128], buf_[:, col:col + 1], identf[:],
                start=(i2 == 0), stop=(i2 == 3), is_transpose=True),
                wait=W(ev_rstd, ev_bank[3]) if i2 == 0 else ())
        ev_mrow = s.emit("scalar", lambda sc: sc.copy(mrow[:], pb[3][0:1, 0:512]),
                         wait=W(ev))
        ev_bank[3] = ev_mrow
        ev1[("mrow", ss)] = ev_mrow
        ev = s.emit("tensor", lambda t: t.matmul(pb[2][0:128, 0:C], onesf[0:1, :],
                                                 mrow[0:1, 0:C], start=True, stop=False),
                    wait=W(ev_mrow, ev_bank[2], ev1.get(("xh", ss - 1)),
                           ev1.get(("hsq", ss - 1))))
        ev = s.emit("tensor", lambda t: t.matmul(pb[2][0:128, C:2 * C], onesf[0:1, :],
                                                 mrow[0:1, C:2 * C], start=False, stop=True))
        ev_mrep = s.emit("scalar", lambda sc: sc.copy(mrep[:], pb[2][0:128, 0:2 * C]),
                         wait=W(ev))
        ev_bank[2] = ev_mrep
        ev = None
        for kc in range(KC):
            s.emit("vector", lambda v, kc=kc: v.tensor_tensor(
                xf_slice(xh, kc), xf_slice(x_fm, kc), mrep[:, 0:C], OP.subtract),
                wait=W(ev1[("xfm", ss)], ev_mrep) if kc == 0 else ())
            ev = s.emit("vector", lambda v, kc=kc, b=b: v.scalar_tensor_tensor(
                xf_slice(xh, kc), xf_slice(xh, kc), m_ca1[:, b * KC + kc:b * KC + kc + 1],
                mrep[:, C:2 * C], OP.mult, OP.mult))
            ev1[("xhk", ss, kc)] = ev
        ev1[("xh", ss)] = ev

    def emit_S21(ss):
        b = ss % 2
        ev_h = None
        for kc in range(KC):
            ev_h = s.emit("scalar", lambda sc, kc=kc, b=b: sc.activation(
                xf_slice(hsq, kc), xf_slice(xh, kc), AF.Square,
                bias=m_sa[:, b * KC + kc:b * KC + kc + 1]),
                wait=W(ev1[("xhk", ss, kc)]))
        ev1[("hsq", ss)] = ev_h
        ev = None
        for kc in range(KC):
            ev = s.emit("tensor", lambda t, kc=kc: t.matmul(
                pb[3][0:1, 0:C], onesb[:], xf_slice(hsq, kc),
                start=(kc == 0), stop=(kc == KC - 1)),
                wait=W(ev_h, ev_bank[3]) if kc == 0 else ())
        ev = s.emit("scalar", lambda sc: sc.copy(srow[:], pb[3][0:1, 0:C]), wait=W(ev))
        ev = s.emit("scalar", lambda sc: sc.activation(srow[:], srow[:], AF.Ln,
                                                       bias=cvec[0:1, CEPS5:CEPS5 + 1]))
        ev = s.emit("scalar", lambda sc: sc.activation(srow[:], srow[:], AF.Exp, scale=-0.5))
        evm = s.emit("tensor", lambda t: t.matmul(pb[3][0:128, 0:C], onesf[0:1, :],
                                                  srow[0:1, :], start=True, stop=True),
                     wait=W(ev, ev1.get(("xn", ss - 1))))
        ev = s.emit("scalar", lambda sc: sc.copy(grep[:], pb[3][0:128, 0:C]),
                    wait=W(evm))
        ev_bank[3] = ev
        ev1[("grep", ss)] = ev

    def emit_XN(ss):
        b = ss % 2
        ck2 = ss // 2
        ids = []
        for kc in range(KC):
            ids.append(s.emit("scalar", lambda sc, kc=kc, b=b: sc.activation(
                xf_slice(xh, kc), xf_slice(xh, kc), AF.Identity,
                scale=gam[:, kc:kc + 1], bias=sagam[:, b * KC + kc:b * KC + kc + 1])))
        ev = None
        for kc in range(KC):
            ev = s.emit("vector", lambda v, kc=kc, b=b: v.tensor_tensor(
                xf_slice(xn[b], kc), xf_slice(xh, kc), grep[:], OP.mult),
                wait=W(ev1[("grep", ss)], ids[kc],
                       ev1.get(("PpairB", ck2 - 1, b)) if kc == 0 else None))
        ev1[("xn", ss)] = ev

    def chain_stream(ts, hs, wprev_key):
        b, ck = ts % 2, ts // 2
        p = ts % 2
        HW = 4 * C

        def sl(buf):
            return buf[:, hs * HW:(hs + 1) * HW]
        wprev = ev1.get(wprev_key) if wprev_key else None
        evP_ = ev1[("evP", ts)]
        h = {}
        steps = []

        def op(eng, fn, wk=(), name=None, free_tag=None):
            def th(eng=eng, fn=fn, wk=wk, name=name, free_tag=free_tag):
                ev = s.emit(eng, fn, wait=W(*[h[k] if isinstance(k, str) else k for k in wk]))
                if name:
                    h[name] = ev
                if free_tag:
                    ev1[(free_tag, ts, hs)] = ev
            steps.append(th)

        op("vector", lambda v: v.tensor_tensor(
            sl(qb[p]), sl(qb[p]), sl(sgb[p]), OP.mult),
            wk=(evP_, wprev), name="qg", free_tag="fsgb")
        op("vector", lambda v: v.tensor_tensor(
            sl(w11), sl(kbn[p]), sl(vb[p]), OP.mult),
            wk=(evP_, wprev), name="kv", free_tag="fkv")
        op("scalar", lambda sc: sc.activation(sl(w8), sl(arb[p]), AF.Square),
           wk=(evP_, wprev), name="e1")
        op("scalar", lambda sc: sc.activation(sl(w9), sl(aib[p]), AF.Square), name="e2")
        op("vector", lambda v: v.tensor_tensor(sl(w8), sl(w8), sl(w9), OP.add),
           wk=("e2",), name="e3")
        op("scalar", lambda sc: sc.activation(sl(w9), sl(w8), AF.Ln,
                                              bias=cvec[:, C18:C18 + 1]),
           wk=("e3",), name="e4")
        op("scalar", lambda sc: sc.activation(sl(w9), sl(w9), AF.Exp, scale=0.5),
           name="e5")
        op("vector", lambda v: v.tensor_tensor(sl(w10), sl(w9), sl(arb[p]), OP.add),
           wk=("e5",), name="e6a", free_tag="farb")
        op("scalar", lambda sc: sc.activation(sl(w10), sl(w10), AF.Relu),
           wk=("e6a",), name="e6b")
        op("vector", lambda v: v.tensor_tensor(sl(w8), sl(w9), sl(w10), OP.mult),
           wk=("e6b",), name="e7")
        op("scalar", lambda sc: sc.activation(sl(w8), sl(w8), AF.Ln, scale=2.0,
                                              bias=cvec[:, C18:C18 + 1]),
           wk=("e7",), name="e8")
        op("scalar", lambda sc: sc.activation(sl(w8), sl(w8), AF.Exp, scale=0.5),
           name="e9")
        op("vector", lambda v: v.tensor_tensor(sl(w8), sl(w8), sl(w10), OP.add),
           wk=("e9",), name="e10")
        op("scalar", lambda sc: sc.activation(sl(w8), sl(w8), AF.Ln,
                                              bias=cvec[:, C19:C19 + 1]),
           wk=("e10",), name="e11")
        op("scalar", lambda sc: sc.activation(sl(w8), sl(w8), AF.Exp, scale=-1.0),
           name="e12")
        op("vector", lambda v: v.tensor_tensor(sl(w10), sl(aib[p]), sl(w8), OP.mult),
           wk=("e12",), name="e13a", free_tag="faib")
        op("vector", lambda v: v.tensor_scalar(sl(w10), sl(w10), 1.0, -1.0,
                                               OP.min, OP.max), name="e13b")
        op("scalar", lambda sc: sc.activation(sl(w7), sl(w9), AF.Sigmoid),
           wk=(wprev,), name="e14")
        op("scalar", lambda sc: sc.activation(sl(w10), sl(w10), AF.Arctan),
           wk=("e13b",), name="e15")

        def scanstep(out, in0, in1, state, wname):
            def th(out=out, in0=in0, in1=in1, state=state, wname=wname):
                for kx in range(4):
                    kc = hs * 4 + kx
                    init = 0.0 if ck == 0 or FORCE_ZERO_STATE else state[b][:, kc:kc + 1]
                    s.emit("vector", lambda v, kc=kc, init=init, out=out, in0=in0,
                           in1=in1: v.tensor_tensor_scan(
                               xf_slice(out, kc), xf_slice(in0, kc),
                               xf_slice(in1, kc) if in1 is not zeroC else zeroC[:],
                               init, OP.mult if in1 is not zeroC else OP.add, OP.add),
                           wait=W(h[wname]) if kx == 0 else ())
            steps.append(th)

        def statecopy(state, srcbuf):
            def th(state=state, srcbuf=srcbuf):
                lc = srcbuf[:].rearrange("p (kc c) -> p kc c", c=C)[
                    :, hs * 4:(hs + 1) * 4, C - 1]
                s.emit("vector", lambda v, lc=lc: v.tensor_copy(
                    state[b][:, hs * 4:(hs + 1) * 4], lc))
            steps.append(th)

        scanstep(psi, w10, zeroC, st_psi, "e15")
        statecopy(st_psi, psi)
        op("vector", lambda v: v.tensor_scalar(
            sl(w9), sl(psi), 2.0 / PI, RMAGIC, OP.mult, OP.add), name="w8i")
        op("vector", lambda v: v.tensor_scalar(
            sl(w9), sl(w9), RMAGIC, -PI / 2.0, OP.subtract, OP.mult))
        op("vector", lambda v: v.tensor_tensor(
            sl(w8), sl(psi), sl(w9), OP.add), name="e20")
        op("scalar", lambda sc: sc.activation(sl(w9), sl(w8), AF.Sin, scale=2.0),
           wk=("e20",), name="e21")
        op("scalar", lambda sc: sc.activation(sl(w9), sl(w9), AF.Square), name="e23")
        op("scalar", lambda sc: sc.activation(sl(w10), sl(w9), AF.Identity,
                                              scale=-2.0, bias=cvec[:, CONE:CONE + 1]),
           name="e27")
        op("scalar", lambda sc: sc.activation(sl(w9), sl(w8), AF.Sin, scale=4.0),
           name="e28")
        op("vector", lambda v: v.tensor_tensor(sl(w8), sl(w10), sl(w11), OP.mult),
           wk=("e27", "e28"), name="e29")
        op("vector", lambda v: v.tensor_tensor(sl(w11), sl(w9), sl(w11), OP.mult),
           wk=("e28",), name="e30")
        scanstep(psi, w7, w8, st_zr, "e14")
        statecopy(st_zr, psi)
        scanstep(w8, w7, w11, st_zi, "e14")
        statecopy(st_zi, w8)
        op("vector", lambda v: v.tensor_tensor(sl(w10), sl(w10), sl(psi), OP.mult), name="m1")
        op("vector", lambda v: v.tensor_tensor(sl(w9), sl(w9), sl(w8), OP.mult), name="m2")
        op("vector", lambda v: v.tensor_tensor(sl(w10), sl(w10), sl(w9), OP.add), name="m3")

        def m4():
            ev = s.emit("vector", lambda v: v.tensor_tensor(
                sl(yb[p]), sl(w10), sl(qb[p]), OP.mult))
            ev1[("y", ts, hs)] = ev
        steps.append(m4)
        return steps

    def p1b_thunks(ck):
        g0 = attn_pg0[ck]
        bankrot = [4, 5, 6, 7]
        state = {"bi": 0, "last_evac": {}, "last_mm": {}}
        thunks = []
        desc = proj_desc()
        for pi in range(0, len(desc), 2):
            pair = [(gi, desc[gi]) for gi in (pi, pi + 1)]
            def th(pair=pair):
                bk = bankrot[state["bi"] % 4]
                state["bi"] += 1
                ev = None
                for half_i, (gi, (pas, b, kind, mc, hb)) in enumerate(pair):
                    g = g0 + gi
                    for kc in range(KC):
                        first = half_i == 0 and kc == 0
                        ev = s.emit("tensor", lambda t, g=g, kc=kc, bk=bk, b=b,
                                    half_i=half_i, first=first: t.matmul(
                                        pb[bk][0:128, half_i * C:(half_i + 1) * C],
                                        ring_ap(ring, g, kc * 128, 128),
                                        xn[b][:, kc * C:(kc + 1) * C],
                                        start=first,
                                        stop=(half_i == 1 and kc == KC - 1)),
                                    wait=W(ring_wait(g), ev1[("xn", 2 * ck + b)],
                                           ev_bank[bk] if first else None) if kc == 0 else ())
                    ring_release(g, ev)
                    state["last_mm"][b] = ev
                gi, (pas, b, kind, mc, hb) = pair[0]
                ts = 2 * ck + b
                tsp = ts - 2
                silu = False
                if kind == "k":
                    dst, mcc, fw = kbn[b], mc, ("fkv", tsp, hb)
                elif kind == "v":
                    dst, mcc, fw = vb[b], mc, ("fkv", tsp, hb)
                elif kind == "g":
                    dst, mcc, fw = sgb[b], mc, ("fsgb", tsp, hb)
                    silu = True
                elif kind == "ar":
                    dst, mcc, fw = arb[b], mc, ("farb", tsp, hb)
                elif kind == "ai":
                    dst, mcc, fw = aib[b], mc - 8, ("faib", tsp, hb)
                else:
                    dst, mcc, fw = qb[b], mc, ("y", tsp, hb)
                wfree = ev1.get(fw)
                assert wfree is not None or tsp < 0, fw
                if silu:
                    evv = s.emit("scalar", lambda sc, dst=dst, mcc=mcc, bk=bk: sc.activation(
                        xf_slice(dst, mcc, nt=2 * C), pb[bk][0:128, 0:2 * C], AF.Silu),
                        wait=W(ev, wfree))
                else:
                    evv = s.emit("scalar", lambda sc, dst=dst, mcc=mcc, bk=bk: sc.copy(
                        xf_slice(dst, mcc, nt=2 * C), pb[bk][0:128, 0:2 * C]),
                        wait=W(ev, wfree))
                ev_bank[bk] = evv
                state["last_evac"][b] = evv
            thunks.append(th)

        def fin():
            for b in range(2):
                ev1[("PpairB", ck, b)] = state["last_mm"][b]
                ev1[("evP", 2 * ck + b)] = state["last_evac"][b]
        return thunks, fin

    def stream_zip(sa, sb):
        out = []
        for i in range(max(len(sa), len(sb))):
            if i < len(sa):
                out.append(sa[i])
            if i < len(sb):
                out.append(sb[i])
        return out

    def run_interleaved(steps, thunks):
        n = len(steps)
        bounds = ([4] * 12 + [16] * 4 + [32] * 4 + [n] * 4) if n else [0] * 24
        feeds = {}
        pos = 0
        for j, th in enumerate(thunks):
            want = 4 + (j * max(n - 6, 1)) // max(len(thunks) - 1, 1)
            pos = min(max(pos, bounds[j] if j < len(bounds) else n, want), n)
            feeds.setdefault(pos, []).append(th)
        for i, stp in enumerate(steps):
            stp()
            for th in feeds.get(i, []):
                th()
        for th in feeds.get(n, []):
            th()

    def emit_W1pair(ck):
        g0 = attn_wg0[ck]
        evr = {}
        for oc in range(2):
            evmm = {}
            for kcp in range(4):
                g = g0 + oc * 4 + kcp
                for jj in range(2):
                    kc = kcp * 2 + jj
                    for b in range(2):
                        for t_ in range(TT):
                            bk = 4 + b * 2 + t_
                            wv = []
                            if jj == 0 and b == 0 and t_ == 0:
                                wv.append(ring_wait(g))
                            if kc == 0:
                                wv += [ev1[("y", 2 * ck + b, 0)], ev_bank[bk]]
                            if kc == 4:
                                wv.append(ev1[("y", 2 * ck + b, 1)])
                            evmm[(b, t_)] = s.emit(
                                "tensor", lambda t, g=g, kc=kc, jj=jj, t_=t_, bk=bk, b=b: t.matmul(
                                    pb[bk][0:128, 0:512],
                                    yb[b][:, kc * C + t_ * 128:kc * C + (t_ + 1) * 128],
                                    ring_ap(ring, g, jj * 512, 512),
                                    start=(kc == 0), stop=(kc == KC - 1)),
                                wait=W(*wv))
                            if jj == 1 and b == 1 and t_ == TT - 1:
                                ring_release(g, evmm[(b, t_)])
            for b in range(2):
                slot = (2 * ck + b) % 4
                for t_ in range(TT):
                    bk = 4 + b * 2 + t_
                    s.emit("vector", lambda v, bk=bk, b=b, oc=oc: v.tensor_tensor(
                        psi[:, 0:512], pb[bk][0:128, 0:512],
                        ga_rep[:, b * D + oc * 512:b * D + (oc + 1) * 512], OP.mult),
                        wait=W(evmm[(b, t_)], ga_ready))
                    evr[(b, t_)] = s.emit("vector", lambda v, slot=slot, t_=t_, oc=oc: v.tensor_tensor(
                        stage[slot][:, t_ * 1024 + oc * 512:t_ * 1024 + (oc + 1) * 512],
                        stage[slot][:, t_ * 1024 + oc * 512:t_ * 1024 + (oc + 1) * 512],
                        psi[:, 0:512], OP.add))
                    ev_bank[bk] = evr[(b, t_)]
        for b in range(2):
            ts = 2 * ck + b
            slot = ts % 4
            dstm = dxmid.ap()[b].rearrange("(ck tt p) d -> ck p tt d", p=128, tt=TT)[ck]
            ev1[("store", ts)] = s.emit("sync", lambda sy, slot=slot, dstm=dstm: sy.dma_start(
                out=dstm, in_=stage[slot][:]), wait=W(evr[(b, TT - 1)]), sig=s_sos[slot], amt=16)

    if build_stage >= 1:
        s.emit("sync", lambda sy: sy.dma_start(out=ga_rep[:], in_=dws_grep.ap()[:, 0:B * D]),
               wait=W(bar1, ev_grs), sig=s_gr, amt=16)
        ga_ready = (s_gr, s.cnt["s_gr"])
        for b_ in range(B):
            s.emit("vector", lambda v, b_=b_: v.tensor_tensor(
                sagam[:, b_ * KC:(b_ + 1) * KC], m_sa[:, b_ * KC:(b_ + 1) * KC],
                gam[:, 0:KC], OP.mult), wait=W(bar1, modv_done) if b_ == 0 else ())
        for b0 in range(2):
            emit_A1_tp(b0)
            emit_A1_stats(b0)
            emit_S21(b0)
            emit_XN(b0)
        for it in range(NCH + 1):
            th96, finb = p1b_thunks(it) if it < NCH else ([], None)
            for pas in (0, 1):
                if it >= 1:
                    if pas == 0:
                        sa = chain_stream(2 * it - 2, 0, ("y", 2 * it - 3, 0))
                        sb = chain_stream(2 * it - 1, 1, ("y", 2 * it - 4, 1))
                    else:
                        sa = chain_stream(2 * it - 2, 1, ("y", 2 * it - 1, 1))
                        sb = chain_stream(2 * it - 1, 0, ("y", 2 * it - 2, 0))
                    steps = stream_zip(sa, sb)
                else:
                    steps = []
                run_interleaved(steps, th96[pas * 24:(pas + 1) * 24])
            if finb is not None:
                finb()
            if it >= 1:
                emit_W1pair(it - 1)
            if it + 1 < NCH:
                for ss2 in (2 * it + 2, 2 * it + 3):
                    emit_A1_tp(ss2)
                    emit_A1_stats(ss2)
                    emit_S21(ss2)
                    emit_XN(ss2)

    s.emit("gpsimd", lambda e: None, sig=s_bar)

    ffn_m1g0 = {}
    ffn_m2g0 = {}

    def queue_mm1_groups(ss, extra=()):
        ffn_m1g0[ss] = ring_state["g"]
        for hc in range(HC):
            srcw = dws_w1t.ap()[:, (hc * KC) * 128:(hc * KC + 8) * 128]
            ring_load(ringf, [(srcw, 0, 1024)], wait_extra=extra)
            extra = ()

    def queue_mm2_groups(ts):
        ffn_m2g0[ts] = ring_state["g"]
        for oc in range(2):
            for kcp in range(16):
                reqs = []
                for jj in range(2):
                    kc = kcp * 2 + jj
                    reqs.append((dws_w2t.ap()[kc * 128:(kc + 1) * 128,
                                              oc * 512:(oc + 1) * 512], jj * 512, 512))
                ring_load(ringf, reqs)

    if build_stage >= 2:
        queue_mm1_groups(0, extra=tt_done)
        for ss in range(1, NST):
            queue_mm1_groups(ss)
            queue_mm2_groups(ss - 1)
        queue_mm2_groups(NST - 1)

    p1.close()
    for eng in ENGINES:
        if eng != "gpsimd":
            s.emit(eng, lambda e: None, sig=s_bar)
    bar2 = (s_bar, s.cnt["s_bar"])

    p2 = contextlib.ExitStack()
    P2 = p2.enter_context
    HHALF = HC // 2
    x2fm = P2(nc.sbuf_tensor("x2fm", [128, KC * C], F32))
    xh2 = P2(nc.sbuf_tensor("xh2", [128, KC * C], F32))
    w2d = P2(nc.sbuf_tensor("w2d", [128, 4 * C], F32))
    m2scr = P2(nc.sbuf_tensor("m2scr", [128, 512], F32))
    hsq1 = P2(nc.sbuf_tensor("hsq1", [128, KC * C], BF16))
    xq1 = [P2(nc.sbuf_tensor(f"xq1_{i}", [128, KC * C], BF16)) for i in range(2)]
    Rb = P2(nc.sbuf_tensor("Rb", [128, HC * C], F32))
    B2 = P2(nc.sbuf_tensor("B2", [128, HC * C], F32))
    h2b = P2(nc.sbuf_tensor("h2b", [128, HC * C], BF16))
    xq2 = P2(nc.sbuf_tensor("xq2", [128, HC * C], BF16))
    gb_rep = P2(nc.sbuf_tensor("gb_rep", [128, B * D], BF16))
    mrow2 = P2(nc.sbuf_tensor("mrow2", [1, 2 * C], F32))
    srow2 = P2(nc.sbuf_tensor("srow2", [1, C], F32))
    arow = P2(nc.sbuf_tensor("arow", [1, C], F32))
    growA = P2(nc.sbuf_tensor("growA", [1, C], F32))
    growB = P2(nc.sbuf_tensor("growB", [1, C], F32))
    hrow = P2(nc.sbuf_tensor("hrow", [1, C], F32))
    hrowB = P2(nc.sbuf_tensor("hrowB", [1, C], F32))
    mrep2 = P2(nc.sbuf_tensor("mrep2", [128, 2 * C], F32))
    g1rep = P2(nc.sbuf_tensor("g1rep", [128, 4 * C], F32))
    g2rep = P2(nc.sbuf_tensor("g2rep", [128, 4 * C], F32))
    a2rep = [P2(nc.sbuf_tensor(f"a2rep{i}", [128, 2 * C], F32)) for i in range(2)]
    mvst2 = P2(nc.sbuf_tensor("mvst2", [128, 4], F32))
    varr2 = P2(nc.sbuf_tensor("varr2", [128, 2], F32))
    msq22 = P2(nc.sbuf_tensor("msq22", [128, 2], F32))
    mx2 = P2(nc.sbuf_tensor("mx2", [128, 2], F32))
    sqmx = P2(nc.sbuf_tensor("sqmx", [128, 2], F32))
    f2tp = P2(nc.sbuf_tensor("f2tp", [128, 2], F32))
    f2tok = P2(nc.sbuf_tensor("f2tok", [128, 2], F32))
    abn2 = P2(nc.sbuf_tensor("abn2", [128, HC], F32))
    sbn1 = P2(nc.sbuf_tensor("sbn1", [128, B * KC], F32))

    def emit_xl2(ss):
        slot = ss % 4
        lastp1 = {0: NST - 4, 1: NST - 3, 2: NST - 2, 3: NST - 1}
        wv = ([ev1[("store", ss)], ev1.get(("store", lastp1[ss]))] if ss < 4
              else [ev1.get(("ostore", ss - 4)), ev1[("store", ss)]])
        b, ck = ss % 2, ss // 2
        src = dxmid.ap()[b].rearrange("(ck tt p) d -> ck p tt d", p=128, tt=TT)[ck]
        ev1[("xl2", ss)] = s.emit("sync", lambda sy, slot=slot, src=src: sy.dma_start(
            out=stage[slot][:], in_=src), wait=W(*wv), sig=s_xls[slot], amt=16)

    def emit_A2_tp(ss):
        slot = ss % 4
        last = None
        for tt in range(TT):
            for kcg in range(2):
                bk = kcg
                ev = None
                for i2 in range(4):
                    kc = kcg * 4 + i2
                    ev = s.emit("tensor", lambda t, slot=slot, tt=tt, kc=kc, i2=i2, bk=bk: t.matmul(
                        pb[bk][0:128, i2 * 128:(i2 + 1) * 128],
                        stage[slot][:, tt * 1024 + kc * 128:tt * 1024 + (kc + 1) * 128],
                        identf[:], start=(i2 == 0), stop=(i2 == 3), is_transpose=True),
                        wait=W(ev1[("xl2", ss)], ev_bank[bk],
                               ev1.get(("xh2", ss - 1)), bar2 if ss == 0 else None) if i2 == 0 else ())

                def evacx(sc, tt=tt, kcg=kcg, bk=bk):
                    dst = x2fm[:].rearrange("p (kc c) -> p kc c", c=C)[
                        :, kcg * 4:(kcg + 1) * 4, tt * 128:(tt + 1) * 128]
                    return sc.copy(dst, pb[bk][0:128, 0:512])
                last = ev_bank[bk] = s.emit("scalar", evacx, wait=W(ev))
        ev1[("xfm2", ss)] = last

    def emit_stats_a_v(ss):
        slot = ss % 4
        ev = None
        for tt in range(TT):
            s.emit("vector", lambda v, slot=slot, tt=tt: v.tensor_scalar(
                hsq1[:].bitcast(F32)[:, 0:1024], stage[slot][:, tt * 1024:(tt + 1) * 1024],
                1.0 / D, None, OP.mult, OP.add, accum_out=mvst2[:, tt:tt + 1]),
                wait=W(ev1[("xl2", ss)], ev1.get(("mrow2", ss - 1)),
                       ev1.get(("sqabs", ss - 1))) if tt == 0 else ())
            ev = s.emit("vector", lambda v, slot=slot, tt=tt: v.scalar_tensor_tensor(
                hsq1[:].bitcast(F32)[:, 0:1024], stage[slot][:, tt * 1024:(tt + 1) * 1024], 1.0,
                stage[slot][:, tt * 1024:(tt + 1) * 1024], OP.mult, OP.mult,
                accum_out=mvst2[:, 2 + tt:3 + tt]))
        ev = s.emit("vector", lambda v: v.tensor_scalar(
            varr2[:], mvst2[:, 2:4], 1.0 / D, None, OP.mult), wait=W(ev))
        ev = s.emit("vector", lambda v: v.tensor_tensor(
            msq22[:], mvst2[:, 0:2], mvst2[:, 0:2], OP.mult))
        ev = s.emit("vector", lambda v: v.tensor_tensor(varr2[:], varr2[:], msq22[:], OP.subtract))
        ev1[("acc", ss)] = ev
        ev = s.emit("scalar", lambda sc: sc.activation(varr2[:], varr2[:], AF.Ln,
                                                       bias=cvec[:, CEPS5:CEPS5 + 1]), wait=W(ev))
        ev1[("rstd", ss)] = s.emit("scalar", lambda sc: sc.activation(
            varr2[:], varr2[:], AF.Exp, scale=-0.5))

    def emit_stats_a_pe(ss):
        ev = None
        for i2, (buf_, col) in enumerate([(mvst2, 0), (mvst2, 1), (varr2, 0), (varr2, 1)]):
            ev = s.emit("tensor", lambda t, buf_=buf_, col=col, i2=i2: t.matmul(
                pb[3][0:1, i2 * 128:(i2 + 1) * 128], buf_[:, col:col + 1], identf[:],
                start=(i2 == 0), stop=(i2 == 3), is_transpose=True),
                wait=W(ev1[("rstd", ss)], ev_bank[3]) if i2 == 0 else ())
        ev_mrow = s.emit("scalar", lambda sc: sc.copy(mrow2[:], pb[3][0:1, 0:512]), wait=W(ev))
        ev_bank[3] = ev_mrow
        ev1[("mrow2", ss)] = ev_mrow
        ev = s.emit("tensor", lambda t: t.matmul(pb[2][0:128, 0:C], onesf[0:1, :],
                                                 mrow2[0:1, 0:C], start=True, stop=False),
                    wait=W(ev_mrow, ev_bank[2], ev1.get(("xh2", ss - 1))))
        ev = s.emit("tensor", lambda t: t.matmul(pb[2][0:128, C:2 * C], onesf[0:1, :],
                                                 mrow2[0:1, C:2 * C], start=False, stop=True))
        ev_mrep = s.emit("scalar", lambda sc: sc.copy(mrep2[:], pb[2][0:128, 0:2 * C]), wait=W(ev))
        ev_bank[2] = ev_mrep
        ev1[("mrep", ss)] = ev_mrep

    def emit_stats_b(ss):
        b = ss % 2
        ev = None
        for kc in range(KC):
            s.emit("vector", lambda v, kc=kc: v.tensor_tensor(
                xf_slice(xh2, kc), xf_slice(x2fm, kc), mrep2[:, 0:C], OP.subtract),
                wait=W(ev1[("xfm2", ss)], ev1[("mrep", ss)],
                       ev1.get(("xq1", ss - 1))) if kc == 0 else ())
            ev = s.emit("vector", lambda v, kc=kc, b=b: v.scalar_tensor_tensor(
                xf_slice(xh2, kc), xf_slice(xh2, kc), m_cb1[:, b * KC + kc:b * KC + kc + 1],
                mrep2[:, C:2 * C], OP.mult, OP.mult))
            ev1[("xh2k", ss, kc)] = ev
        ev1[("xh2", ss)] = ev

    def emit_SQ_a_s(ss):
        b = ss % 2
        for kc in range(KC):
            ev1[("hsqk", ss, kc)] = s.emit("scalar", lambda sc, kc=kc, b=b: sc.activation(
                xf_slice(hsq1, kc), xf_slice(xh2, kc), AF.Square,
                bias=m_sb[:, b * KC + kc:b * KC + kc + 1]),
                wait=W(ev1[("xh2k", ss, kc)], ev1[("acc", ss)] if kc == 0 else None))
        ev = None
        for kc in range(KC):
            ev = s.emit("scalar", lambda sc, kc=kc, b=b: sc.activation(
                xf_slice(xh2, kc), xf_slice(xh2, kc), AF.Identity,
                scale=n1v[:, kc:kc + 1], bias=sbn1[:, b * KC + kc:b * KC + kc + 1]))
        ev1[("xn1", ss)] = ev

    def emit_SQ_row(ss):
        ev = s.emit("scalar", lambda sc: sc.copy(srow2[:], pb[3][0:1, 0:C]),
                    wait=[lambda ss=ss: ev1[("ssum1", ss)]])
        ev1[("srowev", ss)] = ev
        ev = s.emit("scalar", lambda sc: sc.activation(srow2[:], srow2[:], AF.Ln, scale=1.0 / D,
                                                       bias=cvec[0:1, CEPS8:CEPS8 + 1]))
        ev1[("r1", ss)] = s.emit("scalar", lambda sc: sc.activation(
            srow2[:], srow2[:], AF.Exp, scale=-0.5))
        ev1[("sqabs", ss)] = s.emit("scalar", lambda sc: sc.activation(hsq1[:], xh2[:], AF.Abs),
                                    wait=[lambda ss=ss: ev1[("ssum1", ss)]] + W(ev1[("xn1", ss)]))

    def emit_SQ_sums(ss):
        ev = None
        for kc in range(KC):
            ev = s.emit("tensor", lambda t, kc=kc: t.matmul(
                pb[3][0:1, 0:C], onesb[:], xf_slice(hsq1, kc),
                start=(kc == 0), stop=(kc == KC - 1)),
                wait=W(ev1[("hsqk", ss, kc)], ev_bank[3] if kc == 0 else None))
        ev1[("ssum1", ss)] = ev
        ev_bank[3] = ev1[("srowev", ss)]

    def emit_SQ_b(ss):
        ev = None
        for sz in [4, 2, 1]:
            ev = s.emit("vector", lambda v, sz=sz: v.tensor_tensor(
                hsq1[:, 0:sz * C], hsq1[:, 0:sz * C], hsq1[:, sz * C:2 * sz * C], OP.max),
                wait=W(ev1[("sqabs", ss)]) if sz == 4 else ())
        ev_tr = s.emit("vector", lambda v: v.tensor_copy(w2d[:, 0:C], hsq1[:, 0:C]),
                       wait=W(ev, ev1.get(("p2tp", ss - 2))))
        evm = None
        for tt in range(TT):
            ev = s.emit("tensor", lambda t, tt=tt: t.matmul(
                pb[0][0:128, 0:128], w2d[:, tt * 128:(tt + 1) * 128], identf[:],
                start=True, stop=True, is_transpose=True),
                wait=W(ev_tr, ev_bank[0]))
            ev = s.emit("scalar", lambda sc, tt=tt: sc.copy(
                w2d[:, C + tt * 128:C + (tt + 1) * 128], pb[0][0:128, 0:128]), wait=W(ev))
            ev_bank[0] = ev
            evm = s.emit("vector", lambda v, tt=tt: v.tensor_reduce(
                sqmx[:, tt:tt + 1], w2d[:, C + tt * 128:C + (tt + 1) * 128], AX.X, OP.max),
                wait=W(ev))
        ev1[("sqtp", ss)] = ev_bank[0]
        evc = s.emit("vector", lambda v: v.tensor_scalar(
            sqmx[:], sqmx[:], 1e-5, None, OP.max), wait=W(evm))
        ev = None
        for tt in range(TT):
            ev = s.emit("tensor", lambda t, tt=tt: t.matmul(
                pb[1][0:1, tt * 128:(tt + 1) * 128], sqmx[:, tt:tt + 1], identf[:],
                start=(tt == 0), stop=(tt == TT - 1), is_transpose=True),
                wait=W(evc, ev_bank[1]) if tt == 0 else ())
        ev = s.emit("scalar", lambda sc: sc.copy(growA[:], pb[1][0:1, 0:C]), wait=W(ev))
        ev_bank[1] = ev
        ev = s.emit("vector", lambda v: v.tensor_tensor(growA[:], growA[:], srow2[:], OP.mult),
                    wait=W(ev, ev1[("r1", ss)]))
        ev = s.emit("vector", lambda v: v.tensor_scalar(growA[:], growA[:], 1e-5, None, OP.max))
        ev = s.emit("scalar", lambda sc: sc.activation(growB[:], growA[:], AF.Ln), wait=W(ev))
        ev_s1 = s.emit("scalar", lambda sc: sc.activation(growB[:], growB[:], AF.Exp, scale=-1.0,
                                                          bias=cvec[0:1, CL127:CL127 + 1]))
        ev_g1 = s.emit("vector", lambda v: v.tensor_tensor(growB[:], growB[:], srow2[:], OP.mult),
                       wait=W(ev_s1))
        ev_a = s.emit("vector", lambda v: v.tensor_scalar(
            arow[:], growA[:], w1sr[:], 1.0 / 127.0, OP.mult, OP.mult), wait=W(ev))
        ev_a = s.emit("scalar", lambda sc: sc.activation(arow[:], arow[:], AF.Square),
                      wait=W(ev_a))
        ev_a = s.emit("vector", lambda v: v.tensor_scalar(
            arow[:], arow[:], ascr[:], None, OP.mult), wait=W(ev_a))
        ev = s.emit("tensor", lambda t: t.matmul(pb[3][0:128, 0:C], onesf[0:1, :], growB[0:1, :],
                                                 start=True, stop=True),
                    wait=W(ev_g1, ev_bank[3], ev1.get(("q1", ss - 1))))
        evg = s.emit("scalar", lambda sc: sc.copy(g1rep[:, 0:C], pb[3][0:128, 0:C]), wait=W(ev))
        ev_bank[3] = evg
        evg = s.emit("vector", lambda v: v.tensor_copy(g1rep[:, C:2 * C], g1rep[:, 0:C]),
                     wait=W(evg))
        evg = s.emit("vector", lambda v: v.tensor_copy(g1rep[:, 2 * C:4 * C], g1rep[:, 0:2 * C]))
        ev1[("g1rep", ss)] = evg
        ev = s.emit("tensor", lambda t: t.matmul(pb[3][0:128, 0:C], onesf[0:1, :], arow[0:1, :],
                                                 start=True, stop=True),
                    wait=W(ev_a, ev_bank[3], ev1.get(("p1a", ss - 2))))
        eva = s.emit("scalar", lambda sc, ss=ss: sc.copy(a2rep[ss % 2][:, 0:C], pb[3][0:128, 0:C]),
                     wait=W(ev))
        ev_bank[3] = eva
        eva = s.emit("vector", lambda v, ss=ss: v.tensor_copy(
            a2rep[ss % 2][:, C:2 * C], a2rep[ss % 2][:, 0:C]), wait=W(eva))
        ev1[("a2rep", ss)] = eva

    def emit_Q1(ss):
        p = ss % 2
        ev = None
        for j in range(2):
            ev = s.emit("vector", lambda v, j=j: v.tensor_tensor(
                xh2[:, j * 4 * C:(j + 1) * 4 * C], xh2[:, j * 4 * C:(j + 1) * 4 * C],
                g1rep[:], OP.mult),
                wait=W(ev1[("g1rep", ss)], ev1[("xn1", ss)],
                       ev1.get(("M1done", ss - 2))) if j == 0 else ())
        ev = s.emit("vector", lambda v, p=p: v.tensor_scalar(
            xq1[p][:], xh2[:], RMAGIC, RMAGIC, OP.add, OP.subtract), wait=W(ev))
        ev1[("xq1", ss)] = ev
        ev1[("q1", ss)] = ev

    def emit_M1(ss, half, q=None):
        p = ss % 2
        g0 = ffn_m1g0[ss]
        ev_evac = None
        evlast = None
        hcxr = range(0, HHALF, 2) if q is None else range(q * 8, q * 8 + 8, 2)
        for hcx in hcxr:
            hc0 = half * HHALF + hcx
            bk = 4 + ((hcx // 2 + 2) % 4)
            ev = None
            for hi in range(2):
                hc = hc0 + hi
                g = g0 + hc
                for kc in range(KC):
                    first = hi == 0 and kc == 0
                    ev = s.emit("tensor", lambda t, g=g, kc=kc, bk=bk, p=p, hi=hi,
                                first=first: t.matmul(
                                    pb[bk][0:128, hi * C:(hi + 1) * C],
                                    ring_ap(ringf, g, kc * 128, 128),
                                    xq1[p][:, kc * C:(kc + 1) * C],
                                    start=first, stop=(hi == 1 and kc == KC - 1)),
                        wait=W(ring_wait(g), ev1[("xq1", ss)],
                               ev_bank[bk] if first else None) if kc == 0 else ())
                ring_release(g, ev)
            evlast = ev
            ev_evac = ev_bank[bk] = s.emit("scalar", lambda sc, hc0=hc0, bk=bk: sc.activation(
                Rb[:, hc0 * C:(hc0 + 2) * C], pb[bk][0:128, 0:2 * C], AF.Relu),
                wait=W(ev, ev1.get(("Rfree", ss - 1, half))))
        if q is None or q == 1:
            ev1[("Rh", ss, half)] = ev_evac
        if half == 1 and q == 1:
            ev1[("M1done", ss)] = evlast

    def hf_slice(buf, hc):
        return buf[:, hc * C:(hc + 1) * C]

    def emit_p1(ts):
        for half in range(2):
            h0 = half * HHALF
            ev = s.emit("vector", lambda v, h0=h0: v.tensor_tensor(
                Rb[:, h0 * C:(h0 + HHALF) * C], Rb[:, h0 * C:(h0 + HHALF) * C],
                Rb[:, h0 * C:(h0 + HHALF) * C], OP.mult),
                wait=W(ev1[("Rh", ts, half)]))
            for q in range(8):
                ev = s.emit("vector", lambda v, h0=h0, q=q, ts=ts: v.tensor_tensor(
                    Rb[:, (h0 + 2 * q) * C:(h0 + 2 * q + 2) * C],
                    Rb[:, (h0 + 2 * q) * C:(h0 + 2 * q + 2) * C],
                    a2rep[ts % 2][:], OP.mult),
                    wait=W(ev1[("a2rep", ts)]) if q == 0 else ())
            ev1[("p1a", ts, half)] = ev
            evw = None
            for hcx in range(HHALF):
                hc = h0 + hcx
                evw = s.emit("scalar", lambda sc, hc=hc: sc.activation(
                    hf_slice(B2, hc), hf_slice(Rb, hc), AF.Identity,
                    scale=n2v[:, hc:hc + 1], bias=abn2[:, hc:hc + 1]),
                    wait=W(ev, ev1.get(("xq2", ts - 1))) if hcx == 0 else ())
            evsq = s.emit("scalar", lambda sc, h0=h0: sc.activation(
                h2b[:, h0 * C:(h0 + HHALF) * C], Rb[:, h0 * C:(h0 + HHALF) * C],
                AF.Square, bias=abi[:]),
                wait=W(evw, ev1.get(("treedone", ts - 1))))
            ev1[("Rfree", ts, half)] = evsq
            ev1[("h2b", ts, half)] = evsq
        ev1[("p1a", ts)] = ev1[("p1a", ts, 1)]

    def emit_p2row_sums(ts):
        ev = None
        for hc in range(HC):
            ev = s.emit("tensor", lambda t, hc=hc: t.matmul(
                pb[3][0:1, 0:C], onesb[:], hf_slice(h2b, hc),
                start=(hc == 0), stop=(hc == HC - 1)),
                wait=W(ev1[("h2b", ts, 0)], ev1[("h2b", ts, 1)],
                       ev_bank[3]) if hc == 0 else ())
        ev1[("ssum2", ts)] = ev
        ev = s.emit("scalar", lambda sc: sc.copy(hrow[:], pb[3][0:1, 0:C]), wait=W(ev))
        ev_bank[3] = ev
        ev = s.emit("scalar", lambda sc: sc.activation(hrow[:], hrow[:], AF.Ln, scale=1.0 / H,
                                                       bias=cvec[0:1, CEPS8:CEPS8 + 1]))
        ev1[("r2", ts)] = s.emit("scalar", lambda sc: sc.activation(
            hrow[:], hrow[:], AF.Exp, scale=-0.5))

    def emit_p2row_abs(ts):
        ev = s.emit("scalar", lambda sc: sc.activation(h2b[:], B2[:], AF.Abs),
                    wait=W(ev1[("ssum2", ts)]))
        ev1[("absdone", ts)] = ev
        for sz in [16, 8, 4, 2, 1]:
            ev = s.emit("vector", lambda v, sz=sz: v.tensor_tensor(
                h2b[:, 0:sz * C], h2b[:, 0:sz * C], h2b[:, sz * C:2 * sz * C], OP.max),
                wait=W(ev) if sz == 16 else ())
        ev = s.emit("vector", lambda v: v.tensor_copy(w2d[:, 0:C], h2b[:, 0:C]),
                    wait=W(ev, ev1.get(("sqtp", ts + 1))))
        ev1[("treedone", ts)] = ev

    def emit_p2row_rows(ts):
        evm = None
        for tt in range(TT):
            ev2 = s.emit("tensor", lambda t, tt=tt: t.matmul(
                pb[0][0:128, 0:128], w2d[:, tt * 128:(tt + 1) * 128], identf[:],
                start=True, stop=True, is_transpose=True),
                wait=W(ev1[("treedone", ts)], ev_bank[0]))
            ev2 = s.emit("scalar", lambda sc, tt=tt: sc.copy(
                w2d[:, C + tt * 128:C + (tt + 1) * 128], pb[0][0:128, 0:128]), wait=W(ev2))
            ev_bank[0] = ev2
            evm = s.emit("vector", lambda v, tt=tt: v.tensor_reduce(
                mx2[:, tt:tt + 1], w2d[:, C + tt * 128:C + (tt + 1) * 128], AX.X, OP.max),
                wait=W(ev2))
        ev1[("p2tp", ts)] = ev_bank[0]
        evc = s.emit("vector", lambda v: v.tensor_scalar(mx2[:], mx2[:], 0.0, None, OP.max),
                     wait=W(evm))
        ev1[("mx2", ts)] = evc
        ev = None
        for tt in range(TT):
            ev = s.emit("tensor", lambda t, tt=tt: t.matmul(
                pb[1][0:1, tt * 128:(tt + 1) * 128], mx2[:, tt:tt + 1], identf[:],
                start=(tt == 0), stop=(tt == TT - 1), is_transpose=True),
                wait=W(evc, ev_bank[1]) if tt == 0 else ())
        ev = s.emit("scalar", lambda sc: sc.copy(hrowB[:], pb[1][0:1, 0:C]), wait=W(ev))
        ev_bank[1] = ev
        ev = s.emit("vector", lambda v: v.tensor_tensor(hrowB[:], hrowB[:], hrow[:], OP.mult),
                    wait=W(ev, ev1[("r2", ts)]))
        ev = s.emit("vector", lambda v: v.tensor_scalar(hrowB[:], hrowB[:], 1e-5, None, OP.max))
        ev = s.emit("scalar", lambda sc: sc.activation(hrowB[:], hrowB[:], AF.Ln), wait=W(ev))
        ev = s.emit("scalar", lambda sc: sc.activation(hrowB[:], hrowB[:], AF.Exp, scale=-1.0,
                                                       bias=cvec[0:1, CL127:CL127 + 1]))
        ev_g2 = s.emit("vector", lambda v: v.tensor_tensor(hrowB[:], hrowB[:], hrow[:], OP.mult),
                       wait=W(ev))
        ev = s.emit("tensor", lambda t: t.matmul(pb[3][0:128, 0:C], onesf[0:1, :], hrowB[0:1, :],
                                                 start=True, stop=True),
                    wait=W(ev_g2, ev_bank[3], ev1.get(("xq2", ts - 1))))
        evg = s.emit("scalar", lambda sc: sc.copy(g2rep[:, 0:C], pb[3][0:128, 0:C]), wait=W(ev))
        ev_bank[3] = evg
        evg = s.emit("vector", lambda v: v.tensor_copy(g2rep[:, C:2 * C], g2rep[:, 0:C]),
                     wait=W(evg))
        evg = s.emit("vector", lambda v: v.tensor_copy(g2rep[:, 2 * C:4 * C], g2rep[:, 0:2 * C]))
        ev1[("g2rep", ts)] = evg
        evt = None
        for tt in range(TT):
            evt = s.emit("tensor", lambda t, tt=tt: t.matmul(
                pb[1][0:128, tt:tt + 1], hrow[0:1, tt * 128:(tt + 1) * 128],
                onesf[0:1, 0:1], start=(tt == 0), stop=(tt == TT - 1), is_transpose=True),
                wait=W(ev1[("r2", ts)], ev_bank[1]) if tt == 0 else ())
        evt = s.emit("scalar", lambda sc: sc.copy(f2tp[:, 0:TT], pb[1][0:128, 0:TT]),
                     wait=W(evt))
        ev_bank[1] = evt
        evf = s.emit("vector", lambda v: v.tensor_tensor(
            f2tok[:], mx2[:], f2tp[:, 0:TT], OP.mult), wait=W(evt, ev1[("mx2", ts)]))
        evf = s.emit("vector", lambda v: v.tensor_scalar(
            f2tok[:], f2tok[:], 1e-5, None, OP.max))
        evf = s.emit("vector", lambda v: v.tensor_scalar(
            f2tok[:], f2tok[:], 1.0 / 127.0, w2sv[:], OP.mult, OP.mult))
        ev1[("f2", ts)] = evf

    def emit_quant2(ts):
        ev = None
        for j in range(8):
            ev = s.emit("vector", lambda v, j=j: v.tensor_tensor(
                B2[:, j * 4 * C:(j + 1) * 4 * C], B2[:, j * 4 * C:(j + 1) * 4 * C],
                g2rep[:], OP.mult),
                wait=W(ev1[("g2rep", ts)], ev1[("absdone", ts)],
                       ev1.get(("M2done", ts - 1))) if j == 0 else ())
        ev = s.emit("vector", lambda v: v.tensor_scalar(
            xq2[:], B2[:], RMAGIC, RMAGIC, OP.add, OP.subtract), wait=W(ev))
        ev1[("xq2", ts)] = ev

    def emit_M2(ts, oc, part=0):
        b, ck = ts % 2, ts // 2
        g0 = ffn_m2g0[ts]
        slot = ts % 4
        if oc == 0:
            kr = range(0, 4) if part == 0 else range(4, 16)
        else:
            kr = range(0, 8) if part == 0 else range(8, 16)
        bks = (6, 7) if oc == 0 else (4, 5)
        for kcp in kr:
            g = g0 + oc * 16 + kcp
            for jj in range(2):
                kc = kcp * 2 + jj
                for t_ in range(TT):
                    bk = bks[t_]
                    wv = []
                    if jj == 0 and t_ == 0:
                        wv.append(ring_wait(g))
                    if kc == 0:
                        wv += [ev1[("xq2", ts)], ev_bank[bk]]
                    ev1[("m2mm", ts, oc, t_)] = s.emit(
                        "tensor", lambda t, g=g, kc=kc, jj=jj, t_=t_, bk=bk: t.matmul(
                            pb[bk][0:128, 0:512],
                            xq2[:, kc * C + t_ * 128:kc * C + (t_ + 1) * 128],
                            ring_ap(ringf, g, jj * 512, 512),
                            start=(kc == 0), stop=(kc == HC - 1)),
                        wait=W(*wv))
                    if jj == 1 and t_ == TT - 1:
                        ring_release(g, ev1[("m2mm", ts, oc, t_)])
        if part == 1:
            evr = ev1.get(("m2evr", ts))
            for t_ in range(TT):
                bk = bks[t_]
                ev = s.emit("vector", lambda v, bk=bk, b=b, oc=oc, t_=t_: v.scalar_tensor_tensor(
                    m2scr[:], pb[bk][0:128, 0:512], f2tok[:, t_:t_ + 1],
                    gb_rep[:, b * D + oc * 512:b * D + (oc + 1) * 512], OP.mult, OP.mult),
                    wait=W(ev1[("m2mm", ts, oc, t_)], ev1[("f2", ts)], gb_ready))
                evr = s.emit("vector", lambda v, slot=slot, t_=t_, oc=oc: v.tensor_tensor(
                    stage[slot][:, t_ * 1024 + oc * 512:t_ * 1024 + (oc + 1) * 512],
                    stage[slot][:, t_ * 1024 + oc * 512:t_ * 1024 + (oc + 1) * 512],
                    m2scr[:], OP.add))
                ev_bank[bk] = evr
            ev1[("m2evr", ts)] = evr
        if oc == 1 and part == 1:
            ev1[("M2done", ts)] = ev1[("m2mm", ts, 1, TT - 1)]
            dsto = dout.ap()[b].rearrange("(ck tt p) d -> ck p tt d", p=128, tt=TT)[ck]
            ev1[("ostore", ts)] = s.emit("sync", lambda sy, slot=slot, dsto=dsto: sy.dma_start(
                out=dsto, in_=stage[slot][:]), wait=W(ev1[("m2evr", ts)]),
                sig=s_sos[slot], amt=16)

    if build_stage >= 2:
        s.emit("sync", lambda sy: sy.dma_start(out=gb_rep[:], in_=dws_grep.ap()[:, B * D:]),
               wait=W(bar2, ev_grs), sig=s_gr, amt=16)
        gb_ready = (s_gr, s.cnt["s_gr"])
        s.emit("vector", lambda v: v.tensor_scalar(
            abn2[:], n2v[:], abi[:], None, OP.mult), wait=W(bar2))
        for b_ in range(B):
            s.emit("vector", lambda v, b_=b_: v.tensor_tensor(
                sbn1[:, b_ * KC:(b_ + 1) * KC], m_sb[:, b_ * KC:(b_ + 1) * KC],
                n1v[:, 0:KC], OP.mult))
        emit_xl2(0)
        emit_xl2(1)
        emit_A2_tp(0)
        emit_stats_a_v(0)
        emit_stats_a_pe(0)
        emit_stats_b(0)
        emit_SQ_a_s(0)
        emit_SQ_row(0)
        emit_SQ_sums(0)
        emit_SQ_b(0)
        emit_Q1(0)
        for it in range(NST + 1):
            nss = it + 1
            if nss + 1 < NST:
                emit_xl2(nss + 1)
            if it >= 1:
                emit_p2row_sums(it - 1)
                emit_p2row_abs(it - 1)
            if it < NST:
                emit_M1(it, 0)
            if it >= 1:
                emit_p2row_rows(it - 1)
            if nss < NST:
                emit_stats_a_v(nss)
            if it >= 1:
                emit_quant2(it - 1)
            if nss < NST:
                emit_A2_tp(nss)
            if it < NST:
                emit_M1(it, 1, 0)
            if nss < NST:
                emit_stats_a_pe(nss)
            if it < NST:
                emit_M1(it, 1, 1)
            if nss < NST:
                emit_stats_b(nss)
                emit_SQ_a_s(nss)
                emit_SQ_row(nss)
            if it < NST:
                emit_p1(it)
            if it >= 1:
                emit_M2(it - 1, 0, 0)
            if nss < NST:
                emit_SQ_sums(nss)
            if it >= 1:
                emit_M2(it - 1, 0, 1)
                emit_M2(it - 1, 1, 0)
            if nss < NST:
                emit_SQ_b(nss)
                emit_Q1(nss)
            if it >= 1:
                emit_M2(it - 1, 1, 1)
    p2.close()

    tap_src = {"xmid": dxmid, "mod": dws_mod, "w1t": dws_w1t, "w2t": dws_w2t,
               "attn": dws_attn, "wo": dws_wo}
    all_done = [(s_sos[i], s.cnt[f"s_so{i}"]) for i in range(4)] + sc_done_l
    prev_tap = None
    for name in taps:
        srcd = tap_src[name]
        prev_tap = s.emit("gpsimd", lambda sy, a=srcd, d=tap_d[name]: sy.dma_start(
            out=d.ap(), in_=a.ap()), wait=all_done + W(prev_tap), sig=s_st, amt=16)
        all_done = []
    fin = [(s_sos[i], s.cnt[f"s_so{i}"]) for i in range(4)] + [(s_st, s.cnt["s_st"])]
    s.emit("sync", lambda sy: None, wait=fin)

    block = es.enter_context(nc.Block())
    s.install(block, self_waits=debug_sync)
    es.close()
    return nc


_CACHE = {}


def kernel(L0_attn_x, L0_attn_gamma, L0_attn_wq, L0_attn_wk, L0_attn_wv,
           L0_attn_wa, L0_attn_wg, L0_attn_wo, L0_ffn_norm1_w, L0_ffn_w1,
           L0_ffn_act_scale, L0_ffn_act_bias, L0_ffn_norm2_w, L0_ffn_w2,
           L0_c, L0_wmod):
    if "nc" not in _CACHE:
        _CACHE["nc"] = build_program()
    nc = _CACHE["nc"]
    cc = np.ascontiguousarray
    shared = {
        "wq": cc(L0_attn_wq), "wk": cc(L0_attn_wk), "wv": cc(L0_attn_wv),
        "wa": cc(L0_attn_wa), "wg": cc(L0_attn_wg), "wo": cc(L0_attn_wo),
        "gamma": cc(L0_attn_gamma), "n1w": cc(L0_ffn_norm1_w),
        "n2w": cc(L0_ffn_norm2_w), "w1": cc(L0_ffn_w1), "w2": cc(L0_ffn_w2),
        "ascale": cc(L0_ffn_act_scale), "abias": cc(L0_ffn_act_bias),
        "wmod": cc(L0_wmod),
    }
    in_maps = []
    for i in range(NCORE):
        m = dict(shared)
        m["x"] = cc(L0_attn_x[i * B:(i + 1) * B])
        m["c"] = cc(L0_c[i * B:(i + 1) * B])
        in_maps.append(m)
    res = run_bass_kernel_spmd(nc, in_maps, core_ids=list(range(NCORE)))
    return np.concatenate([r["out"] for r in res.results], axis=0)
